# version 08
# baseline (speedup 1.0000x reference)
import contextlib

import numpy as np
import ml_dtypes
import concourse.bass as bass
import concourse.mybir as mybir
from concourse.bass_utils import run_bass_kernel_spmd

F32 = mybir.dt.float32
BF16 = mybir.dt.bfloat16
ALU = mybir.AluOpType
AF = mybir.ActivationFunctionType
AX = mybir.AxisListType.X

NCORES = 8
D = 1024
FF = 4096
NTOK = 4096
T = 512
NT = NTOK // T
TD = 256
NTD = NTOK // TD
KD = D // 128
KF = FF // 128
C2 = 512
EPS_LN = 1e-5
EPS_Q = 1e-8
SIN_SCL = float(2.0 * np.pi / 1024.0)

ENGINES = ("sync", "tensor", "vector", "scalar", "gpsimd")


def _ctr(m):
    m = np.asarray(m, np.int64) % 1024
    return (m - 1024 * (m >= 512)).astype(np.float32)


def _rows_to_chunks(a, p=128):
    r, w = a.shape
    return np.ascontiguousarray(
        a.reshape(r // p, p, w).transpose(1, 0, 2).reshape(p, (r // p) * w))


_CACHE = {}


def _host_consts():
    if _CACHE:
        return _CACHE
    e = np.arange(1024)[:, None]
    c = np.arange(C2)[None, :]
    mcc = _ctr(e * c + 256)
    mbb = _ctr(-e * c)
    mbb[:, 0] = _ctr(256 + 512 * e)[:, 0]
    ci = np.arange(C2)[:, None]
    dd = np.arange(D)[None, :]
    mia = _ctr(ci * dd + 256)
    mib = _ctr(-ci * dd)
    mib[0, :] = _ctr(256 + 512 * dd)[0, :]
    coefa = np.full((C2, 1), 2.0 / 1024.0, np.float32)
    coefa[0, 0] = 1.0 / 1024.0
    coefb = coefa.copy()
    _CACHE.update(
        ident=np.eye(128, dtype=np.float32),
        mcc=_rows_to_chunks(mcc), mbb=_rows_to_chunks(mbb),
        mia=_rows_to_chunks(mia), mib=_rows_to_chunks(mib),
        coefa=_rows_to_chunks(coefa), coefb=_rows_to_chunks(coefb),
    )
    return _CACHE


SEM_OWNER = {"v": "vector", "a": "scalar", "g": "gpsimd"}


class Prog:
    def __init__(self):
        self.q = {e: [] for e in ENGINES}
        self.sem = {}
        self.cnt = {}
        self.mark = {e: {} for e in ENGINES}
        self.scratch = {}
        self.settled = {}

    def add_sem(self, name, handle):
        self.sem[name] = handle
        self.cnt[name] = 0

    def wait(self, eng, tok, raw=False):
        if tok is None:
            return
        if isinstance(tok, list):
            for t in tok:
                self.wait(eng, t, raw=raw)
            return
        name, val = tok
        owner = SEM_OWNER.get(name)
        if not raw and owner is not None and owner != eng:
            st = self.settled.get(name)
            if st is None or st[1] < val:
                cover = self.cnt[name]
                stok = self.sop(owner, self._mk_settle(owner), name)
                self.settled[name] = (stok, cover)
                tok = stok
            else:
                tok = st[0]
            name, val = tok
        if self.mark[eng].get(name, 0) >= val:
            return
        self.mark[eng][name] = val
        h = self.sem[name]
        self.q[eng].append(lambda e, h=h, v=val: e.wait_ge(h, v))

    def _mk_settle(self, owner):
        a = self.scratch[owner]
        if owner == "scalar":
            return lambda e, a=a: e.copy(a, a)
        return lambda e, a=a: e.tensor_copy(a, a)

    def op(self, eng, build, sem=None, n=1):
        if sem is None:
            self.q[eng].append(build)
            return None
        self.cnt[sem] += n
        tok = (sem, self.cnt[sem])
        h = self.sem[sem]
        self.q[eng].append(lambda e, b=build, h=h, n=n: b(e).then_inc(h, n))
        return tok

    def settle(self, eng, sem, scratch_ap):
        if eng == "scalar":
            return self.sop(eng, lambda e, a=scratch_ap: e.copy(a, a), sem)
        return self.sop(eng, lambda e, a=scratch_ap: e.tensor_copy(a, a), sem)

    def quiesce(self):
        for s, owner in SEM_OWNER.items():
            if self.cnt.get(s, 0) > 0:
                self.sop(owner, self._mk_settle(owner), s)
        snap = [(s, c) for s, c in self.cnt.items() if c > 0]
        for eng in ENGINES:
            for s, c in snap:
                self.wait(eng, (s, c), raw=True)

    def sop(self, eng, build, sem):
        tok = self.op(eng, build, sem=sem)
        self.wait(eng, tok, raw=True)
        return tok

    def dma(self, eng, out_ap, in_ap, sem, slow=False):
        return self.op(eng, lambda e, o=out_ap, i=in_ap, s=slow: e.dma_start(
            out=o, in_=i, allow_slow_non_contiguous=s), sem=sem, n=16)

    def replay(self, block):
        for eng in ENGINES:
            queue = self.q[eng]

            def run(e, queue=queue):
                for fn in queue:
                    fn(e)
            getattr(block, eng)(run)


def emit_quant_pair(P, srcs, r_pps, bufs, outs, deps, sbias_col, *,
                    weight_mode=False):
    yA, fA, gA, yB, fB, gB = [b[:] for b in bufs]
    lo = -128.0 if weight_mode else 0.0
    hi = 127.0 if weight_mode else 255.0
    P.wait("vector", deps)
    ty = {}
    for s, (src_ap, r_pp, y, f) in enumerate(
            ((srcs[0], r_pps[0], yA, fA), (srcs[1], r_pps[1], yB, fB))):
        P.op("vector", lambda e, y=y, src_ap=src_ap, r=r_pp: e.tensor_scalar(
            y, src_ap, r, lo, ALU.mult, ALU.max))
        P.op("vector", lambda e, y=y: e.tensor_single_scalar(y, y, hi, ALU.min))
        ty[s] = P.op("vector", lambda e, f=f, y=y: e.tensor_copy(f, y), sem="v")
    va = {0: ty[0], 1: ty[1]}
    ta = {}
    for i, k in enumerate(range(7, -1, -1)):
        c = -float(2 ** (k - 1))
        for s, (f, g) in enumerate(((fA, gA), (fB, gB))):
            P.wait("scalar", va[s], raw=True)
            ta[s] = P.op("scalar", lambda e, f=f, g=g: e.activation(
                g, f, AF.Sign, bias=sbias_col[:, 0:1], scale=1.0), sem="a")
        for s, (f, g) in enumerate(((fA, gA), (fB, gB))):
            P.wait("vector", ta[s], raw=True)
            va[s] = P.op("vector", lambda e, f=f, g=g, c=c: e.scalar_tensor_tensor(
                f, g, c, f, ALU.mult, ALU.add), sem="v")
    touts = []
    for s, (y, f, out_ap) in enumerate(((yA, fA, outs[0]), (yB, fB, outs[1]))):
        P.wait("vector", va[s], raw=True)
        if weight_mode:
            tk = P.op("vector", lambda e, y=y, f=f, o=out_ap: e.scalar_tensor_tensor(
                o, y, -0.5, f, ALU.add, ALU.subtract), sem="v")
        else:
            tk = P.op("vector", lambda e, y=y, f=f, o=out_ap: e.tensor_sub(
                o, y, f), sem="v")
        touts.append(tk)
    return touts[0], touts[1], ta[1]


def emit_round_wide(P, y_ap, q_ap, ab_ap, t1_ap, t2_ap, sg_ap, zbias, deps):
    P.wait("scalar", deps)
    tsg = P.op("scalar", lambda e: e.activation(sg_ap, y_ap, AF.Sign,
                                                bias=zbias[:, 0:1], scale=1.0),
               sem="a")
    tab = P.op("scalar", lambda e: e.activation(ab_ap, y_ap, AF.Abs,
                                                bias=zbias[:, 0:1], scale=1.0),
               sem="a")
    P.wait("vector", [tab, tsg])
    P.sop("vector", lambda e: e.tensor_scalar_add(t1_ap, ab_ap, 0.5), "v")
    for k in range(39, -1, -1):
        c = float(2 ** k)
        P.sop("vector", lambda e, c=c: e.tensor_scalar(t2_ap, t1_ap, c, -c,
                                                       ALU.is_ge, ALU.mult), "v")
        P.sop("vector", lambda e: e.tensor_add(t1_ap, t1_ap, t2_ap), "v")
    P.sop("vector", lambda e: e.scalar_tensor_tensor(t2_ap, ab_ap, 0.5, t1_ap,
                                                     ALU.add, ALU.subtract), "v")
    return P.sop("vector", lambda e: e.tensor_mul(q_ap, t2_ap, sg_ap), "v")


def build_program(debug=False):
    nc = bass.Bass("TRN2", target_bir_lowering=False)

    def din(name, shape, dtype=F32):
        return nc.dram_tensor(name, shape, dtype, kind="ExternalInput").ap()

    x_in = din("x", [NTOK, D])
    wq_my = din("wq_my", [D, 128])
    wk_my = din("wk_my", [D, 128])
    wv_my = din("wv_my", [D, 128])
    w1_my = din("w1_my", [512, D])
    w2_my = din("w2_my", [128, FF])
    b1_in = din("b1t", [128, KF])
    b2_in = din("b2t", [128, KD])
    ident_in = din("ident", [128, 128])
    mcc_in = din("mcc", [128, KD * C2])
    mbb_in = din("mbb", [128, KD * C2])
    mia_in = din("mia", [128, 4 * D])
    mib_in = din("mib", [128, 4 * D])
    coefa_in = din("coefa", [128, 4])
    coefb_in = din("coefb", [128, 4])
    mask_in = din("mask8", [128, 8])

    out_d = nc.dram_tensor("out", [NTOK, D], F32, kind="ExternalOutput").ap()

    kind_dbg = "ExternalOutput" if debug else "Internal"
    xnt_d = nc.dram_tensor("xnt_sp", [128, NT * KD * T], F32, kind=kind_dbg).ap()
    p_d = nc.dram_tensor("p_sp", [128, NT * 8 * T], F32, kind=kind_dbg).ap()
    fq_d = nc.dram_tensor("fq_sp", [128, NT * 8 * T], F32, kind=kind_dbg).ap()
    s_d = nc.dram_tensor("s_sp", [128, NT * KD * T], F32, kind=kind_dbg).ap()
    h_d = nc.dram_tensor("h_sp", [128, KF * NTOK], F32, kind=kind_dbg).ap()
    xn_dbg = nc.dram_tensor("xn_dbg", [128, 4 * D], F32, kind=kind_dbg).ap()
    cc_dbg = nc.dram_tensor("cc_dbg", [128, KD * C2], F32, kind=kind_dbg).ap()
    tab_dbg = nc.dram_tensor("tab_dbg", [128, KD * C2], F32, kind=kind_dbg).ap()
    agi_dbg = nc.dram_tensor("agi_dbg", [128, 6 * C2 + 5], F32, kind=kind_dbg).ap()
    diag_dbg = nc.dram_tensor("diag_dbg", [128, 128], F32, kind=kind_dbg).ap()
    qw1_dbg = nc.dram_tensor("qw1_dbg", [128, KD * FF], BF16, kind=kind_dbg).ap()
    xq_dbg = nc.dram_tensor("xq_dbg", [128, KD * T], BF16, kind=kind_dbg).ap()
    qh_dbg = nc.dram_tensor("qh_dbg", [128, KF * TD], BF16, kind=kind_dbg).ap()
    qw2_dbg = nc.dram_tensor("qw2_dbg", [128, KF * D], BF16, kind=kind_dbg).ap()
    gb_dbg = nc.dram_tensor("gb_dbg", [128, KD * TD], F32, kind=kind_dbg).ap()
    rb_dbg = nc.dram_tensor("rb_dbg", [128, KD * TD], F32, kind=kind_dbg).ap()
    gfull_dbg = nc.dram_tensor("gfull_dbg", [128, KD * NTOK], F32, kind=kind_dbg).ap()
    qhfull_dbg = nc.dram_tensor("qhfull_dbg", [128, KF * NTOK], BF16, kind=kind_dbg).ap()
    ob_dbg = nc.dram_tensor("ob_dbg", [128, 2 * D], F32, kind=kind_dbg).ap()
    sd_dbg = nc.dram_tensor("sd_dbg", [128, KD * TD], F32, kind=kind_dbg).ap()
    id_dbg = nc.dram_tensor("id_dbg", [128, 128], F32, kind=kind_dbg).ap()
    qw1t_d = nc.dram_tensor("qw1t_d", [128, KD * FF], BF16).ap()
    qw2t_d = nc.dram_tensor("qw2t_d", [128, KF * D], BF16).ap()

    ag1_in = nc.dram_tensor("ag1_in", [128, 6 * C2 + 5], F32)
    ag1_out = nc.dram_tensor("ag1_out", [1024, 6 * C2 + 5], F32)
    agw1_in = nc.dram_tensor("agw1_in", [512, D], F32)
    agw1_out = nc.dram_tensor("agw1_out", [FF, D], F32)
    agw2_in = nc.dram_tensor("agw2_in", [128, FF], F32)
    agw2_out = nc.dram_tensor("agw2_out", [D, FF], F32)
    ag2_in = nc.dram_tensor("ag2_in", [128, 10], F32)
    ag2_out = nc.dram_tensor("ag2_out", [1024, 10], F32)
    ag3_in = nc.dram_tensor("ag3_in", [128, 1], F32)
    ag3_out = nc.dram_tensor("ag3_out", [1024, 1], F32)

    P = Prog()
    es = contextlib.ExitStack()
    with es:
        for s in ("ld", "sp", "cc", "t", "v", "a", "g", "st"):
            P.add_sem(s, es.enter_context(nc.semaphore(s)))

        sb = {}

        def alloc(name, shape, dtype=F32, stack=es):
            t = stack.enter_context(nc.sbuf_tensor("sb_" + name, shape, dtype))
            sb[name] = t
            return t

        ident = alloc("ident", [128, 128])
        identb = alloc("identb", [128, 128], BF16)
        ones128 = alloc("ones128", [128, 128])
        sbias_a = alloc("sbias_a", [128, 1])
        sbias_w = alloc("sbias_w", [128, 1])
        zbias = alloc("zbias", [128, 1])
        mask8 = alloc("mask8", [128, 8])
        m1loc = alloc("m1loc", [128, 1])
        m2loc = alloc("m2loc", [128, 1])
        totals = alloc("totals", [128, 8])
        offs = alloc("offs", [128, 8])
        s1pp = alloc("s1pp", [128, 1])
        r1pp = alloc("r1pp", [128, 1])
        s2pp = alloc("s2pp", [128, 1])
        r2pp = alloc("r2pp", [128, 1])
        sw1t = alloc("sw1t", [128, KF])
        sw2t = alloc("sw2t", [128, KD])
        cs1t = alloc("cs1t", [128, KF])
        cs2t = alloc("cs2t", [128, KD])
        scale1 = alloc("scale1", [128, KF])
        bias1 = alloc("bias1", [128, KF])
        scale2 = alloc("scale2", [128, KD])
        bias2 = alloc("bias2", [128, KD])
        b1t = alloc("b1t", [128, KF])
        b2t = alloc("b2t", [128, KD])
        eps_ln = alloc("eps_ln", [128, 1])
        zcol = alloc("zcol", [128, 1])
        red1 = alloc("red1", [128, 1])
        redA = alloc("redA", [128, 1])
        agb = alloc("agb", [128, 80])
        qb1 = alloc("qb1", [128, KF])
        qb2 = alloc("qb2", [128, KF])
        qb3 = alloc("qb3", [128, KF])
        qb4 = alloc("qb4", [128, KF])
        qb5 = alloc("qb5", [128, KF])
        qb6 = alloc("qb6", [128, KF])
        diag_dbg_sb = alloc("diag_sb", [128, 128])
        dumv = alloc("dumv", [128, 1])
        dumg = alloc("dumg", [128, 1])
        agst = alloc("agst", [128, 10])
        duma = alloc("duma", [128, 1])

        t_id = P.dma("sync", ident[:], ident_in, "ld")
        t_mk = P.dma("sync", mask8[:], mask_in, "ld")
        t_b1 = P.dma("sync", b1t[:], b1_in, "ld")
        t_b2 = P.dma("sync", b2t[:], b2_in, "ld")
        P.wait("vector", t_id)
        P.op("vector", lambda e: e.tensor_copy(identb[:], ident[:]))
        P.op("vector", lambda e: e.memset(ones128[:], 1.0))
        P.op("vector", lambda e: e.memset(eps_ln[:], EPS_LN))
        P.op("vector", lambda e: e.memset(m1loc[:], -1e30))
        P.op("vector", lambda e: e.memset(m2loc[:], 0.0))
        P.op("vector", lambda e: e.memset(totals[:], 0.0))
        P.op("vector", lambda e: e.memset(offs[:], 0.0))
        P.op("vector", lambda e: e.memset(dumv[:], 0.0))
        P.op("vector", lambda e: e.memset(duma[:], 0.0))
        P.op("vector", lambda e: e.memset(dumg[:], 0.0))
        P.scratch = {"vector": dumv[:], "scalar": duma[:], "gpsimd": dumg[:]}
        P.op("vector", lambda e: e.memset(sbias_a[:], -127.5))
        P.op("vector", lambda e: e.memset(sbias_w[:], 0.5))
        P.op("vector", lambda e: e.memset(zbias[:], 0.0))
        t_init = P.op("vector", lambda e: e.tensor_copy(zcol[:], eps_ln[:]), sem="v")

        def emit_allmax_bcast(src8_ap, dst_pp, dep, br_ps):
            P.wait("vector", dep)
            tr = P.sop("vector", lambda e: e.tensor_reduce(
                red1[:], src8_ap, AX, ALU.max), "v")
            P.wait("tensor", tr)
            tt = P.op("tensor", lambda e: e.transpose(
                br_ps[0:1, 0:128], red1[:], ident[:]), sem="t")
            P.wait("vector", tt)
            tr2 = P.sop("vector", lambda e: e.tensor_reduce(
                redA[0:1, 0:1], br_ps[0:1, 0:128], AX, ALU.max), "v")
            P.sop("vector", lambda e: e.memset(zcol[:], 0.0), "v")
            tz = P.sop("vector", lambda e: e.tensor_copy(
                zcol[0:1, 0:1], redA[0:1, 0:1]), "v")
            tz = P.settle("vector", "v", dumv[:])
            P.wait("tensor", tz)
            tb = P.op("tensor", lambda e: e.matmul(
                br_ps[:, 0:1], ones128[:], zcol[:], start=True, stop=True), sem="t")
            P.wait("vector", tb)
            P.sop("vector", lambda e: e.tensor_copy(dst_pp, br_ps[:, 0:1]), "v")
            return P.settle("vector", "v", dumv[:])

        st_atab = contextlib.ExitStack()
        atabs = [alloc("tab" + str(i), [128, KD * C2], stack=st_atab)
                 for i in range(6)]

        with contextlib.ExitStack() as st1:
            wsh1 = alloc("wsh1", [128, 4 * D], stack=st1)
            wsh2 = alloc("wsh2", [128, FF], stack=st1)
            swsh1 = alloc("swsh1", [128, 4], stack=st1)
            rwsh1 = alloc("rwsh1", [128, 4], stack=st1)
            swsh2 = alloc("swsh2", [128, 1], stack=st1)
            rwsh2 = alloc("rwsh2", [128, 1], stack=st1)
            st1a = contextlib.ExitStack()
            cc_t = alloc("cc_t", [128, KD * C2], stack=st1a)
            bb_t = alloc("bb_t", [128, KD * C2], stack=st1a)
            wmy = alloc("wmy", [128, KD * 128], stack=st1a)
            agin = alloc("agin", [128, 6 * C2 + 5], stack=st1a)
            comb_ps = st1a.enter_context(nc.psum_tensor("comb_ps", [128, C2], F32))

            t_mcc = P.dma("sync", cc_t[:], mcc_in, "ld")
            t_mbb = P.dma("sync", bb_t[:], mbb_in, "ld")
            P.wait("scalar", t_mcc)
            t_cc = P.op("scalar", lambda e: e.activation(
                cc_t[:], cc_t[:], AF.Sin, bias=0.0, scale=SIN_SCL), sem="a")
            P.wait("scalar", t_mbb)
            t_bb = P.op("scalar", lambda e: e.activation(
                bb_t[:], bb_t[:], AF.Sin, bias=0.0, scale=SIN_SCL), sem="a")

            t_w1 = P.dma("sync", wsh1[:].rearrange("p (c d) -> p c d", c=4),
                         w1_my.rearrange("(c p) d -> p c d", p=128), "ld")
            t_w2 = P.dma("sync", wsh2[:], w2_my, "ld")
            P.wait("vector", t_w1)
            for ch in range(4):
                P.sop("vector", lambda e, ch=ch: e.tensor_reduce(
                    swsh1[:, ch:ch + 1], wsh1[:, ch * D:(ch + 1) * D], AX, ALU.max,
                    apply_absolute_value=True), "v")
            P.sop("vector", lambda e: e.tensor_scalar(
                swsh1[:], swsh1[:], EPS_Q, 1.0 / 127.0, ALU.max, ALU.mult), "v")
            t_sw1l = P.sop("vector", lambda e: e.reciprocal(rwsh1[:], swsh1[:]),
                           "v")
            P.wait("vector", t_w2)
            P.sop("vector", lambda e: e.tensor_reduce(
                swsh2[:], wsh2[:], AX, ALU.max, apply_absolute_value=True), "v")
            P.sop("vector", lambda e: e.tensor_scalar(
                swsh2[:], swsh2[:], EPS_Q, 1.0 / 127.0, ALU.max, ALU.mult), "v")
            t_sw2l = P.sop("vector", lambda e: e.reciprocal(rwsh2[:], swsh2[:]),
                           "v")
            P.sop("vector", lambda e: e.tensor_copy(agin[:, 6 * C2:6 * C2 + 4],
                                                    swsh1[:]), "v")
            P.sop("vector", lambda e: e.tensor_copy(
                agin[:, 6 * C2 + 4:6 * C2 + 5], swsh2[:]), "v")
            t_scl = P.settle("vector", "v", dumv[:])

            tok_last_ev = None
            tok_last_mm = None
            for wi, wap in enumerate((wq_my, wk_my, wv_my)):
                P.wait("sync", tok_last_mm)
                t_wm = P.dma("sync", wmy[:].rearrange("p (k d) -> p k d", k=KD),
                             wap.rearrange("(k p) d -> p k d", p=128), "ld")
                for ti, (tbl, ttok) in enumerate(((cc_t, t_cc), (bb_t, t_bb))):
                    P.wait("tensor", [t_wm, ttok])
                    if tok_last_ev is not None:
                        P.wait("tensor", tok_last_ev)
                    tmm = None
                    for k in range(KD):
                        sem = "t" if k == KD - 1 else None
                        tmm = P.op("tensor", lambda e, k=k, tbl=tbl: e.matmul(
                            comb_ps[:], wmy[:, k * 128:(k + 1) * 128],
                            tbl[:, k * C2:(k + 1) * C2],
                            start=(k == 0), stop=(k == KD - 1)), sem=sem) or tmm
                    slot = 2 * wi + ti
                    tok_last_mm = tmm
                    P.wait("scalar", tmm)
                    tok_last_ev = P.op("scalar", lambda e, s=slot: e.copy(
                        agin[:, s * C2:(s + 1) * C2], comb_ps[:]), sem="a")

            tok_last_ev = P.settle("scalar", "a", duma[:])
            P.wait("sync", [tok_last_ev, t_scl])
            if debug:
                P.dma("sync", cc_dbg, cc_t[:], "ld")
                P.dma("sync", agi_dbg, agin[:], "ld")
            t_agin = P.dma("sync", ag1_in.ap(), agin[:], "ld")
            P.wait("gpsimd", t_agin)
            t_ag1 = P.op("gpsimd", lambda e: e.collective_compute(
                "AllGather", ALU.bypass, replica_groups=[list(range(NCORES))],
                ins=[ag1_in.ap()], outs=[ag1_out.ap()]), sem="cc")
            P.wait("sync", t_ag1)
            t_tabs = None
            for m in range(6):
                t_tabs = P.dma(
                    "sync", atabs[m][:].rearrange("p (k c) -> p k c", k=KD),
                    ag1_out.ap()[:, m * C2:(m + 1) * C2].rearrange(
                        "(k p) c -> p k c", p=128), "ld")
            if debug:
                P.wait("sync", t_tabs)
                P.dma("sync", tab_dbg, atabs[0][:], "ld")
            t_sw1 = P.dma("sync", sw1t[:].rearrange("p (r c) -> p r c", r=8),
                          ag1_out.ap()[:, 6 * C2:6 * C2 + 4].rearrange(
                              "(r p) c -> p r c", p=128), "ld")
            t_sw2 = P.dma("sync", sw2t[:].rearrange("p (r c) -> p r c", r=8),
                          ag1_out.ap()[:, 6 * C2 + 4:6 * C2 + 5].rearrange(
                              "(r p) c -> p r c", p=128), "ld", slow=True)

            st1a.close()
            qsh = alloc("qsh", [128, 4 * D], F32, stack=st1)
            wbufs = tuple(alloc("wqb%d" % i, [128, D], stack=st1)
                          for i in range(6))
            tq = []
            tprev = None
            for pr in range(2):
                s0, s1 = 2 * pr, 2 * pr + 1
                tkA, tkB, tprev = emit_quant_pair(
                    P, [wsh1[:, s0 * D:(s0 + 1) * D], wsh1[:, s1 * D:(s1 + 1) * D]],
                    [rwsh1[:, s0:s0 + 1], rwsh1[:, s1:s1 + 1]], wbufs,
                    [qsh[:, s0 * D:(s0 + 1) * D], qsh[:, s1 * D:(s1 + 1) * D]],
                    [t_sw1l, tprev, t_agin], sbias_w, weight_mode=True)
                tq += [tkA, tkB]
            tq.append(P.settle("vector", "v", dumv[:]))
            P.wait("sync", tq)
            t_q1d = P.dma("sync", agw1_in.ap().rearrange("(c p) d -> p c d", p=128),
                          qsh[:].rearrange("p (c d) -> p c d", c=4), "ld")
            P.wait("gpsimd", t_q1d)
            t_agw1 = P.op("gpsimd", lambda e: e.collective_compute(
                "AllGather", ALU.bypass, replica_groups=[list(range(NCORES))],
                ins=[agw1_in.ap()], outs=[agw1_out.ap()]), sem="cc")

            tq = []
            for pr in range(2):
                s0, s1 = 2 * pr, 2 * pr + 1
                tkA, tkB, tprev = emit_quant_pair(
                    P, [wsh2[:, s0 * D:(s0 + 1) * D], wsh2[:, s1 * D:(s1 + 1) * D]],
                    [rwsh2[:], rwsh2[:]], wbufs,
                    [qsh[:, s0 * D:(s0 + 1) * D], qsh[:, s1 * D:(s1 + 1) * D]],
                    [t_sw2l, tprev, t_q1d], sbias_w, weight_mode=True)
                tq += [tkA, tkB]
            tq.append(P.settle("vector", "v", dumv[:]))
            P.wait("sync", tq)
            t_q2d = P.dma("sync", agw2_in.ap(), qsh[:], "ld")
            P.wait("gpsimd", t_q2d)
            t_agw2 = P.op("gpsimd", lambda e: e.collective_compute(
                "AllGather", ALU.bypass, replica_groups=[list(range(NCORES))],
                ins=[agw2_in.ap()], outs=[agw2_out.ap()]), sem="cc")

            tbin = alloc("tbin", [128, FF], F32, stack=st1)
            tbout = alloc("tbout", [128, KD * 128], BF16, stack=st1)
            tb_ps = st1.enter_context(nc.psum_tensor("tb_ps", [128, 1024], F32))

            P.wait("sync", t_agw1)
            t_dma_prev = None
            t_cs_prev = None
            for j in range(KF):
                if t_dma_prev is not None:
                    P.wait("sync", t_dma_prev)
                P.wait("sync", t_cs_prev)
                t_in = P.dma("sync", tbin[:, 0:D],
                             agw1_out.ap()[j * 128:(j + 1) * 128, 0:D], "ld")
                P.wait("vector", t_in)
                t_cs_prev = P.sop("vector", lambda e, j=j: e.tensor_reduce(
                    cs1t[:, j:j + 1], tbin[:, 0:D], AX, ALU.add), "v")
                P.wait("tensor", t_in)
                ttr = None
                for k in range(KD):
                    sem = "t" if k == KD - 1 else None
                    ttr = P.op("tensor", lambda e, k=k: e.transpose(
                        tb_ps[:, k * 128:(k + 1) * 128],
                        tbin[:, k * 128:(k + 1) * 128], ident[:]), sem=sem) or ttr
                P.wait("scalar", ttr)
                P.wait("scalar", t_dma_prev)
                tev = P.op("scalar", lambda e: e.copy(tbout[:], tb_ps[:, 0:1024]),
                           sem="a")
                P.wait("tensor", tev)
                tset = P.settle("scalar", "a", duma[:])
                P.wait("sync", [tset, ttr])
                t_dma_prev = P.dma(
                    "sync", qw1t_d[:, j * 1024:(j + 1) * 1024], tbout[:], "ld")

            P.wait("sync", t_agw2)
            for j in range(KD):
                P.wait("sync", t_dma_prev)
                P.wait("sync", t_cs_prev)
                t_in = P.dma("sync", tbin[:, 0:FF],
                             agw2_out.ap()[j * 128:(j + 1) * 128, 0:FF], "ld")
                P.wait("vector", t_in)
                t_cs_prev = P.sop("vector", lambda e, j=j: e.tensor_reduce(
                    cs2t[:, j:j + 1], tbin[:, 0:FF], AX, ALU.add), "v")
                P.wait("tensor", t_in)
                for kb in range(4):
                    ttr = None
                    for k in range(8):
                        kk = kb * 8 + k
                        sem = "t" if k == 7 else None
                        ttr = P.op("tensor", lambda e, kk=kk, k=k: e.transpose(
                            tb_ps[:, k * 128:(k + 1) * 128],
                            tbin[:, kk * 128:(kk + 1) * 128], ident[:]),
                            sem=sem) or ttr
                    P.wait("scalar", ttr)
                    P.wait("scalar", t_dma_prev)
                    tev = P.op("scalar", lambda e: e.copy(
                        tbout[:, 0:1024], tb_ps[:, 0:1024]), sem="a")
                    P.wait("tensor", tev)
                    tset = P.settle("scalar", "a", duma[:])
                    P.wait("sync", tset)
                    t_dma_prev = P.dma(
                        "sync",
                        qw2t_d[:, (j * 4 + kb) * 1024:(j * 4 + kb + 1) * 1024],
                        tbout[:, 0:1024], "ld")
            t_wt_done = t_dma_prev

        P.quiesce()
        with contextlib.ExitStack() as b1:
            xbuf = [alloc("xb0", [128, 4 * D], stack=b1),
                    alloc("xb1", [128, 4 * D], stack=b1)]
            sqt = alloc("sqt", [128, D], stack=b1)
            mt = alloc("mt", [128, 1], stack=b1)
            vt = alloc("vt", [128, 1], stack=b1)
            rstd = alloc("rstd", [128, 1], stack=b1)
            xnt = alloc("xnt", [128, KD * T], stack=b1)
            fqb = alloc("fqb", [128, 8 * T], stack=b1)
            fvt = alloc("fvt", [128, 2 * T], stack=b1)
            pb = alloc("pb", [128, 8 * T], stack=b1)
            vt1 = alloc("vt1", [128, T], stack=b1)
            vt2 = alloc("vt2", [128, T], stack=b1)
            ttp = alloc("ttp", [128, 1], stack=b1)
            tr_ps = [b1.enter_context(nc.psum_tensor("trp0", [128, 512], F32)),
                     b1.enter_context(nc.psum_tensor("trp1", [128, 512], F32))]
            pl_ps = [b1.enter_context(nc.psum_tensor("plp%d" % i, [128, 512], F32))
                     for i in range(6)]

            tok_x_free = [None, None]
            tok_xnt_sp = None
            tok_fq_sp = None
            tok_p_sp = None
            tok_proj_done = None
            tok_ev_done = None

            for i in range(NT):
                sl = i % 2
                P.wait("sync", tok_x_free[sl])
                t_x = P.dma("sync",
                            xbuf[sl][:].rearrange("p (c d) -> p c d", c=4),
                            x_in[i * T:(i + 1) * T, :].rearrange(
                                "(c p) d -> p c d", p=128), "ld")
                ln_toks = []
                P.wait("vector", t_x)
                for tc in range(4):
                    ch = xbuf[sl][:, tc * D:(tc + 1) * D]
                    P.sop("vector", lambda e, ch=ch: e.tensor_reduce(
                        mt[:], ch, AX, ALU.add), "v")
                    P.sop("vector", lambda e: e.tensor_single_scalar(
                        mt[:], mt[:], 1.0 / D, ALU.mult), "v")
                    tv = P.op("vector", lambda e, ch=ch: e.tensor_scalar(
                        ch, ch, mt[:, 0:1], None, ALU.subtract), sem="v")
                    P.wait("scalar", tv)
                    ta = P.op("scalar", lambda e, ch=ch: e.activation(
                        sqt[:], ch, AF.Square, bias=0.0, scale=1.0), sem="a")
                    P.wait("vector", ta)
                    P.sop("vector", lambda e: e.tensor_reduce(
                        vt[:], sqt[:], AX, ALU.add), "v")
                    tv2 = P.op("vector", lambda e: e.tensor_copy(vt[:], vt[:]),
                               sem="v")
                    P.wait("scalar", tv2)
                    P.op("scalar", lambda e: e.activation(
                        rstd[:], vt[:], AF.Sqrt, bias=eps_ln[:, 0:1],
                        scale=1.0 / D))
                    ta2 = P.op("scalar", lambda e: e.copy(duma[:], duma[:]),
                               sem="a")
                    P.wait("vector", ta2)
                    P.sop("vector", lambda e: e.reciprocal(rstd[:], rstd[:]), "v")
                    tln = P.op("vector", lambda e, ch=ch: e.tensor_scalar(
                        ch, ch, rstd[:, 0:1], None, ALU.mult), sem="v")
                    ln_toks.append(tln)
                if debug and i == 0:
                    P.wait("gpsimd", ln_toks)
                    P.dma("gpsimd", xn_dbg, xbuf[sl][:], "sp")
                    P.dma("gpsimd", id_dbg, ident[:], "sp")
                P.sop("vector", lambda e, sl=sl: e.tensor_reduce(
                    ttp[:], xbuf[sl][:], AX, ALU.max), "v")
                t_m1 = P.sop("vector", lambda e: e.tensor_max(
                    m1loc[:], m1loc[:], ttp[:]), "v")

                ev_toks = []
                for tc in range(4):
                    P.wait("tensor", ln_toks[tc])
                    ttr = None
                    for half in range(2):
                        ps = tr_ps[half]
                        for k in range(4):
                            kk = half * 4 + k
                            sem = "t" if k == 3 else None
                            ttr = P.op("tensor", lambda e, ps=ps, k=k, kk=kk, sl=sl,
                                       tc=tc: e.transpose(
                                           ps[:, k * 128:(k + 1) * 128],
                                           xbuf[sl][:, tc * D + kk * 128:
                                                    tc * D + (kk + 1) * 128],
                                           ident[:]), sem=sem) or ttr
                        P.wait("scalar", ttr)
                        if tok_proj_done is not None:
                            P.wait("scalar", tok_proj_done)
                        P.wait("scalar", tok_xnt_sp)
                        dst = xnt[:].rearrange("p (k t) -> p k t", k=KD)[
                            :, half * 4:(half + 1) * 4, tc * 128:(tc + 1) * 128]
                        tev = P.op("scalar", lambda e, dst=dst, ps=ps: e.copy(
                            dst, ps[:].rearrange("p (k t) -> p k t", k=4)), sem="a")
                        ev_toks.append(tev)
                        P.wait("tensor", tev)
                tok_x_free[sl] = ttr
                tok_ev_done = ev_toks[-1]

                t_set_a = P.settle("scalar", "a", duma[:])
                P.wait("gpsimd", [t_set_a] + ev_toks)
                tok_xnt_sp = P.dma("gpsimd", xnt_d[:, i * KD * T:(i + 1) * KD * T],
                                   xnt[:], "sp")

                p_toks = []
                for jc in range(4):
                    mm_tok = [None] * 6
                    for pl in range(6):
                        P.wait("tensor", ev_toks)
                        if i > 0 or jc > 0:
                            pass
                        P.wait("tensor", t_tabs)
                        for k in range(KD):
                            sem = "t" if k == KD - 1 else None
                            tk = P.op("tensor", lambda e, pl=pl, k=k, jc=jc: e.matmul(
                                pl_ps[pl][:],
                                atabs[pl][:, k * C2 + jc * 128:k * C2 + (jc + 1) * 128],
                                xnt[:, k * T:(k + 1) * T],
                                start=(k == 0), stop=(k == KD - 1)), sem=sem)
                            if sem:
                                mm_tok[pl] = tk
                    P.wait("scalar", [mm_tok[0], mm_tok[1], mm_tok[4], mm_tok[5]])
                    P.wait("scalar", tok_fq_sp)
                    e1 = P.op("scalar", lambda e, jc=jc: e.copy(
                        fqb[:, jc * T:(jc + 1) * T], pl_ps[0][:]), sem="a")
                    e2 = P.op("scalar", lambda e, jc=jc: e.copy(
                        fqb[:, (4 + jc) * T:(5 + jc) * T], pl_ps[1][:]), sem="a")
                    e3 = P.op("scalar", lambda e: e.copy(fvt[:, 0:T], pl_ps[4][:]),
                              sem="a")
                    e4 = P.op("scalar", lambda e: e.copy(fvt[:, T:2 * T],
                                                         pl_ps[5][:]), sem="a")
                    P.wait("vector", [mm_tok[2], mm_tok[3], e3, e4])
                    P.wait("vector", tok_p_sp)
                    P.op("vector", lambda e: e.tensor_mul(
                        vt1[:], pl_ps[2][:], fvt[:, 0:T]))
                    P.sop("vector", lambda e: e.tensor_mul(
                        vt2[:], pl_ps[3][:], fvt[:, T:2 * T]), "v")
                    P.op("vector", lambda e, jc=jc: e.tensor_sub(
                        pb[:, jc * T:(jc + 1) * T], vt1[:], vt2[:]))
                    P.op("vector", lambda e: e.tensor_mul(
                        vt1[:], pl_ps[2][:], fvt[:, T:2 * T]))
                    tpv = P.sop("vector", lambda e: e.tensor_mul(
                        vt2[:], pl_ps[3][:], fvt[:, 0:T]), "v")
                    P.op("vector", lambda e, jc=jc: e.tensor_add(
                        pb[:, (4 + jc) * T:(5 + jc) * T], vt1[:], vt2[:]))
                    if jc == 0:
                        P.sop("vector", lambda e: e.tensor_mul(
                            vt1[0:1, :], pl_ps[3][0:1, :], fvt[0:1, T:2 * T]), "v")
                        P.op("vector", lambda e: e.tensor_add(
                            pb[0:1, 0:T], pb[0:1, 0:T], vt1[0:1, :]))
                        P.op("vector", lambda e: e.tensor_copy(
                            pb[0:1, 4 * T:5 * T], vt1[0:1, :]))
                    P.sop("vector", lambda e, jc=jc: e.tensor_reduce(
                        ttp[:], pb[:, jc * T:(jc + 1) * T], AX, ALU.add), "v")
                    P.sop("vector", lambda e, jc=jc: e.tensor_add(
                        totals[:, jc:jc + 1], totals[:, jc:jc + 1], ttp[:]), "v")
                    P.sop("vector", lambda e, jc=jc: e.tensor_reduce(
                        ttp[:], pb[:, (4 + jc) * T:(5 + jc) * T], AX, ALU.add), "v")
                    tlast = P.sop("vector", lambda e, jc=jc: e.tensor_add(
                        totals[:, 4 + jc:5 + jc], totals[:, 4 + jc:5 + jc],
                        ttp[:]), "v")
                    p_toks.append(tlast)
                    P.wait("tensor", [tpv, tlast, e1, e2])
                tok_proj_done = p_toks[-1]
                fq_ev = [e1, e2, P.settle("scalar", "a", duma[:])]
                p_toks = p_toks + [P.settle("vector", "v", dumv[:])]
                tok_proj_done = p_toks[-1]
                P.wait("gpsimd", p_toks)
                P.wait("gpsimd", fq_ev)
                tok_fq_sp = P.dma("gpsimd", fq_d[:, i * 8 * T:(i + 1) * 8 * T],
                                  fqb[:], "sp")
                tok_p_sp = P.dma("gpsimd", p_d[:, i * 8 * T:(i + 1) * 8 * T],
                                 pb[:], "sp")

        st_atab.close()
        P.quiesce()
        st_if = contextlib.ExitStack()
        ifa = alloc("ifa", [128, 4 * D], stack=st_if)
        ifb = alloc("ifb", [128, 4 * D], stack=st_if)
        with contextlib.ExitStack() as bar1:
            br1_ps = bar1.enter_context(nc.psum_tensor("br1_ps", [128, 256], F32))
            coef_t = alloc("coef_t", [128, 8], stack=bar1)
            t_mia = P.dma("sync", ifa[:], mia_in, "ld")
            t_mib = P.dma("sync", ifb[:], mib_in, "ld")
            t_ca = P.dma("sync", coef_t[:, 0:4], coefa_in, "ld")
            t_cb = P.dma("sync", coef_t[:, 4:8], coefb_in, "ld")
            P.wait("scalar", t_mia)
            t_ifa = P.op("scalar", lambda e: e.activation(
                ifa[:], ifa[:], AF.Sin, bias=0.0, scale=SIN_SCL), sem="a")
            P.wait("scalar", t_mib)
            t_ifb = P.op("scalar", lambda e: e.activation(
                ifb[:], ifb[:], AF.Sin, bias=0.0, scale=SIN_SCL), sem="a")
            P.wait("vector", [t_ifa, t_ca])
            for cc in range(4):
                P.op("vector", lambda e, cc=cc: e.tensor_scalar(
                    ifa[:, cc * D:(cc + 1) * D], ifa[:, cc * D:(cc + 1) * D],
                    coef_t[:, cc:cc + 1], None, ALU.mult))
            P.wait("vector", [t_ifb, t_cb])
            t_iftab = None
            for cc in range(4):
                t_iftab = P.op("vector", lambda e, cc=cc: e.tensor_scalar(
                    ifb[:, cc * D:(cc + 1) * D], ifb[:, cc * D:(cc + 1) * D],
                    coef_t[:, 4 + cc:5 + cc], None, ALU.mult), sem="v")

            P.wait("vector", [tok_proj_done, t_m1])
            P.sop("vector", lambda e: e.tensor_copy(agst[:, 0:8], totals[:]), "v")
            P.sop("vector", lambda e: e.tensor_copy(agst[:, 8:9], m1loc[:]), "v")
            P.sop("vector", lambda e: e.memset(agst[:, 9:10], 0.0), "v")
            t_stl = P.settle("vector", "v", dumv[:])
            P.wait("sync", t_stl)
            t_a2a = P.dma("sync", ag2_in.ap(), agst[:], "ld")
            P.wait("gpsimd", t_a2a)
            t_ag2 = P.op("gpsimd", lambda e: e.collective_compute(
                "AllGather", ALU.bypass, replica_groups=[list(range(NCORES))],
                ins=[ag2_in.ap()], outs=[ag2_out.ap()]), sem="cc")
            P.wait("sync", t_ag2)
            t_agb = P.dma("sync", agb[:].rearrange("p (r c) -> p r c", r=8),
                          ag2_out.ap().rearrange("(r p) c -> p r c", p=128), "ld")

            P.wait("vector", t_agb)
            toff = None
            for r in range(8):
                toff = P.sop("vector", lambda e, r=r: e.scalar_tensor_tensor(
                    offs[:], agb[:, 10 * r:10 * r + 8], mask8[:, r:r + 1], offs[:],
                    ALU.mult, ALU.add), "v")
            m1cands = agb[:].rearrange("p (r c) -> p c r", r=8)[:, 8:9, :]
            tmax = emit_allmax_bcast(m1cands, s1pp[:], t_agb, br1_ps)
            P.wait("vector", tmax)
            P.sop("vector", lambda e: e.tensor_scalar(
                s1pp[:], s1pp[:], EPS_Q, 1.0 / 255.0, ALU.max, ALU.mult), "v")
            t_s1 = P.sop("vector", lambda e: e.reciprocal(r1pp[:], s1pp[:]), "v")

            P.wait("vector", [t_sw1, t_b1, t_s1])
            P.sop("vector", lambda e: e.tensor_scalar(
                scale1[:], sw1t[:], s1pp[:, 0:1], None, ALU.mult), "v")
            P.sop("vector", lambda e: e.reciprocal(qb1[:], scale1[:]), "v")
            tyb = P.sop("vector", lambda e: e.tensor_mul(qb2[:], b1t[:], qb1[:]),
                        "v")
            tq1 = emit_round_wide(P, qb2[:], qb3[:], qb4[:], qb1[:], qb6[:],
                                  qb5[:], zbias, [tyb])
            P.wait("vector", tq1)
            P.sop("vector", lambda e: e.tensor_mul(qb3[:], qb3[:], scale1[:]), "v")
            P.sop("vector", lambda e: e.tensor_scalar(
                qb1[:], cs1t[:], 127.5, None, ALU.mult), "v")
            P.sop("vector", lambda e: e.tensor_mul(qb1[:], qb1[:], scale1[:]), "v")
            t_bias1 = P.sop("vector", lambda e: e.tensor_add(
                bias1[:], qb3[:], qb1[:]), "v")
            t_bias1 = P.settle("vector", "v", dumv[:])

        with contextlib.ExitStack() as b2:
            pb2 = [alloc("pb2_0", [128, 8 * T], stack=b2),
                   alloc("pb2_1", [128, 8 * T], stack=b2)]
            fb2 = [alloc("fb2_0", [128, 8 * T], stack=b2),
                   alloc("fb2_1", [128, 8 * T], stack=b2)]
            memb = [alloc("memb0", [128, 8 * T], stack=b2),
                    alloc("memb1", [128, 8 * T], stack=b2)]
            zb = alloc("zb", [128, 8 * T], stack=b2)
            sob = alloc("sob", [128, KD * T], stack=b2)
            zt0 = alloc("zt0", [128, T], stack=b2)
            vt1b = alloc("vt1b", [128, T], stack=b2)
            vt2b = alloc("vt2b", [128, T], stack=b2)
            if_ps = [b2.enter_context(nc.psum_tensor("ifp0", [128, 512], F32)),
                     b2.enter_context(nc.psum_tensor("ifp1", [128, 512], F32))]

            P.op("vector", lambda e: e.memset(zt0[:], 0.0))
            tok_pf_free = [None, None]
            tok_s_sp = None
            tok_z_done = None
            tok_zb_free = None
            for i in range(NT):
                sl = i % 2
                P.wait("sync", tok_pf_free[sl])
                t_p = P.dma("sync", pb2[sl][:], p_d[:, i * 8 * T:(i + 1) * 8 * T],
                            "ld")
                t_f = P.dma("sync", fb2[sl][:], fq_d[:, i * 8 * T:(i + 1) * 8 * T],
                            "ld")
                P.wait("vector", t_p)
                if i == 0:
                    P.wait("vector", toff)
                scan_toks = []
                for q in range(8):
                    if i == 0:
                        init = offs[:, q:q + 1]
                    else:
                        init = memb[1 - sl][:, q * T + T - 1:q * T + T]
                    tk = P.op("vector", lambda e, q=q, sl=sl, init=init: (
                        e.tensor_tensor_scan(
                            memb[sl][:, q * T:(q + 1) * T],
                            pb2[sl][:, q * T:(q + 1) * T], zt0[:],
                            init, ALU.add, ALU.add)), sem="v")
                    scan_toks.append(tk)
                P.wait("vector", t_f)
                P.wait("vector", tok_zb_free)
                z_toks = []
                for jc in range(4):
                    mre = memb[sl][:, jc * T:(jc + 1) * T]
                    mim = memb[sl][:, (4 + jc) * T:(5 + jc) * T]
                    qre = fb2[sl][:, jc * T:(jc + 1) * T]
                    qim = fb2[sl][:, (4 + jc) * T:(5 + jc) * T]
                    zre = zb[:, jc * T:(jc + 1) * T]
                    zim = zb[:, (4 + jc) * T:(5 + jc) * T]
                    P.op("vector", lambda e, a=mre, b=qre: e.tensor_mul(
                        vt1b[:], a, b))
                    P.sop("vector", lambda e, a=mim, b=qim: e.tensor_mul(
                        vt2b[:], a, b), "v")
                    P.op("vector", lambda e, o=zre: e.tensor_add(
                        o, vt1b[:], vt2b[:]))
                    P.op("vector", lambda e, a=mim, b=qre: e.tensor_mul(
                        vt1b[:], a, b))
                    P.sop("vector", lambda e, a=mre, b=qim: e.tensor_mul(
                        vt2b[:], a, b), "v")
                    tk = P.op("vector", lambda e, o=zim: e.tensor_sub(
                        o, vt1b[:], vt2b[:]), sem="v")
                    if jc == 0:
                        P.sop("vector", lambda e, a=mim, b=qim: e.tensor_mul(
                            vt1b[0:1, :], a[0:1, :], b[0:1, :]), "v")
                        P.op("vector", lambda e, o=zre: e.tensor_sub(
                            o[0:1, :], o[0:1, :], vt1b[0:1, :]))
                        tk = P.op("vector", lambda e, o=zim: e.tensor_copy(
                            o[0:1, :], vt1b[0:1, :]), sem="v")
                    z_toks.append(tk)
                tok_pf_free[sl] = z_toks[-1]
                tok_z_done = z_toks[-1]
                s_ev = []
                for k in range(KD):
                    ps = if_ps[k % 2]
                    P.wait("tensor", z_toks)
                    P.wait("tensor", t_iftab)
                    tmm = None
                    for q in range(8):
                        tbl = ifa if q < 4 else ifb
                        cc = q % 4
                        sem = "t" if q == 7 else None
                        tmm = P.op("tensor", lambda e, ps=ps, tbl=tbl, cc=cc, k=k,
                                   q=q: e.matmul(
                                       ps[:], tbl[:, cc * D + k * 128:
                                                  cc * D + (k + 1) * 128],
                                       zb[:, q * T:(q + 1) * T],
                                       start=(q == 0), stop=(q == 7)),
                                   sem=sem) or tmm
                    P.wait("scalar", tmm)
                    P.wait("scalar", tok_s_sp)
                    tev = P.op("scalar", lambda e, k=k, ps=ps: e.copy(
                        sob[:, k * T:(k + 1) * T], ps[:]), sem="a")
                    s_ev.append(tev)
                    P.wait("tensor", tev)
                tok_zb_free = tmm
                s_ev.append(P.settle("scalar", "a", duma[:]))
                P.wait("gpsimd", s_ev)
                tok_s_sp = P.dma("gpsimd", s_d[:, i * KD * T:(i + 1) * KD * T],
                                 sob[:], "sp")
            tok_b2_sp = tok_s_sp
        st_if.close()
        P.quiesce()

        with contextlib.ExitStack() as pc:
            qw1sb = alloc("qw1sb", [128, KD * FF], BF16, stack=pc)
            xc = [alloc("xc0", [128, KD * T], stack=pc),
                  alloc("xc1", [128, KD * T], stack=pc)]
            xq = alloc("xq", [128, KD * T], BF16, stack=pc)
            hb = alloc("hb", [128, KF * T], stack=pc)
            qbufs = tuple(alloc("qcb%d" % i, [128, 1024], stack=pc)
                          for i in range(6))
            ttp2 = alloc("ttp2", [128, 1], stack=pc)
            h_ps = [pc.enter_context(nc.psum_tensor("hp%d" % v_, [128, 512], F32))
                    for v_ in range(2)]

            t_qw1 = P.dma("sync", qw1sb[:], qw1t_d, "ld")
            tok_xc_free = [None, None]
            tok_h_sp = [None, None]
            tok_m2 = None
            tok_m2_prev = None
            tok_xq_free = None
            tq_prev = None
            for i in range(NT):
                sl = i % 2
                P.wait("sync", tok_xc_free[sl])
                t_xc = P.dma("sync", xc[sl][:],
                             xnt_d[:, i * KD * T:(i + 1) * KD * T], "ld")
                tq_toks = []
                for pr in range(2):
                    a0, a1 = 2 * pr * 1024, (2 * pr + 1) * 1024
                    tkA, tkB, tq_prev = emit_quant_pair(
                        P, [xc[sl][:, a0:a0 + 1024], xc[sl][:, a1:a1 + 1024]],
                        [r1pp[:, 0:1], r1pp[:, 0:1]], qbufs,
                        [xq[:, a0:a0 + 1024], xq[:, a1:a1 + 1024]],
                        [t_xc, t_s1, tq_prev, tok_xq_free], sbias_a)
                    tq_toks += [tkA, tkB]
                tok_xc_free[sl] = tq_toks[-1]
                if debug and i == 0:
                    P.wait("sync", tq_toks)
                    P.dma("sync", xq_dbg, xq[:], "ld")
                ev_toks = []
                for fj in range(KF):
                    ps = h_ps[fj % 2]
                    P.wait("tensor", tq_toks)
                    P.wait("tensor", t_qw1)
                    tmm = None
                    for k in range(KD):
                        sem = "t" if k == KD - 1 else None
                        tmm = P.op("tensor", lambda e, ps=ps, fj=fj, k=k: e.matmul(
                            ps[:], qw1sb[:, fj * 1024 + k * 128:fj * 1024 + (k + 1) * 128],
                            xq[:, k * T:(k + 1) * T],
                            start=(k == 0), stop=(k == KD - 1)), sem=sem) or tmm
                    P.wait("scalar", tmm)
                    P.wait("scalar", [t_bias1, tok_h_sp[0 if fj < 16 else 1],
                                      tok_m2_prev])
                    tev = P.op("scalar", lambda e, fj=fj, ps=ps: e.activation(
                        hb[:, fj * T:(fj + 1) * T], ps[:], AF.Relu,
                        bias=bias1[:, fj:fj + 1], scale=scale1[:, fj:fj + 1]),
                        sem="a")
                    ev_toks.append(tev)
                    P.wait("tensor", tev)
                    tok_xq_free = tmm
                P.wait("vector", ev_toks)
                P.sop("vector", lambda e: e.tensor_reduce(
                    ttp2[:], hb[:], AX, ALU.max), "v")
                tok_m2 = P.sop("vector", lambda e: e.tensor_max(
                    m2loc[:], m2loc[:], ttp2[:]), "v")
                tok_m2_prev = tok_m2
                P.quiesce()
                t_set_c = P.settle("scalar", "a", duma[:])
                for hf in range(2):
                    P.wait("gpsimd", [ev_toks[hf * 16 + 15], t_set_c])
                    tok_h_sp[hf] = P.dma(
                        "gpsimd",
                        h_d[:, i * KF * T + hf * 16 * T:
                            i * KF * T + (hf + 1) * 16 * T],
                        hb[:, hf * 16 * T:(hf + 1) * 16 * T], "sp")

        P.quiesce()
        with contextlib.ExitStack() as bar2:
            br2_ps = bar2.enter_context(nc.psum_tensor("br2_ps", [128, 256], F32))
            t_stl2 = P.settle("vector", "v", dumv[:])
            P.wait("sync", [tok_m2, t_stl2])
            t_a3 = P.dma("sync", ag3_in.ap(), m2loc[:], "ld")
            P.wait("gpsimd", t_a3)
            t_ag3 = P.op("gpsimd", lambda e: e.collective_compute(
                "AllGather", ALU.bypass, replica_groups=[list(range(NCORES))],
                ins=[ag3_in.ap()], outs=[ag3_out.ap()]), sem="cc")
            P.wait("sync", t_ag3)
            t_agb3 = P.dma("sync", agb[:, 0:8].rearrange("p (r c) -> p r c", r=8),
                           ag3_out.ap().rearrange("(r p) c -> p r c", p=128), "ld",
                           slow=True)
            tmax2 = emit_allmax_bcast(agb[:, 0:8], s2pp[:], t_agb3, br2_ps)
            P.wait("vector", tmax2)
            P.sop("vector", lambda e: e.tensor_scalar(
                s2pp[:], s2pp[:], EPS_Q, 1.0 / 255.0, ALU.max, ALU.mult), "v")
            t_s2 = P.sop("vector", lambda e: e.reciprocal(r2pp[:], s2pp[:]), "v")
            P.wait("vector", [t_sw2, t_b2, t_s2])
            P.sop("vector", lambda e: e.tensor_scalar(
                scale2[:], sw2t[:], s2pp[:, 0:1], None, ALU.mult), "v")
            P.sop("vector", lambda e: e.reciprocal(qb1[:, 0:KD], scale2[:]), "v")
            tyb2 = P.sop("vector", lambda e: e.tensor_mul(
                qb2[:, 0:KD], b2t[:], qb1[:, 0:KD]), "v")
            tq2r = emit_round_wide(P, qb2[:, 0:KD], qb3[:, 0:KD], qb4[:, 0:KD],
                                   qb1[:, 0:KD], qb6[:, 0:KD], qb5[:, 0:KD],
                                   zbias, [tyb2])
            P.wait("vector", tq2r)
            P.sop("vector", lambda e: e.tensor_mul(
                qb3[:, 0:KD], qb3[:, 0:KD], scale2[:]), "v")
            P.sop("vector", lambda e: e.tensor_scalar(
                qb1[:, 0:KD], cs2t[:], 127.5, None, ALU.mult), "v")
            P.sop("vector", lambda e: e.tensor_mul(
                qb1[:, 0:KD], qb1[:, 0:KD], scale2[:]), "v")
            t_bias2 = P.sop("vector", lambda e: e.tensor_add(
                bias2[:], qb3[:, 0:KD], qb1[:, 0:KD]), "v")
            t_bias2 = P.settle("vector", "v", dumv[:])

        with contextlib.ExitStack() as pd:
            qw2sb = alloc("qw2sb", [128, KF * D], BF16, stack=pd)
            hD = alloc("hd0", [128, KF * TD], stack=pd)
            qh = alloc("qh", [128, KF * TD], BF16, stack=pd)
            gb = alloc("gb", [128, KD * TD], stack=pd)
            sD = [alloc("sd0", [128, KD * TD], stack=pd),
                  alloc("sd1", [128, KD * TD], stack=pd)]
            rb = alloc("rb", [128, KD * TD], stack=pd)
            xD = [alloc("xd0", [128, 2 * D], stack=pd),
                  alloc("xd1", [128, 2 * D], stack=pd)]
            ob = alloc("ob", [128, 2 * D], stack=pd)
            dbufs = tuple(alloc("qdb%d" % i, [128, 1024], stack=pd)
                          for i in range(6))
            y2_ps = [pd.enter_context(nc.psum_tensor("y2p%d" % v_, [128, 256], F32))
                     for v_ in range(2)]
            rt_ps = [pd.enter_context(nc.psum_tensor("rtp%d" % v_, [128, 1024], F32))
                     for v_ in range(2)]
            scrap_ps = pd.enter_context(nc.psum_tensor("scrap_ps", [128, 128], F32))

            t_qw2 = P.dma("sync", qw2sb[:], qw2t_d, "ld")
            if debug:
                P.wait("vector", t_bias2)
                dcols = [(s1pp, 0), (r1pp, 1), (s2pp, 2), (r2pp, 3), (m1loc, 4),
                         (m2loc, 5)]
                for t_, j in dcols:
                    P.sop("vector", lambda e, t_=t_, j=j: e.tensor_copy(
                        diag_dbg_sb[:, j:j + 1], t_[:]), "v")
                for t_, j0, w in [(scale2, 8, KD), (bias2, 16, KD),
                                  (scale1, 24, KF), (bias1, 56, KF),
                                  (sw2t, 88, KD), (cs2t, 96, KD), (b2t, 104, KD)]:
                    P.sop("vector", lambda e, t_=t_, j0=j0, w=w: e.tensor_copy(
                        diag_dbg_sb[:, j0:j0 + w], t_[:]), "v")
                tdg = P.op("vector", lambda e: e.tensor_copy(
                    diag_dbg_sb[:, 120:121], eps_ln[:]), sem="v")
                P.wait("sync", tdg)
                P.dma("sync", diag_dbg, diag_dbg_sb[:], "ld")
                P.dma("sync", qw1_dbg, qw1t_d, "ld")
                P.dma("sync", qw2_dbg, qw2t_d, "ld")
            tok_hD_free = [None, None]
            tok_sD_free = [None, None]
            tok_xD_free = [None, None]
            tok_ob_free = None
            tok_qh_free = None
            tok_rb_free = None
            tok_gb_free = None
            dq_prev = None
            t_out_last = None
            for t in range(NTD):
                sl = t % 2
                ci, hf = t // 2, t % 2
                t_hh = []
                for hh in range(2):
                    P.wait("sync", tok_hD_free[hh])
                    tk = P.dma(
                        "sync",
                        hD[:, hh * 16 * TD:(hh + 1) * 16 * TD].rearrange(
                            "p (o f w) -> p o f w", o=1, f=16),
                        h_d.rearrange("p (i f w) -> p i f w", i=NT, f=KF)[
                            :, ci:ci + 1, hh * 16:(hh + 1) * 16,
                            hf * TD:(hf + 1) * TD], "ld")
                    t_hh.append(tk)
                P.wait("sync", tok_sD_free[sl])
                t_s = P.dma(
                    "sync", sD[sl][:].rearrange("p (o k w) -> p o k w", o=1, k=KD),
                    s_d.rearrange("p (i k w) -> p i k w", i=NT, k=KD)[
                        :, ci:ci + 1, :, hf * TD:(hf + 1) * TD], "ld")
                P.wait("sync", tok_xD_free[sl])
                t_x2 = P.dma("sync", xD[sl][:].rearrange("p (c d) -> p c d", c=2),
                             x_in[t * TD:(t + 1) * TD, :].rearrange(
                                 "(c p) d -> p c d", p=128), "ld")
                tq_toks = []
                for pr in range(4):
                    a0, a1 = 2 * pr * 1024, (2 * pr + 1) * 1024
                    tkA, tkB, dq_prev = emit_quant_pair(
                        P, [hD[:, a0:a0 + 1024], hD[:, a1:a1 + 1024]],
                        [r2pp[:, 0:1], r2pp[:, 0:1]], dbufs,
                        [qh[:, a0:a0 + 1024], qh[:, a1:a1 + 1024]],
                        [t_hh[pr // 2], t_s2, dq_prev, tok_qh_free], sbias_a)
                    tq_toks += [tkA, tkB]
                    if pr == 1:
                        tok_hD_free[0] = tkB
                tok_hD_free[1] = tq_toks[-1]
                g_toks = []
                for k in range(KD):
                    ps = y2_ps[k % 2]
                    P.wait("tensor", tq_toks)
                    P.wait("tensor", t_qw2)
                    tmm = None
                    for fj in range(KF):
                        sem = "t" if fj == KF - 1 else None
                        col = (k * 4 + fj // 8) * 1024 + (fj % 8) * 128
                        tmm = P.op("tensor", lambda e, ps=ps, fj=fj, c0=col: e.matmul(
                            ps[:], qw2sb[:, c0:c0 + 128],
                            qh[:, fj * TD:(fj + 1) * TD],
                            start=(fj == 0), stop=(fj == KF - 1)), sem=sem) or tmm
                    P.wait("scalar", [tmm, t_bias2, tok_gb_free])
                    tev = P.op("scalar", lambda e, k=k, ps=ps: e.activation(
                        gb[:, k * TD:(k + 1) * TD], ps[:], AF.Silu,
                        bias=bias2[:, k:k + 1], scale=scale2[:, k:k + 1]), sem="a")
                    g_toks.append(tev)
                    P.wait("tensor", tev)
                tok_qh_free = tmm
                if debug and t == 0:
                    P.wait("sync", tq_toks)
                    P.dma("sync", qh_dbg, qh[:], "ld")
                    P.wait("sync", g_toks)
                    P.dma("sync", gb_dbg, gb[:], "ld")
                P.wait("vector", [t_s, tok_rb_free] + g_toks)
                t_r = P.op("vector", lambda e, sl=sl: e.tensor_mul(
                    rb[:], sD[sl][:], gb[:]), sem="v")
                tok_sD_free[sl] = t_r
                tok_gb_free = t_r
                t_ob = []
                for tc in range(2):
                    ps = rt_ps[tc]
                    P.wait("tensor", t_r)
                    ttr = None
                    for k in range(KD):
                        sem = "t" if k == KD - 1 else None
                        ttr = P.op("tensor", lambda e, ps=ps, k=k, tc=tc: e.transpose(
                            ps[:, k * 128:(k + 1) * 128],
                            rb[:, k * TD + tc * 128:k * TD + (tc + 1) * 128],
                            ident[:]), sem=sem) or ttr
                    P.wait("vector", [ttr, t_x2, tok_ob_free])
                    tado = P.op("vector", lambda e, tc=tc, sl=sl, ps=ps: e.tensor_add(
                        ob[:, tc * D:(tc + 1) * D], xD[sl][:, tc * D:(tc + 1) * D],
                        ps[:]), sem="v")
                    t_ob.append(tado)
                    P.wait("tensor", tado)
                if debug:
                    P.wait("sync", tq_toks)
                    P.dma("sync", qhfull_dbg[:, t * KF * TD:(t + 1) * KF * TD],
                          qh[:], "ld")
                    P.wait("sync", g_toks)
                    P.dma("sync", gfull_dbg[:, t * KD * TD:(t + 1) * KD * TD],
                          gb[:], "ld")
                if debug and t == 0:
                    P.wait("sync", [t_r] + t_ob)
                    P.dma("sync", rb_dbg, rb[:], "ld")
                    P.dma("sync", ob_dbg, ob[:], "ld")
                    P.dma("sync", sd_dbg, sD[sl][:], "ld")
                tok_xD_free[sl] = t_ob[-1]
                tok_rb_free = ttr
                P.quiesce()
                t_ob = t_ob + [P.settle("vector", "v", dumv[:])]
                P.wait("sync", t_ob)
                t_out_last = P.dma(
                    "sync", out_d[t * TD:(t + 1) * TD, :].rearrange(
                        "(c p) d -> p c d", p=128),
                    ob[:].rearrange("p (c d) -> p c d", c=2), "st")
                tok_ob_free = t_out_last

            P.wait("sync", t_out_last)

        block = es.enter_context(nc.Block())
        P.replay(block)

    return nc


def _make_in_maps(L0_attn_x, L0_attn_Wq, L0_attn_Wk, L0_attn_Wv,
                  L0_ffn_w1, L0_ffn_b1, L0_ffn_w2, L0_ffn_b2):
    x = np.ascontiguousarray(np.asarray(L0_attn_x, np.float32)).reshape(32768, D)
    wq = np.ascontiguousarray(np.asarray(L0_attn_Wq, np.float32))
    wk = np.ascontiguousarray(np.asarray(L0_attn_Wk, np.float32))
    wv = np.ascontiguousarray(np.asarray(L0_attn_Wv, np.float32))
    w1 = np.ascontiguousarray(np.asarray(L0_ffn_w1, np.float32))
    b1 = np.ascontiguousarray(np.asarray(L0_ffn_b1, np.float32))
    w2 = np.ascontiguousarray(np.asarray(L0_ffn_w2, np.float32))
    b2 = np.ascontiguousarray(np.asarray(L0_ffn_b2, np.float32))
    cst = _host_consts()
    b1t = np.ascontiguousarray(b1.reshape(KF, 128).T)
    b2t = np.ascontiguousarray(b2.reshape(KD, 128).T)

    in_maps = []
    for c in range(NCORES):
        mask = np.zeros((128, 8), np.float32)
        if c % 2 == 1:
            mask[:, c - 1] = 1.0
        in_maps.append({
            "x": np.ascontiguousarray(x[c * NTOK:(c + 1) * NTOK]),
            "wq_my": np.ascontiguousarray(wq[:, 128 * c:128 * (c + 1)]),
            "wk_my": np.ascontiguousarray(wk[:, 128 * c:128 * (c + 1)]),
            "wv_my": np.ascontiguousarray(wv[:, 128 * c:128 * (c + 1)]),
            "w1_my": np.ascontiguousarray(w1[512 * c:512 * (c + 1)]),
            "w2_my": np.ascontiguousarray(w2[128 * c:128 * (c + 1)]),
            "b1t": b1t, "b2t": b2t,
            "ident": cst["ident"], "mcc": cst["mcc"], "mbb": cst["mbb"],
            "mia": cst["mia"], "mib": cst["mib"],
            "coefa": cst["coefa"], "coefb": cst["coefb"],
            "mask8": mask,
        })
    return in_maps


def kernel(**inputs):
    nc = build_program()
    in_maps = _make_in_maps(**inputs)
    res = run_bass_kernel_spmd(nc, in_maps, core_ids=list(range(NCORES)))
    out = np.concatenate([r["out"] for r in res.results], axis=0)
    return out.reshape(4, 8192, 1024)


# version 12
# speedup vs baseline: 1.0074x; 1.0074x over previous
import contextlib

import numpy as np
import ml_dtypes
import concourse.bass as bass
import concourse.mybir as mybir
from concourse.bass_utils import run_bass_kernel_spmd

F32 = mybir.dt.float32
BF16 = mybir.dt.bfloat16
ALU = mybir.AluOpType
AF = mybir.ActivationFunctionType
AX = mybir.AxisListType.X

NCORES = 8
D = 1024
FF = 4096
NTOK = 4096
T = 512
NT = NTOK // T
TD = 256
NTD = NTOK // TD
KD = D // 128
KF = FF // 128
C2 = 512
EPS_LN = 1e-5
EPS_Q = 1e-8
SIN_SCL = float(2.0 * np.pi / 1024.0)

ENGINES = ("sync", "tensor", "vector", "scalar", "gpsimd")


def _ctr(m):
    m = np.asarray(m, np.int64) % 1024
    return (m - 1024 * (m >= 512)).astype(np.float32)


def _rows_to_chunks(a, p=128):
    r, w = a.shape
    return np.ascontiguousarray(
        a.reshape(r // p, p, w).transpose(1, 0, 2).reshape(p, (r // p) * w))


_CACHE = {}


def _host_consts():
    if _CACHE:
        return _CACHE
    e = np.arange(1024)[:, None]
    c = np.arange(C2)[None, :]
    mcc = _ctr(e * c + 256)
    mbb = _ctr(-e * c)
    mbb[:, 0] = _ctr(256 + 512 * e)[:, 0]
    ci = np.arange(C2)[:, None]
    dd = np.arange(D)[None, :]
    mia = _ctr(ci * dd + 256)
    mib = _ctr(-ci * dd)
    mib[0, :] = _ctr(256 + 512 * dd)[0, :]
    coefa = np.full((C2, 1), 2.0 / 1024.0, np.float32)
    coefa[0, 0] = 1.0 / 1024.0
    coefb = coefa.copy()
    _CACHE.update(
        ident=np.eye(128, dtype=np.float32),
        mcc=_rows_to_chunks(mcc), mbb=_rows_to_chunks(mbb),
        mia=_rows_to_chunks(mia), mib=_rows_to_chunks(mib),
        coefa=_rows_to_chunks(coefa), coefb=_rows_to_chunks(coefb),
    )
    return _CACHE


SEM_OWNER = {"v": "vector", "a": "scalar", "g": "gpsimd"}


class Prog:
    def __init__(self):
        self.q = {e: [] for e in ENGINES}
        self.sem = {}
        self.cnt = {}
        self.mark = {e: {} for e in ENGINES}
        self.scratch = {}
        self.settled = {}
        self.dma_rr = {}

    def add_sem(self, name, handle):
        self.sem[name] = handle
        self.cnt[name] = 0

    def wait(self, eng, tok, raw=False):
        if tok is None:
            return
        if isinstance(tok, list):
            for t in tok:
                self.wait(eng, t, raw=raw)
            return
        name, val = tok
        owner = SEM_OWNER.get(name)
        if not raw and owner is not None and owner != eng:
            st = self.settled.get(name)
            if st is None or st[1] < val:
                cover = self.cnt[name]
                stok = self.sop(owner, self._mk_settle(owner), name)
                self.settled[name] = (stok, cover)
                tok = stok
            else:
                tok = st[0]
            name, val = tok
        if self.mark[eng].get(name, 0) >= val:
            return
        self.mark[eng][name] = val
        h = self.sem[name]
        self.q[eng].append(lambda e, h=h, v=val: e.wait_ge(h, v))

    def _mk_settle(self, owner):
        a = self.scratch[owner]
        if owner == "scalar":
            return lambda e, a=a: e.copy(a, a)
        return lambda e, a=a: e.tensor_copy(a, a)

    def op(self, eng, build, sem=None, n=1):
        if sem is None:
            self.q[eng].append(build)
            return None
        self.cnt[sem] += n
        tok = (sem, self.cnt[sem])
        h = self.sem[sem]
        self.q[eng].append(lambda e, b=build, h=h, n=n: b(e).then_inc(h, n))
        return tok

    def settle(self, eng, sem, scratch_ap):
        if eng == "scalar":
            return self.sop(eng, lambda e, a=scratch_ap: e.copy(a, a), sem)
        return self.sop(eng, lambda e, a=scratch_ap: e.tensor_copy(a, a), sem)

    def quiesce(self):
        for s, owner in SEM_OWNER.items():
            if self.cnt.get(s, 0) > 0:
                self.sop(owner, self._mk_settle(owner), s)
        snap = [(s, c) for s, c in self.cnt.items() if c > 0]
        for eng in ENGINES:
            for s, c in snap:
                self.wait(eng, (s, c), raw=True)

    def sop(self, eng, build, sem):
        tok = self.op(eng, build, sem=sem)
        self.wait(eng, tok, raw=True)
        return tok

    def dma(self, eng, out_ap, in_ap, sem, slow=False):
        self.dma_rr[sem] = self.dma_rr.get(sem, 0) + 1
        name = "%s%d" % (sem, self.dma_rr[sem] % 8)
        return self.op(eng, lambda e, o=out_ap, i=in_ap, s=slow: e.dma_start(
            out=o, in_=i, allow_slow_non_contiguous=s), sem=name, n=16)

    def replay(self, block):
        for eng in ENGINES:
            queue = self.q[eng]

            def run(e, queue=queue):
                for fn in queue:
                    fn(e)
            getattr(block, eng)(run)


def emit_quant_pair(P, srcs, r_pps, bufs, outs, deps, sbias_col, *,
                    weight_mode=False):
    yA, fA, gA, yB, fB, gB = [b[:] for b in bufs]
    lo = -128.0 if weight_mode else 0.0
    hi = 127.0 if weight_mode else 255.0
    P.wait("vector", deps)
    ty = {}
    for s, (src_ap, r_pp, y, f) in enumerate(
            ((srcs[0], r_pps[0], yA, fA), (srcs[1], r_pps[1], yB, fB))):
        P.op("vector", lambda e, y=y, src_ap=src_ap, r=r_pp: e.tensor_scalar(
            y, src_ap, r, lo, ALU.mult, ALU.max))
        P.op("vector", lambda e, y=y: e.tensor_single_scalar(y, y, hi, ALU.min))
        ty[s] = P.op("vector", lambda e, f=f, y=y: e.tensor_copy(f, y), sem="v")
    va = {0: ty[0], 1: ty[1]}
    ta = {}
    for i, k in enumerate(range(7, -1, -1)):
        c = -float(2 ** (k - 1))
        for s, (f, g) in enumerate(((fA, gA), (fB, gB))):
            P.wait("scalar", va[s], raw=True)
            ta[s] = P.op("scalar", lambda e, f=f, g=g: e.activation(
                g, f, AF.Sign, bias=sbias_col[:, 0:1], scale=1.0), sem="a")
        for s, (f, g) in enumerate(((fA, gA), (fB, gB))):
            P.wait("vector", ta[s], raw=True)
            va[s] = P.op("vector", lambda e, f=f, g=g, c=c: e.scalar_tensor_tensor(
                f, g, c, f, ALU.mult, ALU.add), sem="v")
    touts = []
    for s, (y, f, out_ap) in enumerate(((yA, fA, outs[0]), (yB, fB, outs[1]))):
        P.wait("vector", va[s], raw=True)
        if weight_mode:
            tk = P.op("vector", lambda e, y=y, f=f, o=out_ap: e.scalar_tensor_tensor(
                o, y, -0.5, f, ALU.add, ALU.subtract), sem="v")
        else:
            tk = P.op("vector", lambda e, y=y, f=f, o=out_ap: e.tensor_sub(
                o, y, f), sem="v")
        touts.append(tk)
    return touts[0], touts[1], ta[1]


def emit_round_wide(P, y_ap, q_ap, ab_ap, t1_ap, t2_ap, sg_ap, zbias, deps):
    P.wait("scalar", deps)
    tsg = P.op("scalar", lambda e: e.activation(sg_ap, y_ap, AF.Sign,
                                                bias=zbias[:, 0:1], scale=1.0),
               sem="a")
    tab = P.op("scalar", lambda e: e.activation(ab_ap, y_ap, AF.Abs,
                                                bias=zbias[:, 0:1], scale=1.0),
               sem="a")
    P.wait("vector", [tab, tsg])
    P.sop("vector", lambda e: e.tensor_scalar_add(t1_ap, ab_ap, 0.5), "v")
    for k in range(39, -1, -1):
        c = float(2 ** k)
        P.sop("vector", lambda e, c=c: e.tensor_scalar(t2_ap, t1_ap, c, -c,
                                                       ALU.is_ge, ALU.mult), "v")
        P.sop("vector", lambda e: e.tensor_add(t1_ap, t1_ap, t2_ap), "v")
    P.sop("vector", lambda e: e.scalar_tensor_tensor(t2_ap, ab_ap, 0.5, t1_ap,
                                                     ALU.add, ALU.subtract), "v")
    return P.sop("vector", lambda e: e.tensor_mul(q_ap, t2_ap, sg_ap), "v")


def build_program(debug=False):
    nc = bass.Bass("TRN2", target_bir_lowering=False)

    def din(name, shape, dtype=F32):
        return nc.dram_tensor(name, shape, dtype, kind="ExternalInput").ap()

    x_in = din("x", [NTOK, D])
    wq_my = din("wq_my", [D, 128])
    wk_my = din("wk_my", [D, 128])
    wv_my = din("wv_my", [D, 128])
    w1_my = din("w1_my", [512, D])
    w2_my = din("w2_my", [128, FF])
    b1_in = din("b1t", [128, KF])
    b2_in = din("b2t", [128, KD])
    ident_in = din("ident", [128, 128])
    mcc_in = din("mcc", [128, KD * C2])
    mbb_in = din("mbb", [128, KD * C2])
    mia_in = din("mia", [128, 4 * D])
    mib_in = din("mib", [128, 4 * D])
    coefa_in = din("coefa", [128, 4])
    coefb_in = din("coefb", [128, 4])
    mask_in = din("mask8", [128, 8])

    out_d = nc.dram_tensor("out", [NTOK, D], F32, kind="ExternalOutput").ap()

    kind_dbg = "ExternalOutput" if debug else "Internal"
    xnt_d = nc.dram_tensor("xnt_sp", [128, NT * KD * T], F32, kind=kind_dbg).ap()
    p_d = nc.dram_tensor("p_sp", [128, NT * 8 * T], F32, kind=kind_dbg).ap()
    fq_d = nc.dram_tensor("fq_sp", [128, NT * 8 * T], F32, kind=kind_dbg).ap()
    s_d = nc.dram_tensor("s_sp", [128, NT * KD * T], F32, kind=kind_dbg).ap()
    h_d = nc.dram_tensor("h_sp", [128, KF * NTOK], F32, kind=kind_dbg).ap()
    xn_dbg = nc.dram_tensor("xn_dbg", [128, 4 * D], F32, kind=kind_dbg).ap()
    cc_dbg = nc.dram_tensor("cc_dbg", [128, KD * C2], F32, kind=kind_dbg).ap()
    tab_dbg = nc.dram_tensor("tab_dbg", [128, KD * C2], F32, kind=kind_dbg).ap()
    agi_dbg = nc.dram_tensor("agi_dbg", [128, 6 * C2 + 5], F32, kind=kind_dbg).ap()
    diag_dbg = nc.dram_tensor("diag_dbg", [128, 128], F32, kind=kind_dbg).ap()
    qw1_dbg = nc.dram_tensor("qw1_dbg", [128, KD * FF], BF16, kind=kind_dbg).ap()
    xq_dbg = nc.dram_tensor("xq_dbg", [128, KD * T], BF16, kind=kind_dbg).ap()
    qh_dbg = nc.dram_tensor("qh_dbg", [128, KF * TD], BF16, kind=kind_dbg).ap()
    qw2_dbg = nc.dram_tensor("qw2_dbg", [128, KF * D], BF16, kind=kind_dbg).ap()
    gb_dbg = nc.dram_tensor("gb_dbg", [128, KD * TD], F32, kind=kind_dbg).ap()
    rb_dbg = nc.dram_tensor("rb_dbg", [128, KD * TD], F32, kind=kind_dbg).ap()
    gfull_dbg = nc.dram_tensor("gfull_dbg", [128, KD * NTOK], F32, kind=kind_dbg).ap()
    qhfull_dbg = nc.dram_tensor("qhfull_dbg", [128, KF * NTOK], BF16, kind=kind_dbg).ap()
    ob_dbg = nc.dram_tensor("ob_dbg", [128, 2 * D], F32, kind=kind_dbg).ap()
    sd_dbg = nc.dram_tensor("sd_dbg", [128, KD * TD], F32, kind=kind_dbg).ap()
    id_dbg = nc.dram_tensor("id_dbg", [128, 128], F32, kind=kind_dbg).ap()
    qw1t_d = nc.dram_tensor("qw1t_d", [128, KD * FF], BF16).ap()
    qw2t_d = nc.dram_tensor("qw2t_d", [128, KF * D], BF16).ap()

    ag1_in = nc.dram_tensor("ag1_in", [128, 6 * C2 + 5], F32)
    ag1_out = nc.dram_tensor("ag1_out", [1024, 6 * C2 + 5], F32)
    agw1_in = nc.dram_tensor("agw1_in", [512, D], F32)
    agw1_out = nc.dram_tensor("agw1_out", [FF, D], F32)
    agw2_in = nc.dram_tensor("agw2_in", [128, FF], F32)
    agw2_out = nc.dram_tensor("agw2_out", [D, FF], F32)
    ag2_in = nc.dram_tensor("ag2_in", [128, 10], F32)
    ag2_out = nc.dram_tensor("ag2_out", [1024, 10], F32)
    ag3_in = nc.dram_tensor("ag3_in", [128, 1], F32)
    ag3_out = nc.dram_tensor("ag3_out", [1024, 1], F32)

    P = Prog()
    es = contextlib.ExitStack()
    with es:
        for s in ("ld", "sp", "cc", "t", "v", "a", "g", "st"):
            P.add_sem(s, es.enter_context(nc.semaphore(s)))
        for q in ("ld", "sp", "st"):
            for i in range(8):
                nm = "%s%d" % (q, i)
                P.add_sem(nm, es.enter_context(nc.semaphore(nm)))

        sb = {}

        def alloc(name, shape, dtype=F32, stack=es):
            t = stack.enter_context(nc.sbuf_tensor("sb_" + name, shape, dtype))
            sb[name] = t
            return t

        ident = alloc("ident", [128, 128])
        identb = alloc("identb", [128, 128], BF16)
        ones128 = alloc("ones128", [128, 128])
        sbias_a = alloc("sbias_a", [128, 1])
        sbias_w = alloc("sbias_w", [128, 1])
        zbias = alloc("zbias", [128, 1])
        mask8 = alloc("mask8", [128, 8])
        m1loc = alloc("m1loc", [128, 1])
        m2loc = alloc("m2loc", [128, 1])
        totals = alloc("totals", [128, 8])
        offs = alloc("offs", [128, 8])
        s1pp = alloc("s1pp", [128, 1])
        r1pp = alloc("r1pp", [128, 1])
        s2pp = alloc("s2pp", [128, 1])
        r2pp = alloc("r2pp", [128, 1])
        sw1t = alloc("sw1t", [128, KF])
        sw2t = alloc("sw2t", [128, KD])
        cs1t = alloc("cs1t", [128, KF])
        cs2t = alloc("cs2t", [128, KD])
        scale1 = alloc("scale1", [128, KF])
        bias1 = alloc("bias1", [128, KF])
        scale2 = alloc("scale2", [128, KD])
        bias2 = alloc("bias2", [128, KD])
        b1t = alloc("b1t", [128, KF])
        b2t = alloc("b2t", [128, KD])
        eps_ln = alloc("eps_ln", [128, 1])
        zcol = alloc("zcol", [128, 1])
        red1 = alloc("red1", [128, 1])
        redA = alloc("redA", [128, 1])
        agb = alloc("agb", [128, 80])
        qb1 = alloc("qb1", [128, KF])
        qb2 = alloc("qb2", [128, KF])
        qb3 = alloc("qb3", [128, KF])
        qb4 = alloc("qb4", [128, KF])
        qb5 = alloc("qb5", [128, KF])
        qb6 = alloc("qb6", [128, KF])
        diag_dbg_sb = alloc("diag_sb", [128, 128])
        dumv = alloc("dumv", [128, 1])
        dumg = alloc("dumg", [128, 1])
        agst = alloc("agst", [128, 10])
        duma = alloc("duma", [128, 1])

        t_id = P.dma("sync", ident[:], ident_in, "ld")
        t_mk = P.dma("sync", mask8[:], mask_in, "ld")
        t_b1 = P.dma("sync", b1t[:], b1_in, "ld")
        t_b2 = P.dma("sync", b2t[:], b2_in, "ld")
        P.wait("vector", t_id)
        P.op("vector", lambda e: e.tensor_copy(identb[:], ident[:]))
        P.op("vector", lambda e: e.memset(ones128[:], 1.0))
        P.op("vector", lambda e: e.memset(eps_ln[:], EPS_LN))
        P.op("vector", lambda e: e.memset(m1loc[:], -1e30))
        P.op("vector", lambda e: e.memset(m2loc[:], 0.0))
        P.op("vector", lambda e: e.memset(totals[:], 0.0))
        P.op("vector", lambda e: e.memset(offs[:], 0.0))
        P.op("vector", lambda e: e.memset(dumv[:], 0.0))
        P.op("vector", lambda e: e.memset(duma[:], 0.0))
        P.op("vector", lambda e: e.memset(dumg[:], 0.0))
        P.scratch = {"vector": dumv[:], "scalar": duma[:], "gpsimd": dumg[:]}
        P.op("vector", lambda e: e.memset(sbias_a[:], -127.5))
        P.op("vector", lambda e: e.memset(sbias_w[:], 0.5))
        P.op("vector", lambda e: e.memset(zbias[:], 0.0))
        t_init = P.op("vector", lambda e: e.tensor_copy(zcol[:], eps_ln[:]), sem="v")

        def emit_allmax_bcast(src8_ap, dst_pp, dep, br_ps):
            P.wait("vector", dep)
            tr = P.sop("vector", lambda e: e.tensor_reduce(
                red1[:], src8_ap, AX, ALU.max), "v")
            P.wait("tensor", tr)
            tt = P.op("tensor", lambda e: e.transpose(
                br_ps[0:1, 0:128], red1[:], ident[:]), sem="t")
            P.wait("vector", tt)
            tr2 = P.sop("vector", lambda e: e.tensor_reduce(
                redA[0:1, 0:1], br_ps[0:1, 0:128], AX, ALU.max), "v")
            P.sop("vector", lambda e: e.memset(zcol[:], 0.0), "v")
            tz = P.sop("vector", lambda e: e.tensor_copy(
                zcol[0:1, 0:1], redA[0:1, 0:1]), "v")
            tz = P.settle("vector", "v", dumv[:])
            P.wait("tensor", tz)
            tb = P.op("tensor", lambda e: e.matmul(
                br_ps[:, 0:1], ones128[:], zcol[:], start=True, stop=True), sem="t")
            P.wait("vector", tb)
            P.sop("vector", lambda e: e.tensor_copy(dst_pp, br_ps[:, 0:1]), "v")
            return P.settle("vector", "v", dumv[:])

        st_atab = contextlib.ExitStack()
        atabs = [alloc("tab" + str(i), [128, KD * C2], stack=st_atab)
                 for i in range(6)]

        with contextlib.ExitStack() as st1:
            wsh1 = alloc("wsh1", [128, 4 * D], stack=st1)
            wsh2 = alloc("wsh2", [128, FF], stack=st1)
            swsh1 = alloc("swsh1", [128, 4], stack=st1)
            rwsh1 = alloc("rwsh1", [128, 4], stack=st1)
            swsh2 = alloc("swsh2", [128, 1], stack=st1)
            rwsh2 = alloc("rwsh2", [128, 1], stack=st1)
            st1a = contextlib.ExitStack()
            cc_t = alloc("cc_t", [128, KD * C2], stack=st1a)
            bb_t = alloc("bb_t", [128, KD * C2], stack=st1a)
            wmy = alloc("wmy", [128, KD * 128], stack=st1a)
            agin = alloc("agin", [128, 6 * C2 + 5], stack=st1a)
            comb_ps = st1a.enter_context(nc.psum_tensor("comb_ps", [128, C2], F32))

            t_mcc = P.dma("sync", cc_t[:], mcc_in, "ld")
            t_mbb = P.dma("sync", bb_t[:], mbb_in, "ld")
            P.wait("scalar", t_mcc)
            t_cc = P.op("scalar", lambda e: e.activation(
                cc_t[:], cc_t[:], AF.Sin, bias=0.0, scale=SIN_SCL), sem="a")
            P.wait("scalar", t_mbb)
            t_bb = P.op("scalar", lambda e: e.activation(
                bb_t[:], bb_t[:], AF.Sin, bias=0.0, scale=SIN_SCL), sem="a")

            t_w1 = P.dma("sync", wsh1[:].rearrange("p (c d) -> p c d", c=4),
                         w1_my.rearrange("(c p) d -> p c d", p=128), "ld")
            t_w2 = P.dma("sync", wsh2[:], w2_my, "ld")
            P.wait("vector", t_w1)
            for ch in range(4):
                P.sop("vector", lambda e, ch=ch: e.tensor_reduce(
                    swsh1[:, ch:ch + 1], wsh1[:, ch * D:(ch + 1) * D], AX, ALU.max,
                    apply_absolute_value=True), "v")
            P.sop("vector", lambda e: e.tensor_scalar(
                swsh1[:], swsh1[:], EPS_Q, 1.0 / 127.0, ALU.max, ALU.mult), "v")
            t_sw1l = P.sop("vector", lambda e: e.reciprocal(rwsh1[:], swsh1[:]),
                           "v")
            P.wait("vector", t_w2)
            P.sop("vector", lambda e: e.tensor_reduce(
                swsh2[:], wsh2[:], AX, ALU.max, apply_absolute_value=True), "v")
            P.sop("vector", lambda e: e.tensor_scalar(
                swsh2[:], swsh2[:], EPS_Q, 1.0 / 127.0, ALU.max, ALU.mult), "v")
            t_sw2l = P.sop("vector", lambda e: e.reciprocal(rwsh2[:], swsh2[:]),
                           "v")
            P.sop("vector", lambda e: e.tensor_copy(agin[:, 6 * C2:6 * C2 + 4],
                                                    swsh1[:]), "v")
            P.sop("vector", lambda e: e.tensor_copy(
                agin[:, 6 * C2 + 4:6 * C2 + 5], swsh2[:]), "v")
            t_scl = P.settle("vector", "v", dumv[:])

            tok_last_ev = None
            tok_last_mm = None
            for wi, wap in enumerate((wq_my, wk_my, wv_my)):
                P.wait("sync", tok_last_mm)
                t_wm = P.dma("sync", wmy[:].rearrange("p (k d) -> p k d", k=KD),
                             wap.rearrange("(k p) d -> p k d", p=128), "ld")
                for ti, (tbl, ttok) in enumerate(((cc_t, t_cc), (bb_t, t_bb))):
                    P.wait("tensor", [t_wm, ttok])
                    if tok_last_ev is not None:
                        P.wait("tensor", tok_last_ev)
                    tmm = None
                    for k in range(KD):
                        sem = "t" if k == KD - 1 else None
                        tmm = P.op("tensor", lambda e, k=k, tbl=tbl: e.matmul(
                            comb_ps[:], wmy[:, k * 128:(k + 1) * 128],
                            tbl[:, k * C2:(k + 1) * C2],
                            start=(k == 0), stop=(k == KD - 1)), sem=sem) or tmm
                    slot = 2 * wi + ti
                    tok_last_mm = tmm
                    P.wait("scalar", tmm)
                    tok_last_ev = P.op("scalar", lambda e, s=slot: e.copy(
                        agin[:, s * C2:(s + 1) * C2], comb_ps[:]), sem="a")

            tok_last_ev = P.settle("scalar", "a", duma[:])
            P.wait("sync", [tok_last_ev, t_scl])
            if debug:
                P.dma("sync", cc_dbg, cc_t[:], "ld")
                P.dma("sync", agi_dbg, agin[:], "ld")
            t_agin = P.dma("sync", ag1_in.ap(), agin[:], "ld")
            P.wait("gpsimd", t_agin)
            t_ag1 = P.op("gpsimd", lambda e: e.collective_compute(
                "AllGather", ALU.bypass, replica_groups=[list(range(NCORES))],
                ins=[ag1_in.ap()], outs=[ag1_out.ap()]), sem="cc")
            P.wait("sync", t_ag1)
            t_tabs = None
            for m in range(6):
                t_tabs = P.dma(
                    "sync", atabs[m][:].rearrange("p (k c) -> p k c", k=KD),
                    ag1_out.ap()[:, m * C2:(m + 1) * C2].rearrange(
                        "(k p) c -> p k c", p=128), "ld")
            if debug:
                P.wait("sync", t_tabs)
                P.dma("sync", tab_dbg, atabs[0][:], "ld")
            t_sw1 = P.dma("sync", sw1t[:].rearrange("p (r c) -> p r c", r=8),
                          ag1_out.ap()[:, 6 * C2:6 * C2 + 4].rearrange(
                              "(r p) c -> p r c", p=128), "ld")
            t_sw2 = P.dma("sync", sw2t[:].rearrange("p (r c) -> p r c", r=8),
                          ag1_out.ap()[:, 6 * C2 + 4:6 * C2 + 5].rearrange(
                              "(r p) c -> p r c", p=128), "ld", slow=True)

            st1a.close()
            qsh = alloc("qsh", [128, 4 * D], F32, stack=st1)
            wbufs = tuple(alloc("wqb%d" % i, [128, D], stack=st1)
                          for i in range(6))
            tq = []
            tprev = None
            for pr in range(2):
                s0, s1 = 2 * pr, 2 * pr + 1
                tkA, tkB, tprev = emit_quant_pair(
                    P, [wsh1[:, s0 * D:(s0 + 1) * D], wsh1[:, s1 * D:(s1 + 1) * D]],
                    [rwsh1[:, s0:s0 + 1], rwsh1[:, s1:s1 + 1]], wbufs,
                    [qsh[:, s0 * D:(s0 + 1) * D], qsh[:, s1 * D:(s1 + 1) * D]],
                    [t_sw1l, tprev, t_agin], sbias_w, weight_mode=True)
                tq += [tkA, tkB]
            tq.append(P.settle("vector", "v", dumv[:]))
            P.wait("sync", tq)
            t_q1d = P.dma("sync", agw1_in.ap().rearrange("(c p) d -> p c d", p=128),
                          qsh[:].rearrange("p (c d) -> p c d", c=4), "ld")
            P.wait("gpsimd", t_q1d)
            t_agw1 = P.op("gpsimd", lambda e: e.collective_compute(
                "AllGather", ALU.bypass, replica_groups=[list(range(NCORES))],
                ins=[agw1_in.ap()], outs=[agw1_out.ap()]), sem="cc")

            tq = []
            for pr in range(2):
                s0, s1 = 2 * pr, 2 * pr + 1
                tkA, tkB, tprev = emit_quant_pair(
                    P, [wsh2[:, s0 * D:(s0 + 1) * D], wsh2[:, s1 * D:(s1 + 1) * D]],
                    [rwsh2[:], rwsh2[:]], wbufs,
                    [qsh[:, s0 * D:(s0 + 1) * D], qsh[:, s1 * D:(s1 + 1) * D]],
                    [t_sw2l, tprev, t_q1d], sbias_w, weight_mode=True)
                tq += [tkA, tkB]
            tq.append(P.settle("vector", "v", dumv[:]))
            P.wait("sync", tq)
            t_q2d = P.dma("sync", agw2_in.ap(), qsh[:], "ld")
            P.wait("gpsimd", t_q2d)
            t_agw2 = P.op("gpsimd", lambda e: e.collective_compute(
                "AllGather", ALU.bypass, replica_groups=[list(range(NCORES))],
                ins=[agw2_in.ap()], outs=[agw2_out.ap()]), sem="cc")

            tbin = alloc("tbin", [128, FF], F32, stack=st1)
            tbout = alloc("tbout", [128, KD * 128], BF16, stack=st1)
            tb_ps = st1.enter_context(nc.psum_tensor("tb_ps", [128, 1024], F32))

            P.wait("sync", t_agw1)
            t_dma_prev = None
            t_cs_prev = None
            for j in range(KF):
                if t_dma_prev is not None:
                    P.wait("sync", t_dma_prev)
                P.wait("sync", t_cs_prev)
                t_in = P.dma("sync", tbin[:, 0:D],
                             agw1_out.ap()[j * 128:(j + 1) * 128, 0:D], "ld")
                P.wait("vector", t_in)
                t_cs_prev = P.sop("vector", lambda e, j=j: e.tensor_reduce(
                    cs1t[:, j:j + 1], tbin[:, 0:D], AX, ALU.add), "v")
                P.wait("tensor", t_in)
                ttr = None
                for k in range(KD):
                    sem = "t" if k == KD - 1 else None
                    ttr = P.op("tensor", lambda e, k=k: e.transpose(
                        tb_ps[:, k * 128:(k + 1) * 128],
                        tbin[:, k * 128:(k + 1) * 128], ident[:]), sem=sem) or ttr
                P.wait("scalar", ttr)
                P.wait("scalar", t_dma_prev)
                tev = P.op("scalar", lambda e: e.copy(tbout[:], tb_ps[:, 0:1024]),
                           sem="a")
                P.wait("tensor", tev)
                tset = P.settle("scalar", "a", duma[:])
                P.wait("sync", [tset, ttr])
                t_dma_prev = P.dma(
                    "sync", qw1t_d[:, j * 1024:(j + 1) * 1024], tbout[:], "ld")

            P.wait("sync", t_agw2)
            for j in range(KD):
                P.wait("sync", t_dma_prev)
                P.wait("sync", t_cs_prev)
                t_in = P.dma("sync", tbin[:, 0:FF],
                             agw2_out.ap()[j * 128:(j + 1) * 128, 0:FF], "ld")
                P.wait("vector", t_in)
                t_cs_prev = P.sop("vector", lambda e, j=j: e.tensor_reduce(
                    cs2t[:, j:j + 1], tbin[:, 0:FF], AX, ALU.add), "v")
                P.wait("tensor", t_in)
                for kb in range(4):
                    ttr = None
                    for k in range(8):
                        kk = kb * 8 + k
                        sem = "t" if k == 7 else None
                        ttr = P.op("tensor", lambda e, kk=kk, k=k: e.transpose(
                            tb_ps[:, k * 128:(k + 1) * 128],
                            tbin[:, kk * 128:(kk + 1) * 128], ident[:]),
                            sem=sem) or ttr
                    P.wait("scalar", ttr)
                    P.wait("scalar", t_dma_prev)
                    tev = P.op("scalar", lambda e: e.copy(
                        tbout[:, 0:1024], tb_ps[:, 0:1024]), sem="a")
                    P.wait("tensor", tev)
                    tset = P.settle("scalar", "a", duma[:])
                    P.wait("sync", tset)
                    t_dma_prev = P.dma(
                        "sync",
                        qw2t_d[:, (j * 4 + kb) * 1024:(j * 4 + kb + 1) * 1024],
                        tbout[:, 0:1024], "ld")
            t_wt_done = t_dma_prev

        P.quiesce()
        with contextlib.ExitStack() as b1:
            xbuf = [alloc("xb0", [128, 4 * D], stack=b1),
                    alloc("xb1", [128, 4 * D], stack=b1)]
            sqt = alloc("sqt", [128, D], stack=b1)
            mt = alloc("mt", [128, 1], stack=b1)
            vt = alloc("vt", [128, 1], stack=b1)
            rstd = alloc("rstd", [128, 1], stack=b1)
            xnt = alloc("xnt", [128, KD * T], stack=b1)
            fqb = alloc("fqb", [128, 8 * T], stack=b1)
            fvt = alloc("fvt", [128, 2 * T], stack=b1)
            pb = alloc("pb", [128, 8 * T], stack=b1)
            vt1 = alloc("vt1", [128, T], stack=b1)
            vt2 = alloc("vt2", [128, T], stack=b1)
            ttp = alloc("ttp", [128, 1], stack=b1)
            tr_ps = [b1.enter_context(nc.psum_tensor("trp0", [128, 512], F32)),
                     b1.enter_context(nc.psum_tensor("trp1", [128, 512], F32))]
            pl_ps = [b1.enter_context(nc.psum_tensor("plp%d" % i, [128, 512], F32))
                     for i in range(6)]

            tok_x_free = [None, None]
            tok_xnt_sp = None
            tok_fq_sp = None
            tok_p_sp = None
            tok_proj_done = None
            tok_ev_done = None

            for i in range(NT):
                sl = i % 2
                P.wait("sync", tok_x_free[sl])
                t_x = P.dma("sync",
                            xbuf[sl][:].rearrange("p (c d) -> p c d", c=4),
                            x_in[i * T:(i + 1) * T, :].rearrange(
                                "(c p) d -> p c d", p=128), "ld")
                ln_toks = []
                P.wait("vector", t_x)
                for tc in range(4):
                    ch = xbuf[sl][:, tc * D:(tc + 1) * D]
                    P.sop("vector", lambda e, ch=ch: e.tensor_reduce(
                        mt[:], ch, AX, ALU.add), "v")
                    P.sop("vector", lambda e: e.tensor_single_scalar(
                        mt[:], mt[:], 1.0 / D, ALU.mult), "v")
                    tv = P.op("vector", lambda e, ch=ch: e.tensor_scalar(
                        ch, ch, mt[:, 0:1], None, ALU.subtract), sem="v")
                    P.wait("scalar", tv)
                    ta = P.op("scalar", lambda e, ch=ch: e.activation(
                        sqt[:], ch, AF.Square, bias=0.0, scale=1.0), sem="a")
                    P.wait("vector", ta)
                    P.sop("vector", lambda e: e.tensor_reduce(
                        vt[:], sqt[:], AX, ALU.add), "v")
                    tv2 = P.op("vector", lambda e: e.tensor_copy(vt[:], vt[:]),
                               sem="v")
                    P.wait("scalar", tv2)
                    P.op("scalar", lambda e: e.activation(
                        rstd[:], vt[:], AF.Sqrt, bias=eps_ln[:, 0:1],
                        scale=1.0 / D))
                    ta2 = P.op("scalar", lambda e: e.copy(duma[:], duma[:]),
                               sem="a")
                    P.wait("vector", ta2)
                    P.sop("vector", lambda e: e.reciprocal(rstd[:], rstd[:]), "v")
                    tln = P.op("vector", lambda e, ch=ch: e.tensor_scalar(
                        ch, ch, rstd[:, 0:1], None, ALU.mult), sem="v")
                    ln_toks.append(tln)
                if debug and i == 0:
                    P.wait("gpsimd", ln_toks)
                    P.dma("gpsimd", xn_dbg, xbuf[sl][:], "sp")
                    P.dma("gpsimd", id_dbg, ident[:], "sp")
                P.sop("vector", lambda e, sl=sl: e.tensor_reduce(
                    ttp[:], xbuf[sl][:], AX, ALU.max), "v")
                t_m1 = P.sop("vector", lambda e: e.tensor_max(
                    m1loc[:], m1loc[:], ttp[:]), "v")

                ev_toks = []
                for tc in range(4):
                    P.wait("tensor", ln_toks[tc])
                    ttr = None
                    for half in range(2):
                        ps = tr_ps[half]
                        for k in range(4):
                            kk = half * 4 + k
                            sem = "t" if k == 3 else None
                            ttr = P.op("tensor", lambda e, ps=ps, k=k, kk=kk, sl=sl,
                                       tc=tc: e.transpose(
                                           ps[:, k * 128:(k + 1) * 128],
                                           xbuf[sl][:, tc * D + kk * 128:
                                                    tc * D + (kk + 1) * 128],
                                           ident[:]), sem=sem) or ttr
                        P.wait("scalar", ttr)
                        if tok_proj_done is not None:
                            P.wait("scalar", tok_proj_done)
                        P.wait("scalar", tok_xnt_sp)
                        dst = xnt[:].rearrange("p (k t) -> p k t", k=KD)[
                            :, half * 4:(half + 1) * 4, tc * 128:(tc + 1) * 128]
                        tev = P.op("scalar", lambda e, dst=dst, ps=ps: e.copy(
                            dst, ps[:].rearrange("p (k t) -> p k t", k=4)), sem="a")
                        ev_toks.append(tev)
                        P.wait("tensor", tev)
                tok_x_free[sl] = ttr
                tok_ev_done = ev_toks[-1]

                t_set_a = P.settle("scalar", "a", duma[:])
                P.wait("gpsimd", [t_set_a] + ev_toks)
                tok_xnt_sp = P.dma("gpsimd", xnt_d[:, i * KD * T:(i + 1) * KD * T],
                                   xnt[:], "sp")

                p_toks = []
                for jc in range(4):
                    mm_tok = [None] * 6
                    for pl in range(6):
                        P.wait("tensor", ev_toks)
                        if i > 0 or jc > 0:
                            pass
                        P.wait("tensor", t_tabs)
                        for k in range(KD):
                            sem = "t" if k == KD - 1 else None
                            tk = P.op("tensor", lambda e, pl=pl, k=k, jc=jc: e.matmul(
                                pl_ps[pl][:],
                                atabs[pl][:, k * C2 + jc * 128:k * C2 + (jc + 1) * 128],
                                xnt[:, k * T:(k + 1) * T],
                                start=(k == 0), stop=(k == KD - 1)), sem=sem)
                            if sem:
                                mm_tok[pl] = tk
                    P.wait("scalar", [mm_tok[0], mm_tok[1], mm_tok[4], mm_tok[5]])
                    P.wait("scalar", tok_fq_sp)
                    e1 = P.op("scalar", lambda e, jc=jc: e.copy(
                        fqb[:, jc * T:(jc + 1) * T], pl_ps[0][:]), sem="a")
                    e2 = P.op("scalar", lambda e, jc=jc: e.copy(
                        fqb[:, (4 + jc) * T:(5 + jc) * T], pl_ps[1][:]), sem="a")
                    e3 = P.op("scalar", lambda e: e.copy(fvt[:, 0:T], pl_ps[4][:]),
                              sem="a")
                    e4 = P.op("scalar", lambda e: e.copy(fvt[:, T:2 * T],
                                                         pl_ps[5][:]), sem="a")
                    P.wait("vector", [mm_tok[2], mm_tok[3], e3, e4])
                    P.wait("vector", tok_p_sp)
                    P.op("vector", lambda e: e.tensor_mul(
                        vt1[:], pl_ps[2][:], fvt[:, 0:T]))
                    P.sop("vector", lambda e: e.tensor_mul(
                        vt2[:], pl_ps[3][:], fvt[:, T:2 * T]), "v")
                    P.op("vector", lambda e, jc=jc: e.tensor_sub(
                        pb[:, jc * T:(jc + 1) * T], vt1[:], vt2[:]))
                    P.op("vector", lambda e: e.tensor_mul(
                        vt1[:], pl_ps[2][:], fvt[:, T:2 * T]))
                    tpv = P.sop("vector", lambda e: e.tensor_mul(
                        vt2[:], pl_ps[3][:], fvt[:, 0:T]), "v")
                    P.op("vector", lambda e, jc=jc: e.tensor_add(
                        pb[:, (4 + jc) * T:(5 + jc) * T], vt1[:], vt2[:]))
                    if jc == 0:
                        P.sop("vector", lambda e: e.tensor_mul(
                            vt1[0:1, :], pl_ps[3][0:1, :], fvt[0:1, T:2 * T]), "v")
                        P.op("vector", lambda e: e.tensor_add(
                            pb[0:1, 0:T], pb[0:1, 0:T], vt1[0:1, :]))
                        P.op("vector", lambda e: e.tensor_copy(
                            pb[0:1, 4 * T:5 * T], vt1[0:1, :]))
                    P.sop("vector", lambda e, jc=jc: e.tensor_reduce(
                        ttp[:], pb[:, jc * T:(jc + 1) * T], AX, ALU.add), "v")
                    P.sop("vector", lambda e, jc=jc: e.tensor_add(
                        totals[:, jc:jc + 1], totals[:, jc:jc + 1], ttp[:]), "v")
                    P.sop("vector", lambda e, jc=jc: e.tensor_reduce(
                        ttp[:], pb[:, (4 + jc) * T:(5 + jc) * T], AX, ALU.add), "v")
                    tlast = P.sop("vector", lambda e, jc=jc: e.tensor_add(
                        totals[:, 4 + jc:5 + jc], totals[:, 4 + jc:5 + jc],
                        ttp[:]), "v")
                    p_toks.append(tlast)
                    P.wait("tensor", [tpv, tlast, e1, e2])
                tok_proj_done = p_toks[-1]
                fq_ev = [e1, e2, P.settle("scalar", "a", duma[:])]
                p_toks = p_toks + [P.settle("vector", "v", dumv[:])]
                tok_proj_done = p_toks[-1]
                P.wait("gpsimd", p_toks)
                P.wait("gpsimd", fq_ev)
                tok_fq_sp = P.dma("gpsimd", fq_d[:, i * 8 * T:(i + 1) * 8 * T],
                                  fqb[:], "sp")
                tok_p_sp = P.dma("gpsimd", p_d[:, i * 8 * T:(i + 1) * 8 * T],
                                 pb[:], "sp")

        st_atab.close()
        P.quiesce()
        st_if = contextlib.ExitStack()
        ifa = alloc("ifa", [128, 4 * D], stack=st_if)
        ifb = alloc("ifb", [128, 4 * D], stack=st_if)
        with contextlib.ExitStack() as bar1:
            br1_ps = bar1.enter_context(nc.psum_tensor("br1_ps", [128, 256], F32))
            coef_t = alloc("coef_t", [128, 8], stack=bar1)
            t_mia = P.dma("sync", ifa[:], mia_in, "ld")
            t_mib = P.dma("sync", ifb[:], mib_in, "ld")
            t_ca = P.dma("sync", coef_t[:, 0:4], coefa_in, "ld")
            t_cb = P.dma("sync", coef_t[:, 4:8], coefb_in, "ld")
            P.wait("scalar", t_mia)
            t_ifa = P.op("scalar", lambda e: e.activation(
                ifa[:], ifa[:], AF.Sin, bias=0.0, scale=SIN_SCL), sem="a")
            P.wait("scalar", t_mib)
            t_ifb = P.op("scalar", lambda e: e.activation(
                ifb[:], ifb[:], AF.Sin, bias=0.0, scale=SIN_SCL), sem="a")
            P.wait("vector", [t_ifa, t_ca])
            for cc in range(4):
                P.op("vector", lambda e, cc=cc: e.tensor_scalar(
                    ifa[:, cc * D:(cc + 1) * D], ifa[:, cc * D:(cc + 1) * D],
                    coef_t[:, cc:cc + 1], None, ALU.mult))
            P.wait("vector", [t_ifb, t_cb])
            t_iftab = None
            for cc in range(4):
                t_iftab = P.op("vector", lambda e, cc=cc: e.tensor_scalar(
                    ifb[:, cc * D:(cc + 1) * D], ifb[:, cc * D:(cc + 1) * D],
                    coef_t[:, 4 + cc:5 + cc], None, ALU.mult), sem="v")

            P.wait("vector", [tok_proj_done, t_m1])
            P.sop("vector", lambda e: e.tensor_copy(agst[:, 0:8], totals[:]), "v")
            P.sop("vector", lambda e: e.tensor_copy(agst[:, 8:9], m1loc[:]), "v")
            P.sop("vector", lambda e: e.memset(agst[:, 9:10], 0.0), "v")
            t_stl = P.settle("vector", "v", dumv[:])
            P.wait("sync", t_stl)
            t_a2a = P.dma("sync", ag2_in.ap(), agst[:], "ld")
            P.wait("gpsimd", t_a2a)
            t_ag2 = P.op("gpsimd", lambda e: e.collective_compute(
                "AllGather", ALU.bypass, replica_groups=[list(range(NCORES))],
                ins=[ag2_in.ap()], outs=[ag2_out.ap()]), sem="cc")
            P.wait("sync", t_ag2)
            t_agb = P.dma("sync", agb[:].rearrange("p (r c) -> p r c", r=8),
                          ag2_out.ap().rearrange("(r p) c -> p r c", p=128), "ld")

            P.wait("vector", t_agb)
            toff = None
            for r in range(8):
                toff = P.sop("vector", lambda e, r=r: e.scalar_tensor_tensor(
                    offs[:], agb[:, 10 * r:10 * r + 8], mask8[:, r:r + 1], offs[:],
                    ALU.mult, ALU.add), "v")
            m1cands = agb[:].rearrange("p (r c) -> p c r", r=8)[:, 8:9, :]
            tmax = emit_allmax_bcast(m1cands, s1pp[:], t_agb, br1_ps)
            P.wait("vector", tmax)
            P.sop("vector", lambda e: e.tensor_scalar(
                s1pp[:], s1pp[:], EPS_Q, 1.0 / 255.0, ALU.max, ALU.mult), "v")
            t_s1 = P.sop("vector", lambda e: e.reciprocal(r1pp[:], s1pp[:]), "v")

            P.wait("vector", [t_sw1, t_b1, t_s1])
            P.sop("vector", lambda e: e.tensor_scalar(
                scale1[:], sw1t[:], s1pp[:, 0:1], None, ALU.mult), "v")
            P.sop("vector", lambda e: e.reciprocal(qb1[:], scale1[:]), "v")
            tyb = P.sop("vector", lambda e: e.tensor_mul(qb2[:], b1t[:], qb1[:]),
                        "v")
            tq1 = emit_round_wide(P, qb2[:], qb3[:], qb4[:], qb1[:], qb6[:],
                                  qb5[:], zbias, [tyb])
            P.wait("vector", tq1)
            P.sop("vector", lambda e: e.tensor_mul(qb3[:], qb3[:], scale1[:]), "v")
            P.sop("vector", lambda e: e.tensor_scalar(
                qb1[:], cs1t[:], 127.5, None, ALU.mult), "v")
            P.sop("vector", lambda e: e.tensor_mul(qb1[:], qb1[:], scale1[:]), "v")
            t_bias1 = P.sop("vector", lambda e: e.tensor_add(
                bias1[:], qb3[:], qb1[:]), "v")
            t_bias1 = P.settle("vector", "v", dumv[:])

        with contextlib.ExitStack() as b2:
            pb2 = [alloc("pb2_0", [128, 8 * T], stack=b2),
                   alloc("pb2_1", [128, 8 * T], stack=b2)]
            fb2 = [alloc("fb2_0", [128, 8 * T], stack=b2),
                   alloc("fb2_1", [128, 8 * T], stack=b2)]
            memb = [alloc("memb0", [128, 8 * T], stack=b2),
                    alloc("memb1", [128, 8 * T], stack=b2)]
            zb = alloc("zb", [128, 8 * T], stack=b2)
            sob = alloc("sob", [128, KD * T], stack=b2)
            zt0 = alloc("zt0", [128, T], stack=b2)
            vt1b = alloc("vt1b", [128, T], stack=b2)
            vt2b = alloc("vt2b", [128, T], stack=b2)
            if_ps = [b2.enter_context(nc.psum_tensor("ifp0", [128, 512], F32)),
                     b2.enter_context(nc.psum_tensor("ifp1", [128, 512], F32))]

            P.op("vector", lambda e: e.memset(zt0[:], 0.0))
            tok_pf_free = [None, None]
            tok_s_sp = None
            tok_z_done = None
            tok_zb_free = None
            for i in range(NT):
                sl = i % 2
                P.wait("sync", tok_pf_free[sl])
                t_p = P.dma("sync", pb2[sl][:], p_d[:, i * 8 * T:(i + 1) * 8 * T],
                            "ld")
                t_f = P.dma("sync", fb2[sl][:], fq_d[:, i * 8 * T:(i + 1) * 8 * T],
                            "ld")
                P.wait("vector", t_p)
                if i == 0:
                    P.wait("vector", toff)
                scan_toks = []
                for q in range(8):
                    if i == 0:
                        init = offs[:, q:q + 1]
                    else:
                        init = memb[1 - sl][:, q * T + T - 1:q * T + T]
                    tk = P.op("vector", lambda e, q=q, sl=sl, init=init: (
                        e.tensor_tensor_scan(
                            memb[sl][:, q * T:(q + 1) * T],
                            pb2[sl][:, q * T:(q + 1) * T], zt0[:],
                            init, ALU.add, ALU.add)), sem="v")
                    scan_toks.append(tk)
                P.wait("vector", t_f)
                P.wait("vector", tok_zb_free)
                z_toks = []
                for jc in range(4):
                    mre = memb[sl][:, jc * T:(jc + 1) * T]
                    mim = memb[sl][:, (4 + jc) * T:(5 + jc) * T]
                    qre = fb2[sl][:, jc * T:(jc + 1) * T]
                    qim = fb2[sl][:, (4 + jc) * T:(5 + jc) * T]
                    zre = zb[:, jc * T:(jc + 1) * T]
                    zim = zb[:, (4 + jc) * T:(5 + jc) * T]
                    P.op("vector", lambda e, a=mre, b=qre: e.tensor_mul(
                        vt1b[:], a, b))
                    P.sop("vector", lambda e, a=mim, b=qim: e.tensor_mul(
                        vt2b[:], a, b), "v")
                    P.op("vector", lambda e, o=zre: e.tensor_add(
                        o, vt1b[:], vt2b[:]))
                    P.op("vector", lambda e, a=mim, b=qre: e.tensor_mul(
                        vt1b[:], a, b))
                    P.sop("vector", lambda e, a=mre, b=qim: e.tensor_mul(
                        vt2b[:], a, b), "v")
                    tk = P.op("vector", lambda e, o=zim: e.tensor_sub(
                        o, vt1b[:], vt2b[:]), sem="v")
                    if jc == 0:
                        P.sop("vector", lambda e, a=mim, b=qim: e.tensor_mul(
                            vt1b[0:1, :], a[0:1, :], b[0:1, :]), "v")
                        P.op("vector", lambda e, o=zre: e.tensor_sub(
                            o[0:1, :], o[0:1, :], vt1b[0:1, :]))
                        tk = P.op("vector", lambda e, o=zim: e.tensor_copy(
                            o[0:1, :], vt1b[0:1, :]), sem="v")
                    z_toks.append(tk)
                tok_pf_free[sl] = z_toks[-1]
                tok_z_done = z_toks[-1]
                s_ev = []
                for k in range(KD):
                    ps = if_ps[k % 2]
                    P.wait("tensor", z_toks)
                    P.wait("tensor", t_iftab)
                    tmm = None
                    for q in range(8):
                        tbl = ifa if q < 4 else ifb
                        cc = q % 4
                        sem = "t" if q == 7 else None
                        tmm = P.op("tensor", lambda e, ps=ps, tbl=tbl, cc=cc, k=k,
                                   q=q: e.matmul(
                                       ps[:], tbl[:, cc * D + k * 128:
                                                  cc * D + (k + 1) * 128],
                                       zb[:, q * T:(q + 1) * T],
                                       start=(q == 0), stop=(q == 7)),
                                   sem=sem) or tmm
                    P.wait("scalar", tmm)
                    P.wait("scalar", tok_s_sp)
                    tev = P.op("scalar", lambda e, k=k, ps=ps: e.copy(
                        sob[:, k * T:(k + 1) * T], ps[:]), sem="a")
                    s_ev.append(tev)
                    P.wait("tensor", tev)
                tok_zb_free = tmm
                s_ev.append(P.settle("scalar", "a", duma[:]))
                P.wait("gpsimd", s_ev)
                tok_s_sp = P.dma("gpsimd", s_d[:, i * KD * T:(i + 1) * KD * T],
                                 sob[:], "sp")
            tok_b2_sp = tok_s_sp
        st_if.close()
        P.quiesce()

        with contextlib.ExitStack() as pc:
            qw1sb = alloc("qw1sb", [128, KD * FF], BF16, stack=pc)
            xc = [alloc("xc0", [128, KD * T], stack=pc),
                  alloc("xc1", [128, KD * T], stack=pc)]
            xq = alloc("xq", [128, KD * T], BF16, stack=pc)
            hb = alloc("hb", [128, KF * T], stack=pc)
            qbufs = tuple(alloc("qcb%d" % i, [128, 1024], stack=pc)
                          for i in range(6))
            ttp2 = alloc("ttp2", [128, 1], stack=pc)
            h_ps = [pc.enter_context(nc.psum_tensor("hp%d" % v_, [128, 512], F32))
                    for v_ in range(2)]

            t_qw1 = P.dma("sync", qw1sb[:], qw1t_d, "ld")
            tok_xc_free = [None, None]
            tok_h_sp = [None, None]
            tok_m2 = None
            tok_m2_prev = None
            tok_xq_free = None
            tq_prev = None
            for i in range(NT):
                sl = i % 2
                P.wait("sync", tok_xc_free[sl])
                t_xc = P.dma("sync", xc[sl][:],
                             xnt_d[:, i * KD * T:(i + 1) * KD * T], "ld")
                tq_toks = []
                for pr in range(2):
                    a0, a1 = 2 * pr * 1024, (2 * pr + 1) * 1024
                    tkA, tkB, tq_prev = emit_quant_pair(
                        P, [xc[sl][:, a0:a0 + 1024], xc[sl][:, a1:a1 + 1024]],
                        [r1pp[:, 0:1], r1pp[:, 0:1]], qbufs,
                        [xq[:, a0:a0 + 1024], xq[:, a1:a1 + 1024]],
                        [t_xc, t_s1, tq_prev, tok_xq_free], sbias_a)
                    tq_toks += [tkA, tkB]
                tok_xc_free[sl] = tq_toks[-1]
                if debug and i == 0:
                    P.wait("sync", tq_toks)
                    P.dma("sync", xq_dbg, xq[:], "ld")
                ev_toks = []
                for fj in range(KF):
                    ps = h_ps[fj % 2]
                    P.wait("tensor", tq_toks)
                    P.wait("tensor", t_qw1)
                    tmm = None
                    for k in range(KD):
                        sem = "t" if k == KD - 1 else None
                        tmm = P.op("tensor", lambda e, ps=ps, fj=fj, k=k: e.matmul(
                            ps[:], qw1sb[:, fj * 1024 + k * 128:fj * 1024 + (k + 1) * 128],
                            xq[:, k * T:(k + 1) * T],
                            start=(k == 0), stop=(k == KD - 1)), sem=sem) or tmm
                    P.wait("scalar", tmm)
                    P.wait("scalar", [t_bias1, tok_h_sp[0 if fj < 16 else 1],
                                      tok_m2_prev])
                    tev = P.op("scalar", lambda e, fj=fj, ps=ps: e.activation(
                        hb[:, fj * T:(fj + 1) * T], ps[:], AF.Relu,
                        bias=bias1[:, fj:fj + 1], scale=scale1[:, fj:fj + 1]),
                        sem="a")
                    ev_toks.append(tev)
                    P.wait("tensor", tev)
                    tok_xq_free = tmm
                P.wait("vector", ev_toks)
                P.sop("vector", lambda e: e.tensor_reduce(
                    ttp2[:], hb[:], AX, ALU.max), "v")
                tok_m2 = P.sop("vector", lambda e: e.tensor_max(
                    m2loc[:], m2loc[:], ttp2[:]), "v")
                tok_m2_prev = tok_m2
                t_set_c = P.settle("scalar", "a", duma[:])
                for hf in range(2):
                    P.wait("gpsimd", [ev_toks[hf * 16 + 15], t_set_c])
                    tok_h_sp[hf] = P.dma(
                        "gpsimd",
                        h_d[:, i * KF * T + hf * 16 * T:
                            i * KF * T + (hf + 1) * 16 * T],
                        hb[:, hf * 16 * T:(hf + 1) * 16 * T], "sp")

        P.quiesce()
        with contextlib.ExitStack() as bar2:
            br2_ps = bar2.enter_context(nc.psum_tensor("br2_ps", [128, 256], F32))
            t_stl2 = P.settle("vector", "v", dumv[:])
            P.wait("sync", [tok_m2, t_stl2])
            t_a3 = P.dma("sync", ag3_in.ap(), m2loc[:], "ld")
            P.wait("gpsimd", t_a3)
            t_ag3 = P.op("gpsimd", lambda e: e.collective_compute(
                "AllGather", ALU.bypass, replica_groups=[list(range(NCORES))],
                ins=[ag3_in.ap()], outs=[ag3_out.ap()]), sem="cc")
            P.wait("sync", t_ag3)
            t_agb3 = P.dma("sync", agb[:, 0:8].rearrange("p (r c) -> p r c", r=8),
                           ag3_out.ap().rearrange("(r p) c -> p r c", p=128), "ld",
                           slow=True)
            tmax2 = emit_allmax_bcast(agb[:, 0:8], s2pp[:], t_agb3, br2_ps)
            P.wait("vector", tmax2)
            P.sop("vector", lambda e: e.tensor_scalar(
                s2pp[:], s2pp[:], EPS_Q, 1.0 / 255.0, ALU.max, ALU.mult), "v")
            t_s2 = P.sop("vector", lambda e: e.reciprocal(r2pp[:], s2pp[:]), "v")
            P.wait("vector", [t_sw2, t_b2, t_s2])
            P.sop("vector", lambda e: e.tensor_scalar(
                scale2[:], sw2t[:], s2pp[:, 0:1], None, ALU.mult), "v")
            P.sop("vector", lambda e: e.reciprocal(qb1[:, 0:KD], scale2[:]), "v")
            tyb2 = P.sop("vector", lambda e: e.tensor_mul(
                qb2[:, 0:KD], b2t[:], qb1[:, 0:KD]), "v")
            tq2r = emit_round_wide(P, qb2[:, 0:KD], qb3[:, 0:KD], qb4[:, 0:KD],
                                   qb1[:, 0:KD], qb6[:, 0:KD], qb5[:, 0:KD],
                                   zbias, [tyb2])
            P.wait("vector", tq2r)
            P.sop("vector", lambda e: e.tensor_mul(
                qb3[:, 0:KD], qb3[:, 0:KD], scale2[:]), "v")
            P.sop("vector", lambda e: e.tensor_scalar(
                qb1[:, 0:KD], cs2t[:], 127.5, None, ALU.mult), "v")
            P.sop("vector", lambda e: e.tensor_mul(
                qb1[:, 0:KD], qb1[:, 0:KD], scale2[:]), "v")
            t_bias2 = P.sop("vector", lambda e: e.tensor_add(
                bias2[:], qb3[:, 0:KD], qb1[:, 0:KD]), "v")
            t_bias2 = P.settle("vector", "v", dumv[:])

        with contextlib.ExitStack() as pd:
            qw2sb = alloc("qw2sb", [128, KF * D], BF16, stack=pd)
            hD = alloc("hd0", [128, KF * TD], stack=pd)
            qh = alloc("qh", [128, KF * TD], BF16, stack=pd)
            gb = alloc("gb", [128, KD * TD], stack=pd)
            sD = [alloc("sd0", [128, KD * TD], stack=pd),
                  alloc("sd1", [128, KD * TD], stack=pd)]
            rb = alloc("rb", [128, KD * TD], stack=pd)
            xD = [alloc("xd0", [128, 2 * D], stack=pd),
                  alloc("xd1", [128, 2 * D], stack=pd)]
            ob = alloc("ob", [128, 2 * D], stack=pd)
            dbufs = tuple(alloc("qdb%d" % i, [128, 1024], stack=pd)
                          for i in range(6))
            y2_ps = [pd.enter_context(nc.psum_tensor("y2p%d" % v_, [128, 256], F32))
                     for v_ in range(2)]
            rt_ps = [pd.enter_context(nc.psum_tensor("rtp%d" % v_, [128, 1024], F32))
                     for v_ in range(2)]
            scrap_ps = pd.enter_context(nc.psum_tensor("scrap_ps", [128, 128], F32))

            t_qw2 = P.dma("sync", qw2sb[:], qw2t_d, "ld")
            if debug:
                P.wait("vector", t_bias2)
                dcols = [(s1pp, 0), (r1pp, 1), (s2pp, 2), (r2pp, 3), (m1loc, 4),
                         (m2loc, 5)]
                for t_, j in dcols:
                    P.sop("vector", lambda e, t_=t_, j=j: e.tensor_copy(
                        diag_dbg_sb[:, j:j + 1], t_[:]), "v")
                for t_, j0, w in [(scale2, 8, KD), (bias2, 16, KD),
                                  (scale1, 24, KF), (bias1, 56, KF),
                                  (sw2t, 88, KD), (cs2t, 96, KD), (b2t, 104, KD)]:
                    P.sop("vector", lambda e, t_=t_, j0=j0, w=w: e.tensor_copy(
                        diag_dbg_sb[:, j0:j0 + w], t_[:]), "v")
                tdg = P.op("vector", lambda e: e.tensor_copy(
                    diag_dbg_sb[:, 120:121], eps_ln[:]), sem="v")
                P.wait("sync", tdg)
                P.dma("sync", diag_dbg, diag_dbg_sb[:], "ld")
                P.dma("sync", qw1_dbg, qw1t_d, "ld")
                P.dma("sync", qw2_dbg, qw2t_d, "ld")
            tok_hD_free = [None, None]
            tok_sD_free = [None, None]
            tok_xD_free = [None, None]
            tok_ob_free = None
            tok_qh_free = None
            tok_rb_free = None
            tok_gb_free = None
            dq_prev = None
            t_out_last = None
            for t in range(NTD):
                sl = t % 2
                ci, hf = t // 2, t % 2
                t_hh = []
                for hh in range(2):
                    P.wait("sync", tok_hD_free[hh])
                    tk = P.dma(
                        "sync",
                        hD[:, hh * 16 * TD:(hh + 1) * 16 * TD].rearrange(
                            "p (o f w) -> p o f w", o=1, f=16),
                        h_d.rearrange("p (i f w) -> p i f w", i=NT, f=KF)[
                            :, ci:ci + 1, hh * 16:(hh + 1) * 16,
                            hf * TD:(hf + 1) * TD], "ld")
                    t_hh.append(tk)
                P.wait("sync", tok_sD_free[sl])
                t_s = P.dma(
                    "sync", sD[sl][:].rearrange("p (o k w) -> p o k w", o=1, k=KD),
                    s_d.rearrange("p (i k w) -> p i k w", i=NT, k=KD)[
                        :, ci:ci + 1, :, hf * TD:(hf + 1) * TD], "ld")
                P.wait("sync", tok_xD_free[sl])
                t_x2 = P.dma("sync", xD[sl][:].rearrange("p (c d) -> p c d", c=2),
                             x_in[t * TD:(t + 1) * TD, :].rearrange(
                                 "(c p) d -> p c d", p=128), "ld")
                tq_toks = []
                for pr in range(4):
                    a0, a1 = 2 * pr * 1024, (2 * pr + 1) * 1024
                    tkA, tkB, dq_prev = emit_quant_pair(
                        P, [hD[:, a0:a0 + 1024], hD[:, a1:a1 + 1024]],
                        [r2pp[:, 0:1], r2pp[:, 0:1]], dbufs,
                        [qh[:, a0:a0 + 1024], qh[:, a1:a1 + 1024]],
                        [t_hh[pr // 2], t_s2, dq_prev, tok_qh_free], sbias_a)
                    tq_toks += [tkA, tkB]
                    if pr == 1:
                        tok_hD_free[0] = tkB
                tok_hD_free[1] = tq_toks[-1]
                g_toks = []
                for k in range(KD):
                    ps = y2_ps[k % 2]
                    P.wait("tensor", tq_toks)
                    P.wait("tensor", t_qw2)
                    tmm = None
                    for fj in range(KF):
                        sem = "t" if fj == KF - 1 else None
                        col = (k * 4 + fj // 8) * 1024 + (fj % 8) * 128
                        tmm = P.op("tensor", lambda e, ps=ps, fj=fj, c0=col: e.matmul(
                            ps[:], qw2sb[:, c0:c0 + 128],
                            qh[:, fj * TD:(fj + 1) * TD],
                            start=(fj == 0), stop=(fj == KF - 1)), sem=sem) or tmm
                    P.wait("scalar", [tmm, t_bias2, tok_gb_free])
                    tev = P.op("scalar", lambda e, k=k, ps=ps: e.activation(
                        gb[:, k * TD:(k + 1) * TD], ps[:], AF.Silu,
                        bias=bias2[:, k:k + 1], scale=scale2[:, k:k + 1]), sem="a")
                    g_toks.append(tev)
                    P.wait("tensor", tev)
                tok_qh_free = tmm
                if debug and t == 0:
                    P.wait("sync", tq_toks)
                    P.dma("sync", qh_dbg, qh[:], "ld")
                    P.wait("sync", g_toks)
                    P.dma("sync", gb_dbg, gb[:], "ld")
                P.wait("vector", [t_s, tok_rb_free] + g_toks)
                t_r = P.op("vector", lambda e, sl=sl: e.tensor_mul(
                    rb[:], sD[sl][:], gb[:]), sem="v")
                tok_sD_free[sl] = t_r
                tok_gb_free = t_r
                t_ob = []
                for tc in range(2):
                    ps = rt_ps[tc]
                    P.wait("tensor", t_r)
                    ttr = None
                    for k in range(KD):
                        sem = "t" if k == KD - 1 else None
                        ttr = P.op("tensor", lambda e, ps=ps, k=k, tc=tc: e.transpose(
                            ps[:, k * 128:(k + 1) * 128],
                            rb[:, k * TD + tc * 128:k * TD + (tc + 1) * 128],
                            ident[:]), sem=sem) or ttr
                    P.wait("vector", [ttr, t_x2, tok_ob_free])
                    tado = P.op("vector", lambda e, tc=tc, sl=sl, ps=ps: e.tensor_add(
                        ob[:, tc * D:(tc + 1) * D], xD[sl][:, tc * D:(tc + 1) * D],
                        ps[:]), sem="v")
                    t_ob.append(tado)
                    P.wait("tensor", tado)
                if debug:
                    P.wait("sync", tq_toks)
                    P.dma("sync", qhfull_dbg[:, t * KF * TD:(t + 1) * KF * TD],
                          qh[:], "ld")
                    P.wait("sync", g_toks)
                    P.dma("sync", gfull_dbg[:, t * KD * TD:(t + 1) * KD * TD],
                          gb[:], "ld")
                if debug and t == 0:
                    P.wait("sync", [t_r] + t_ob)
                    P.dma("sync", rb_dbg, rb[:], "ld")
                    P.dma("sync", ob_dbg, ob[:], "ld")
                    P.dma("sync", sd_dbg, sD[sl][:], "ld")
                tok_xD_free[sl] = t_ob[-1]
                tok_rb_free = ttr
                P.quiesce()
                t_ob = t_ob + [P.settle("vector", "v", dumv[:])]
                P.wait("sync", t_ob)
                t_out_last = P.dma(
                    "sync", out_d[t * TD:(t + 1) * TD, :].rearrange(
                        "(c p) d -> p c d", p=128),
                    ob[:].rearrange("p (c d) -> p c d", c=2), "st")
                tok_ob_free = t_out_last

            P.wait("sync", t_out_last)

        block = es.enter_context(nc.Block())
        P.replay(block)

    return nc


def _make_in_maps(L0_attn_x, L0_attn_Wq, L0_attn_Wk, L0_attn_Wv,
                  L0_ffn_w1, L0_ffn_b1, L0_ffn_w2, L0_ffn_b2):
    x = np.ascontiguousarray(np.asarray(L0_attn_x, np.float32)).reshape(32768, D)
    wq = np.ascontiguousarray(np.asarray(L0_attn_Wq, np.float32))
    wk = np.ascontiguousarray(np.asarray(L0_attn_Wk, np.float32))
    wv = np.ascontiguousarray(np.asarray(L0_attn_Wv, np.float32))
    w1 = np.ascontiguousarray(np.asarray(L0_ffn_w1, np.float32))
    b1 = np.ascontiguousarray(np.asarray(L0_ffn_b1, np.float32))
    w2 = np.ascontiguousarray(np.asarray(L0_ffn_w2, np.float32))
    b2 = np.ascontiguousarray(np.asarray(L0_ffn_b2, np.float32))
    cst = _host_consts()
    b1t = np.ascontiguousarray(b1.reshape(KF, 128).T)
    b2t = np.ascontiguousarray(b2.reshape(KD, 128).T)

    in_maps = []
    for c in range(NCORES):
        mask = np.zeros((128, 8), np.float32)
        if c % 2 == 1:
            mask[:, c - 1] = 1.0
        in_maps.append({
            "x": np.ascontiguousarray(x[c * NTOK:(c + 1) * NTOK]),
            "wq_my": np.ascontiguousarray(wq[:, 128 * c:128 * (c + 1)]),
            "wk_my": np.ascontiguousarray(wk[:, 128 * c:128 * (c + 1)]),
            "wv_my": np.ascontiguousarray(wv[:, 128 * c:128 * (c + 1)]),
            "w1_my": np.ascontiguousarray(w1[512 * c:512 * (c + 1)]),
            "w2_my": np.ascontiguousarray(w2[128 * c:128 * (c + 1)]),
            "b1t": b1t, "b2t": b2t,
            "ident": cst["ident"], "mcc": cst["mcc"], "mbb": cst["mbb"],
            "mia": cst["mia"], "mib": cst["mib"],
            "coefa": cst["coefa"], "coefb": cst["coefb"],
            "mask8": mask,
        })
    return in_maps


def kernel(**inputs):
    nc = build_program()
    in_maps = _make_in_maps(**inputs)
    res = run_bass_kernel_spmd(nc, in_maps, core_ids=list(range(NCORES)))
    out = np.concatenate([r["out"] for r in res.results], axis=0)
    return out.reshape(4, 8192, 1024)


# version 13
# speedup vs baseline: 1.0118x; 1.0044x over previous
import contextlib

import numpy as np
import ml_dtypes
import concourse.bass as bass
import concourse.mybir as mybir
from concourse.bass_utils import run_bass_kernel_spmd

F32 = mybir.dt.float32
BF16 = mybir.dt.bfloat16
ALU = mybir.AluOpType
AF = mybir.ActivationFunctionType
AX = mybir.AxisListType.X

NCORES = 8
D = 1024
FF = 4096
NTOK = 4096
T = 512
NT = NTOK // T
TD = 256
NTD = NTOK // TD
KD = D // 128
KF = FF // 128
C2 = 512
EPS_LN = 1e-5
EPS_Q = 1e-8
SIN_SCL = float(2.0 * np.pi / 1024.0)

ENGINES = ("sync", "tensor", "vector", "scalar", "gpsimd")


def _ctr(m):
    m = np.asarray(m, np.int64) % 1024
    return (m - 1024 * (m >= 512)).astype(np.float32)


def _rows_to_chunks(a, p=128):
    r, w = a.shape
    return np.ascontiguousarray(
        a.reshape(r // p, p, w).transpose(1, 0, 2).reshape(p, (r // p) * w))


_CACHE = {}


def _host_consts():
    if _CACHE:
        return _CACHE
    e = np.arange(1024)[:, None]
    c = np.arange(C2)[None, :]
    mcc = _ctr(e * c + 256)
    mbb = _ctr(-e * c)
    mbb[:, 0] = _ctr(256 + 512 * e)[:, 0]
    ci = np.arange(C2)[:, None]
    dd = np.arange(D)[None, :]
    mia = _ctr(ci * dd + 256)
    mib = _ctr(-ci * dd)
    mib[0, :] = _ctr(256 + 512 * dd)[0, :]
    coefa = np.full((C2, 1), 2.0 / 1024.0, np.float32)
    coefa[0, 0] = 1.0 / 1024.0
    coefb = coefa.copy()
    _CACHE.update(
        ident=np.eye(128, dtype=np.float32),
        mcc=_rows_to_chunks(mcc), mbb=_rows_to_chunks(mbb),
        mia=_rows_to_chunks(mia), mib=_rows_to_chunks(mib),
        coefa=_rows_to_chunks(coefa), coefb=_rows_to_chunks(coefb),
    )
    return _CACHE


SEM_OWNER = {"v": "vector", "a": "scalar", "g": "gpsimd"}


class Prog:
    def __init__(self):
        self.q = {e: [] for e in ENGINES}
        self.sem = {}
        self.cnt = {}
        self.mark = {e: {} for e in ENGINES}
        self.scratch = {}
        self.settled = {}
        self.dma_rr = {}

    def add_sem(self, name, handle):
        self.sem[name] = handle
        self.cnt[name] = 0

    def wait(self, eng, tok, raw=False):
        if tok is None:
            return
        if isinstance(tok, list):
            for t in tok:
                self.wait(eng, t, raw=raw)
            return
        name, val = tok
        owner = SEM_OWNER.get(name)
        if not raw and owner is not None and owner != eng:
            st = self.settled.get(name)
            if st is None or st[1] < val:
                cover = self.cnt[name]
                stok = self.sop(owner, self._mk_settle(owner), name)
                self.settled[name] = (stok, cover)
                tok = stok
            else:
                tok = st[0]
            name, val = tok
        if self.mark[eng].get(name, 0) >= val:
            return
        self.mark[eng][name] = val
        h = self.sem[name]
        self.q[eng].append(lambda e, h=h, v=val: e.wait_ge(h, v))

    def _mk_settle(self, owner):
        a = self.scratch[owner]
        if owner == "scalar":
            return lambda e, a=a: e.copy(a, a)
        return lambda e, a=a: e.tensor_copy(a, a)

    def op(self, eng, build, sem=None, n=1):
        if sem is None:
            self.q[eng].append(build)
            return None
        self.cnt[sem] += n
        tok = (sem, self.cnt[sem])
        h = self.sem[sem]
        self.q[eng].append(lambda e, b=build, h=h, n=n: b(e).then_inc(h, n))
        return tok

    def settle(self, eng, sem, scratch_ap):
        if eng == "scalar":
            return self.sop(eng, lambda e, a=scratch_ap: e.copy(a, a), sem)
        return self.sop(eng, lambda e, a=scratch_ap: e.tensor_copy(a, a), sem)

    def quiesce(self):
        for s, owner in SEM_OWNER.items():
            if self.cnt.get(s, 0) > 0:
                self.sop(owner, self._mk_settle(owner), s)
        snap = [(s, c) for s, c in self.cnt.items() if c > 0]
        for eng in ENGINES:
            for s, c in snap:
                self.wait(eng, (s, c), raw=True)

    def sop(self, eng, build, sem):
        tok = self.op(eng, build, sem=sem)
        self.wait(eng, tok, raw=True)
        return tok

    def dma(self, eng, out_ap, in_ap, sem, slow=False):
        self.dma_rr[sem] = self.dma_rr.get(sem, 0) + 1
        name = "%s%d" % (sem, self.dma_rr[sem] % 8)
        return self.op(eng, lambda e, o=out_ap, i=in_ap, s=slow: e.dma_start(
            out=o, in_=i, allow_slow_non_contiguous=s), sem=name, n=16)

    def replay(self, block):
        for eng in ENGINES:
            queue = self.q[eng]

            def run(e, queue=queue):
                for fn in queue:
                    fn(e)
            getattr(block, eng)(run)


def emit_quant_pair(P, srcs, r_pps, bufs, outs, deps, sbias_col, *,
                    weight_mode=False):
    yA, fA, gA, yB, fB, gB = [b[:] for b in bufs]
    lo = -128.0 if weight_mode else 0.0
    hi = 127.0 if weight_mode else 255.0
    P.wait("vector", deps)
    ty = {}
    for s, (src_ap, r_pp, y, f) in enumerate(
            ((srcs[0], r_pps[0], yA, fA), (srcs[1], r_pps[1], yB, fB))):
        P.op("vector", lambda e, y=y, src_ap=src_ap, r=r_pp: e.tensor_scalar(
            y, src_ap, r, lo, ALU.mult, ALU.max))
        P.op("vector", lambda e, y=y: e.tensor_single_scalar(y, y, hi, ALU.min))
        ty[s] = P.op("vector", lambda e, f=f, y=y: e.tensor_copy(f, y), sem="v")
    va = {0: ty[0], 1: ty[1]}
    ta = {}
    for i, k in enumerate(range(7, -1, -1)):
        c = -float(2 ** (k - 1))
        for s, (f, g) in enumerate(((fA, gA), (fB, gB))):
            P.wait("scalar", va[s], raw=True)
            ta[s] = P.op("scalar", lambda e, f=f, g=g: e.activation(
                g, f, AF.Sign, bias=sbias_col[:, 0:1], scale=1.0), sem="a")
        for s, (f, g) in enumerate(((fA, gA), (fB, gB))):
            P.wait("vector", ta[s], raw=True)
            va[s] = P.op("vector", lambda e, f=f, g=g, c=c: e.scalar_tensor_tensor(
                f, g, c, f, ALU.mult, ALU.add), sem="v")
    touts = []
    for s, (y, f, out_ap) in enumerate(((yA, fA, outs[0]), (yB, fB, outs[1]))):
        P.wait("vector", va[s], raw=True)
        if weight_mode:
            tk = P.op("vector", lambda e, y=y, f=f, o=out_ap: e.scalar_tensor_tensor(
                o, y, -0.5, f, ALU.add, ALU.subtract), sem="v")
        else:
            tk = P.op("vector", lambda e, y=y, f=f, o=out_ap: e.tensor_sub(
                o, y, f), sem="v")
        touts.append(tk)
    return touts[0], touts[1], ta[1]


def emit_round_wide(P, y_ap, q_ap, ab_ap, t1_ap, t2_ap, sg_ap, zbias, deps):
    P.wait("scalar", deps)
    tsg = P.op("scalar", lambda e: e.activation(sg_ap, y_ap, AF.Sign,
                                                bias=zbias[:, 0:1], scale=1.0),
               sem="a")
    tab = P.op("scalar", lambda e: e.activation(ab_ap, y_ap, AF.Abs,
                                                bias=zbias[:, 0:1], scale=1.0),
               sem="a")
    P.wait("vector", [tab, tsg])
    P.sop("vector", lambda e: e.tensor_scalar_add(t1_ap, ab_ap, 0.5), "v")
    for k in range(39, -1, -1):
        c = float(2 ** k)
        P.sop("vector", lambda e, c=c: e.tensor_scalar(t2_ap, t1_ap, c, -c,
                                                       ALU.is_ge, ALU.mult), "v")
        P.sop("vector", lambda e: e.tensor_add(t1_ap, t1_ap, t2_ap), "v")
    P.sop("vector", lambda e: e.scalar_tensor_tensor(t2_ap, ab_ap, 0.5, t1_ap,
                                                     ALU.add, ALU.subtract), "v")
    return P.sop("vector", lambda e: e.tensor_mul(q_ap, t2_ap, sg_ap), "v")


def build_program(debug=False):
    nc = bass.Bass("TRN2", target_bir_lowering=False)

    def din(name, shape, dtype=F32):
        return nc.dram_tensor(name, shape, dtype, kind="ExternalInput").ap()

    x_in = din("x", [NTOK, D])
    wq_my = din("wq_my", [D, 128])
    wk_my = din("wk_my", [D, 128])
    wv_my = din("wv_my", [D, 128])
    w1_my = din("w1_my", [512, D])
    w2_my = din("w2_my", [128, FF])
    b1_in = din("b1t", [128, KF])
    b2_in = din("b2t", [128, KD])
    ident_in = din("ident", [128, 128])
    mcc_in = din("mcc", [128, KD * C2])
    mbb_in = din("mbb", [128, KD * C2])
    mia_in = din("mia", [128, 4 * D])
    mib_in = din("mib", [128, 4 * D])
    coefa_in = din("coefa", [128, 4])
    coefb_in = din("coefb", [128, 4])
    mask_in = din("mask8", [128, 8])

    out_d = nc.dram_tensor("out", [NTOK, D], F32, kind="ExternalOutput").ap()

    kind_dbg = "ExternalOutput" if debug else "Internal"
    xnt_d = nc.dram_tensor("xnt_sp", [128, NT * KD * T], F32, kind=kind_dbg).ap()
    p_d = nc.dram_tensor("p_sp", [128, NT * 8 * T], F32, kind=kind_dbg).ap()
    fq_d = nc.dram_tensor("fq_sp", [128, NT * 8 * T], F32, kind=kind_dbg).ap()
    s_d = nc.dram_tensor("s_sp", [128, NT * KD * T], F32, kind=kind_dbg).ap()
    h_d = nc.dram_tensor("h_sp", [128, KF * NTOK], F32, kind=kind_dbg).ap()
    xn_dbg = nc.dram_tensor("xn_dbg", [128, 4 * D], F32, kind=kind_dbg).ap()
    cc_dbg = nc.dram_tensor("cc_dbg", [128, KD * C2], F32, kind=kind_dbg).ap()
    tab_dbg = nc.dram_tensor("tab_dbg", [128, KD * C2], F32, kind=kind_dbg).ap()
    agi_dbg = nc.dram_tensor("agi_dbg", [128, 6 * C2 + 5], F32, kind=kind_dbg).ap()
    diag_dbg = nc.dram_tensor("diag_dbg", [128, 128], F32, kind=kind_dbg).ap()
    qw1_dbg = nc.dram_tensor("qw1_dbg", [128, KD * FF], BF16, kind=kind_dbg).ap()
    xq_dbg = nc.dram_tensor("xq_dbg", [128, KD * T], BF16, kind=kind_dbg).ap()
    qh_dbg = nc.dram_tensor("qh_dbg", [128, KF * TD], BF16, kind=kind_dbg).ap()
    qw2_dbg = nc.dram_tensor("qw2_dbg", [128, KF * D], BF16, kind=kind_dbg).ap()
    gb_dbg = nc.dram_tensor("gb_dbg", [128, KD * TD], F32, kind=kind_dbg).ap()
    rb_dbg = nc.dram_tensor("rb_dbg", [128, KD * TD], F32, kind=kind_dbg).ap()
    gfull_dbg = nc.dram_tensor("gfull_dbg", [128, KD * NTOK], F32, kind=kind_dbg).ap()
    qhfull_dbg = nc.dram_tensor("qhfull_dbg", [128, KF * NTOK], BF16, kind=kind_dbg).ap()
    ob_dbg = nc.dram_tensor("ob_dbg", [128, 2 * D], F32, kind=kind_dbg).ap()
    sd_dbg = nc.dram_tensor("sd_dbg", [128, KD * TD], F32, kind=kind_dbg).ap()
    id_dbg = nc.dram_tensor("id_dbg", [128, 128], F32, kind=kind_dbg).ap()
    qw1t_d = nc.dram_tensor("qw1t_d", [128, KD * FF], BF16).ap()
    qw2t_d = nc.dram_tensor("qw2t_d", [128, KF * D], BF16).ap()

    ag1_in = nc.dram_tensor("ag1_in", [128, 6 * C2 + 5], F32)
    ag1_out = nc.dram_tensor("ag1_out", [1024, 6 * C2 + 5], F32)
    agw1_in = nc.dram_tensor("agw1_in", [512, D], F32)
    agw1_out = nc.dram_tensor("agw1_out", [FF, D], F32)
    agw2_in = nc.dram_tensor("agw2_in", [128, FF], F32)
    agw2_out = nc.dram_tensor("agw2_out", [D, FF], F32)
    ag2_in = nc.dram_tensor("ag2_in", [128, 10], F32)
    ag2_out = nc.dram_tensor("ag2_out", [1024, 10], F32)
    ag3_in = nc.dram_tensor("ag3_in", [128, 1], F32)
    ag3_out = nc.dram_tensor("ag3_out", [1024, 1], F32)

    P = Prog()
    es = contextlib.ExitStack()
    with es:
        for s in ("ld", "sp", "cc", "t", "v", "a", "g", "st"):
            P.add_sem(s, es.enter_context(nc.semaphore(s)))
        for q in ("ld", "sp", "st"):
            for i in range(8):
                nm = "%s%d" % (q, i)
                P.add_sem(nm, es.enter_context(nc.semaphore(nm)))

        sb = {}

        def alloc(name, shape, dtype=F32, stack=es):
            t = stack.enter_context(nc.sbuf_tensor("sb_" + name, shape, dtype))
            sb[name] = t
            return t

        ident = alloc("ident", [128, 128])
        identb = alloc("identb", [128, 128], BF16)
        ones128 = alloc("ones128", [128, 128])
        sbias_a = alloc("sbias_a", [128, 1])
        sbias_w = alloc("sbias_w", [128, 1])
        zbias = alloc("zbias", [128, 1])
        mask8 = alloc("mask8", [128, 8])
        m1loc = alloc("m1loc", [128, 1])
        m2loc = alloc("m2loc", [128, 1])
        totals = alloc("totals", [128, 8])
        offs = alloc("offs", [128, 8])
        s1pp = alloc("s1pp", [128, 1])
        r1pp = alloc("r1pp", [128, 1])
        s2pp = alloc("s2pp", [128, 1])
        r2pp = alloc("r2pp", [128, 1])
        sw1t = alloc("sw1t", [128, KF])
        sw2t = alloc("sw2t", [128, KD])
        cs1t = alloc("cs1t", [128, KF])
        cs2t = alloc("cs2t", [128, KD])
        scale1 = alloc("scale1", [128, KF])
        bias1 = alloc("bias1", [128, KF])
        scale2 = alloc("scale2", [128, KD])
        bias2 = alloc("bias2", [128, KD])
        b1t = alloc("b1t", [128, KF])
        b2t = alloc("b2t", [128, KD])
        eps_ln = alloc("eps_ln", [128, 1])
        zcol = alloc("zcol", [128, 1])
        red1 = alloc("red1", [128, 1])
        redA = alloc("redA", [128, 1])
        agb = alloc("agb", [128, 80])
        qb1 = alloc("qb1", [128, KF])
        qb2 = alloc("qb2", [128, KF])
        qb3 = alloc("qb3", [128, KF])
        qb4 = alloc("qb4", [128, KF])
        qb5 = alloc("qb5", [128, KF])
        qb6 = alloc("qb6", [128, KF])
        diag_dbg_sb = alloc("diag_sb", [128, 128])
        dumv = alloc("dumv", [128, 1])
        dumg = alloc("dumg", [128, 1])
        agst = alloc("agst", [128, 10])
        duma = alloc("duma", [128, 1])

        t_id = P.dma("sync", ident[:], ident_in, "ld")
        t_mk = P.dma("sync", mask8[:], mask_in, "ld")
        t_b1 = P.dma("sync", b1t[:], b1_in, "ld")
        t_b2 = P.dma("sync", b2t[:], b2_in, "ld")
        P.wait("vector", t_id)
        P.op("vector", lambda e: e.tensor_copy(identb[:], ident[:]))
        P.op("vector", lambda e: e.memset(ones128[:], 1.0))
        P.op("vector", lambda e: e.memset(eps_ln[:], EPS_LN))
        P.op("vector", lambda e: e.memset(m1loc[:], -1e30))
        P.op("vector", lambda e: e.memset(m2loc[:], 0.0))
        P.op("vector", lambda e: e.memset(totals[:], 0.0))
        P.op("vector", lambda e: e.memset(offs[:], 0.0))
        P.op("vector", lambda e: e.memset(dumv[:], 0.0))
        P.op("vector", lambda e: e.memset(duma[:], 0.0))
        P.op("vector", lambda e: e.memset(dumg[:], 0.0))
        P.scratch = {"vector": dumv[:], "scalar": duma[:], "gpsimd": dumg[:]}
        P.op("vector", lambda e: e.memset(sbias_a[:], -127.5))
        P.op("vector", lambda e: e.memset(sbias_w[:], 0.5))
        P.op("vector", lambda e: e.memset(zbias[:], 0.0))
        t_init = P.op("vector", lambda e: e.tensor_copy(zcol[:], eps_ln[:]), sem="v")

        def emit_allmax_bcast(src8_ap, dst_pp, dep, br_ps):
            P.wait("vector", dep)
            tr = P.sop("vector", lambda e: e.tensor_reduce(
                red1[:], src8_ap, AX, ALU.max), "v")
            P.wait("tensor", tr)
            tt = P.op("tensor", lambda e: e.transpose(
                br_ps[0:1, 0:128], red1[:], ident[:]), sem="t")
            P.wait("vector", tt)
            tr2 = P.sop("vector", lambda e: e.tensor_reduce(
                redA[0:1, 0:1], br_ps[0:1, 0:128], AX, ALU.max), "v")
            P.sop("vector", lambda e: e.memset(zcol[:], 0.0), "v")
            tz = P.sop("vector", lambda e: e.tensor_copy(
                zcol[0:1, 0:1], redA[0:1, 0:1]), "v")
            tz = P.settle("vector", "v", dumv[:])
            P.wait("tensor", tz)
            tb = P.op("tensor", lambda e: e.matmul(
                br_ps[:, 0:1], ones128[:], zcol[:], start=True, stop=True), sem="t")
            P.wait("vector", tb)
            P.sop("vector", lambda e: e.tensor_copy(dst_pp, br_ps[:, 0:1]), "v")
            return P.settle("vector", "v", dumv[:])

        st_atab = contextlib.ExitStack()
        atabs = [alloc("tab" + str(i), [128, KD * C2], stack=st_atab)
                 for i in range(6)]

        with contextlib.ExitStack() as st1:
            wsh1 = alloc("wsh1", [128, 4 * D], stack=st1)
            wsh2 = alloc("wsh2", [128, FF], stack=st1)
            swsh1 = alloc("swsh1", [128, 4], stack=st1)
            rwsh1 = alloc("rwsh1", [128, 4], stack=st1)
            swsh2 = alloc("swsh2", [128, 1], stack=st1)
            rwsh2 = alloc("rwsh2", [128, 1], stack=st1)
            st1a = contextlib.ExitStack()
            cc_t = alloc("cc_t", [128, KD * C2], stack=st1a)
            bb_t = alloc("bb_t", [128, KD * C2], stack=st1a)
            wmy = alloc("wmy", [128, KD * 128], stack=st1a)
            agin = alloc("agin", [128, 6 * C2 + 5], stack=st1a)
            comb_ps = st1a.enter_context(nc.psum_tensor("comb_ps", [128, C2], F32))

            t_mcc = P.dma("sync", cc_t[:], mcc_in, "ld")
            t_mbb = P.dma("sync", bb_t[:], mbb_in, "ld")
            P.wait("scalar", t_mcc)
            t_cc = P.op("scalar", lambda e: e.activation(
                cc_t[:], cc_t[:], AF.Sin, bias=0.0, scale=SIN_SCL), sem="a")
            P.wait("scalar", t_mbb)
            t_bb = P.op("scalar", lambda e: e.activation(
                bb_t[:], bb_t[:], AF.Sin, bias=0.0, scale=SIN_SCL), sem="a")

            t_w1 = P.dma("sync", wsh1[:].rearrange("p (c d) -> p c d", c=4),
                         w1_my.rearrange("(c p) d -> p c d", p=128), "ld")
            t_w2 = P.dma("sync", wsh2[:], w2_my, "ld")
            P.wait("vector", t_w1)
            for ch in range(4):
                P.sop("vector", lambda e, ch=ch: e.tensor_reduce(
                    swsh1[:, ch:ch + 1], wsh1[:, ch * D:(ch + 1) * D], AX, ALU.max,
                    apply_absolute_value=True), "v")
            P.sop("vector", lambda e: e.tensor_scalar(
                swsh1[:], swsh1[:], EPS_Q, 1.0 / 127.0, ALU.max, ALU.mult), "v")
            t_sw1l = P.sop("vector", lambda e: e.reciprocal(rwsh1[:], swsh1[:]),
                           "v")
            P.wait("vector", t_w2)
            P.sop("vector", lambda e: e.tensor_reduce(
                swsh2[:], wsh2[:], AX, ALU.max, apply_absolute_value=True), "v")
            P.sop("vector", lambda e: e.tensor_scalar(
                swsh2[:], swsh2[:], EPS_Q, 1.0 / 127.0, ALU.max, ALU.mult), "v")
            t_sw2l = P.sop("vector", lambda e: e.reciprocal(rwsh2[:], swsh2[:]),
                           "v")
            P.sop("vector", lambda e: e.tensor_copy(agin[:, 6 * C2:6 * C2 + 4],
                                                    swsh1[:]), "v")
            P.sop("vector", lambda e: e.tensor_copy(
                agin[:, 6 * C2 + 4:6 * C2 + 5], swsh2[:]), "v")
            t_scl = P.settle("vector", "v", dumv[:])

            tok_last_ev = None
            tok_last_mm = None
            for wi, wap in enumerate((wq_my, wk_my, wv_my)):
                P.wait("sync", tok_last_mm)
                t_wm = P.dma("sync", wmy[:].rearrange("p (k d) -> p k d", k=KD),
                             wap.rearrange("(k p) d -> p k d", p=128), "ld")
                for ti, (tbl, ttok) in enumerate(((cc_t, t_cc), (bb_t, t_bb))):
                    P.wait("tensor", [t_wm, ttok])
                    if tok_last_ev is not None:
                        P.wait("tensor", tok_last_ev)
                    tmm = None
                    for k in range(KD):
                        sem = "t" if k == KD - 1 else None
                        tmm = P.op("tensor", lambda e, k=k, tbl=tbl: e.matmul(
                            comb_ps[:], wmy[:, k * 128:(k + 1) * 128],
                            tbl[:, k * C2:(k + 1) * C2],
                            start=(k == 0), stop=(k == KD - 1)), sem=sem) or tmm
                    slot = 2 * wi + ti
                    tok_last_mm = tmm
                    P.wait("scalar", tmm)
                    tok_last_ev = P.op("scalar", lambda e, s=slot: e.copy(
                        agin[:, s * C2:(s + 1) * C2], comb_ps[:]), sem="a")

            tok_last_ev = P.settle("scalar", "a", duma[:])
            P.wait("sync", [tok_last_ev, t_scl])
            if debug:
                P.dma("sync", cc_dbg, cc_t[:], "ld")
                P.dma("sync", agi_dbg, agin[:], "ld")
            t_agin = P.dma("sync", ag1_in.ap(), agin[:], "ld")
            P.wait("gpsimd", t_agin)
            t_ag1 = P.op("gpsimd", lambda e: e.collective_compute(
                "AllGather", ALU.bypass, replica_groups=[list(range(NCORES))],
                ins=[ag1_in.ap()], outs=[ag1_out.ap()]), sem="cc")
            P.wait("sync", t_ag1)
            t_tabs = None
            for m in range(6):
                t_tabs = P.dma(
                    "sync", atabs[m][:].rearrange("p (k c) -> p k c", k=KD),
                    ag1_out.ap()[:, m * C2:(m + 1) * C2].rearrange(
                        "(k p) c -> p k c", p=128), "ld")
            if debug:
                P.wait("sync", t_tabs)
                P.dma("sync", tab_dbg, atabs[0][:], "ld")
            t_sw1 = P.dma("sync", sw1t[:].rearrange("p (r c) -> p r c", r=8),
                          ag1_out.ap()[:, 6 * C2:6 * C2 + 4].rearrange(
                              "(r p) c -> p r c", p=128), "ld")
            t_sw2 = P.dma("sync", sw2t[:].rearrange("p (r c) -> p r c", r=8),
                          ag1_out.ap()[:, 6 * C2 + 4:6 * C2 + 5].rearrange(
                              "(r p) c -> p r c", p=128), "ld", slow=True)

            st1a.close()
            qsh = alloc("qsh", [128, 4 * D], F32, stack=st1)
            wbufs = tuple(alloc("wqb%d" % i, [128, D], stack=st1)
                          for i in range(6))
            tq = []
            tprev = None
            for pr in range(2):
                s0, s1 = 2 * pr, 2 * pr + 1
                tkA, tkB, tprev = emit_quant_pair(
                    P, [wsh1[:, s0 * D:(s0 + 1) * D], wsh1[:, s1 * D:(s1 + 1) * D]],
                    [rwsh1[:, s0:s0 + 1], rwsh1[:, s1:s1 + 1]], wbufs,
                    [qsh[:, s0 * D:(s0 + 1) * D], qsh[:, s1 * D:(s1 + 1) * D]],
                    [t_sw1l, tprev, t_agin], sbias_w, weight_mode=True)
                tq += [tkA, tkB]
            tq.append(P.settle("vector", "v", dumv[:]))
            P.wait("sync", tq)
            t_q1d = P.dma("sync", agw1_in.ap().rearrange("(c p) d -> p c d", p=128),
                          qsh[:].rearrange("p (c d) -> p c d", c=4), "ld")
            P.wait("gpsimd", t_q1d)
            t_agw1 = P.op("gpsimd", lambda e: e.collective_compute(
                "AllGather", ALU.bypass, replica_groups=[list(range(NCORES))],
                ins=[agw1_in.ap()], outs=[agw1_out.ap()]), sem="cc")

            tq = []
            for pr in range(2):
                s0, s1 = 2 * pr, 2 * pr + 1
                tkA, tkB, tprev = emit_quant_pair(
                    P, [wsh2[:, s0 * D:(s0 + 1) * D], wsh2[:, s1 * D:(s1 + 1) * D]],
                    [rwsh2[:], rwsh2[:]], wbufs,
                    [qsh[:, s0 * D:(s0 + 1) * D], qsh[:, s1 * D:(s1 + 1) * D]],
                    [t_sw2l, tprev, t_q1d], sbias_w, weight_mode=True)
                tq += [tkA, tkB]
            tq.append(P.settle("vector", "v", dumv[:]))
            P.wait("sync", tq)
            t_q2d = P.dma("sync", agw2_in.ap(), qsh[:], "ld")
            P.wait("gpsimd", t_q2d)
            t_agw2 = P.op("gpsimd", lambda e: e.collective_compute(
                "AllGather", ALU.bypass, replica_groups=[list(range(NCORES))],
                ins=[agw2_in.ap()], outs=[agw2_out.ap()]), sem="cc")

            tbin = alloc("tbin", [128, FF], F32, stack=st1)
            tbout = alloc("tbout", [128, KD * 128], BF16, stack=st1)
            tb_ps = st1.enter_context(nc.psum_tensor("tb_ps", [128, 1024], F32))

            P.wait("sync", t_agw1)
            t_dma_prev = None
            t_cs_prev = None
            for j in range(KF):
                if t_dma_prev is not None:
                    P.wait("sync", t_dma_prev)
                P.wait("sync", t_cs_prev)
                t_in = P.dma("sync", tbin[:, 0:D],
                             agw1_out.ap()[j * 128:(j + 1) * 128, 0:D], "ld")
                P.wait("vector", t_in)
                t_cs_prev = P.sop("vector", lambda e, j=j: e.tensor_reduce(
                    cs1t[:, j:j + 1], tbin[:, 0:D], AX, ALU.add), "v")
                P.wait("tensor", t_in)
                ttr = None
                for k in range(KD):
                    sem = "t" if k == KD - 1 else None
                    ttr = P.op("tensor", lambda e, k=k: e.transpose(
                        tb_ps[:, k * 128:(k + 1) * 128],
                        tbin[:, k * 128:(k + 1) * 128], ident[:]), sem=sem) or ttr
                P.wait("scalar", ttr)
                P.wait("scalar", t_dma_prev)
                tev = P.op("scalar", lambda e: e.copy(tbout[:], tb_ps[:, 0:1024]),
                           sem="a")
                P.wait("tensor", tev)
                tset = P.settle("scalar", "a", duma[:])
                P.wait("sync", [tset, ttr])
                t_dma_prev = P.dma(
                    "sync", qw1t_d[:, j * 1024:(j + 1) * 1024], tbout[:], "ld")

            P.wait("sync", t_agw2)
            for j in range(KD):
                P.wait("sync", t_dma_prev)
                P.wait("sync", t_cs_prev)
                t_in = P.dma("sync", tbin[:, 0:FF],
                             agw2_out.ap()[j * 128:(j + 1) * 128, 0:FF], "ld")
                P.wait("vector", t_in)
                t_cs_prev = P.sop("vector", lambda e, j=j: e.tensor_reduce(
                    cs2t[:, j:j + 1], tbin[:, 0:FF], AX, ALU.add), "v")
                P.wait("tensor", t_in)
                for kb in range(4):
                    ttr = None
                    for k in range(8):
                        kk = kb * 8 + k
                        sem = "t" if k == 7 else None
                        ttr = P.op("tensor", lambda e, kk=kk, k=k: e.transpose(
                            tb_ps[:, k * 128:(k + 1) * 128],
                            tbin[:, kk * 128:(kk + 1) * 128], ident[:]),
                            sem=sem) or ttr
                    P.wait("scalar", ttr)
                    P.wait("scalar", t_dma_prev)
                    tev = P.op("scalar", lambda e: e.copy(
                        tbout[:, 0:1024], tb_ps[:, 0:1024]), sem="a")
                    P.wait("tensor", tev)
                    tset = P.settle("scalar", "a", duma[:])
                    P.wait("sync", tset)
                    t_dma_prev = P.dma(
                        "sync",
                        qw2t_d[:, (j * 4 + kb) * 1024:(j * 4 + kb + 1) * 1024],
                        tbout[:, 0:1024], "ld")
            t_wt_done = t_dma_prev

        P.quiesce()
        with contextlib.ExitStack() as b1:
            xbuf = [alloc("xb0", [128, 4 * D], stack=b1),
                    alloc("xb1", [128, 4 * D], stack=b1)]
            sqt = alloc("sqt", [128, D], stack=b1)
            mt = alloc("mt", [128, 1], stack=b1)
            vt = alloc("vt", [128, 1], stack=b1)
            rstd = alloc("rstd", [128, 1], stack=b1)
            xnt = alloc("xnt", [128, KD * T], stack=b1)
            fqb = alloc("fqb", [128, 8 * T], stack=b1)
            fvt = alloc("fvt", [128, 2 * T], stack=b1)
            pb = alloc("pb", [128, 8 * T], stack=b1)
            vt1 = alloc("vt1", [128, T], stack=b1)
            vt2 = alloc("vt2", [128, T], stack=b1)
            ttp = alloc("ttp", [128, 1], stack=b1)
            tr_ps = [b1.enter_context(nc.psum_tensor("trp0", [128, 512], F32)),
                     b1.enter_context(nc.psum_tensor("trp1", [128, 512], F32))]
            pl_ps = [b1.enter_context(nc.psum_tensor("plp%d" % i, [128, 512], F32))
                     for i in range(6)]

            tok_x_free = [None, None]
            tok_xnt_sp = None
            tok_fq_sp = None
            tok_p_sp = None
            tok_proj_done = None
            tok_ev_done = None

            for i in range(NT):
                sl = i % 2
                P.wait("sync", tok_x_free[sl])
                t_x = P.dma("sync",
                            xbuf[sl][:].rearrange("p (c d) -> p c d", c=4),
                            x_in[i * T:(i + 1) * T, :].rearrange(
                                "(c p) d -> p c d", p=128), "ld")
                ln_toks = []
                P.wait("vector", t_x)
                for tc in range(4):
                    ch = xbuf[sl][:, tc * D:(tc + 1) * D]
                    P.sop("vector", lambda e, ch=ch: e.tensor_reduce(
                        mt[:], ch, AX, ALU.add), "v")
                    P.sop("vector", lambda e: e.tensor_single_scalar(
                        mt[:], mt[:], 1.0 / D, ALU.mult), "v")
                    tv = P.op("vector", lambda e, ch=ch: e.tensor_scalar(
                        ch, ch, mt[:, 0:1], None, ALU.subtract), sem="v")
                    P.wait("scalar", tv)
                    ta = P.op("scalar", lambda e, ch=ch: e.activation(
                        sqt[:], ch, AF.Square, bias=0.0, scale=1.0), sem="a")
                    P.wait("vector", ta)
                    P.sop("vector", lambda e: e.tensor_reduce(
                        vt[:], sqt[:], AX, ALU.add), "v")
                    tv2 = P.op("vector", lambda e: e.tensor_copy(vt[:], vt[:]),
                               sem="v")
                    P.wait("scalar", tv2)
                    P.op("scalar", lambda e: e.activation(
                        rstd[:], vt[:], AF.Sqrt, bias=eps_ln[:, 0:1],
                        scale=1.0 / D))
                    ta2 = P.op("scalar", lambda e: e.copy(duma[:], duma[:]),
                               sem="a")
                    P.wait("vector", ta2)
                    P.sop("vector", lambda e: e.reciprocal(rstd[:], rstd[:]), "v")
                    tln = P.op("vector", lambda e, ch=ch: e.tensor_scalar(
                        ch, ch, rstd[:, 0:1], None, ALU.mult), sem="v")
                    ln_toks.append(tln)
                if debug and i == 0:
                    P.wait("gpsimd", ln_toks)
                    P.dma("gpsimd", xn_dbg, xbuf[sl][:], "sp")
                    P.dma("gpsimd", id_dbg, ident[:], "sp")
                P.sop("vector", lambda e, sl=sl: e.tensor_reduce(
                    ttp[:], xbuf[sl][:], AX, ALU.max), "v")
                t_m1 = P.sop("vector", lambda e: e.tensor_max(
                    m1loc[:], m1loc[:], ttp[:]), "v")

                ev_toks = []
                for tc in range(4):
                    P.wait("tensor", ln_toks[tc])
                    ttr = None
                    for half in range(2):
                        ps = tr_ps[half]
                        for k in range(4):
                            kk = half * 4 + k
                            sem = "t" if k == 3 else None
                            ttr = P.op("tensor", lambda e, ps=ps, k=k, kk=kk, sl=sl,
                                       tc=tc: e.transpose(
                                           ps[:, k * 128:(k + 1) * 128],
                                           xbuf[sl][:, tc * D + kk * 128:
                                                    tc * D + (kk + 1) * 128],
                                           ident[:]), sem=sem) or ttr
                        P.wait("scalar", ttr)
                        if tok_proj_done is not None:
                            P.wait("scalar", tok_proj_done)
                        P.wait("scalar", tok_xnt_sp)
                        dst = xnt[:].rearrange("p (k t) -> p k t", k=KD)[
                            :, half * 4:(half + 1) * 4, tc * 128:(tc + 1) * 128]
                        tev = P.op("scalar", lambda e, dst=dst, ps=ps: e.copy(
                            dst, ps[:].rearrange("p (k t) -> p k t", k=4)), sem="a")
                        ev_toks.append(tev)
                        P.wait("tensor", tev)
                tok_x_free[sl] = ttr
                tok_ev_done = ev_toks[-1]

                t_set_a = P.settle("scalar", "a", duma[:])
                P.wait("gpsimd", [t_set_a] + ev_toks)
                tok_xnt_sp = P.dma("gpsimd", xnt_d[:, i * KD * T:(i + 1) * KD * T],
                                   xnt[:], "sp")

                p_toks = []
                for jc in range(4):
                    mm_tok = [None] * 6
                    for pl in range(6):
                        P.wait("tensor", ev_toks)
                        if i > 0 or jc > 0:
                            pass
                        P.wait("tensor", t_tabs)
                        for k in range(KD):
                            sem = "t" if k == KD - 1 else None
                            tk = P.op("tensor", lambda e, pl=pl, k=k, jc=jc: e.matmul(
                                pl_ps[pl][:],
                                atabs[pl][:, k * C2 + jc * 128:k * C2 + (jc + 1) * 128],
                                xnt[:, k * T:(k + 1) * T],
                                start=(k == 0), stop=(k == KD - 1)), sem=sem)
                            if sem:
                                mm_tok[pl] = tk
                    P.wait("scalar", [mm_tok[0], mm_tok[1], mm_tok[4], mm_tok[5]])
                    P.wait("scalar", tok_fq_sp)
                    e1 = P.op("scalar", lambda e, jc=jc: e.copy(
                        fqb[:, jc * T:(jc + 1) * T], pl_ps[0][:]), sem="a")
                    e2 = P.op("scalar", lambda e, jc=jc: e.copy(
                        fqb[:, (4 + jc) * T:(5 + jc) * T], pl_ps[1][:]), sem="a")
                    e3 = P.op("scalar", lambda e: e.copy(fvt[:, 0:T], pl_ps[4][:]),
                              sem="a")
                    e4 = P.op("scalar", lambda e: e.copy(fvt[:, T:2 * T],
                                                         pl_ps[5][:]), sem="a")
                    P.wait("vector", [mm_tok[2], mm_tok[3], e3, e4])
                    P.wait("vector", tok_p_sp)
                    P.op("vector", lambda e: e.tensor_mul(
                        vt1[:], pl_ps[2][:], fvt[:, 0:T]))
                    P.sop("vector", lambda e: e.tensor_mul(
                        vt2[:], pl_ps[3][:], fvt[:, T:2 * T]), "v")
                    P.op("vector", lambda e, jc=jc: e.tensor_sub(
                        pb[:, jc * T:(jc + 1) * T], vt1[:], vt2[:]))
                    P.op("vector", lambda e: e.tensor_mul(
                        vt1[:], pl_ps[2][:], fvt[:, T:2 * T]))
                    tpv = P.sop("vector", lambda e: e.tensor_mul(
                        vt2[:], pl_ps[3][:], fvt[:, 0:T]), "v")
                    P.op("vector", lambda e, jc=jc: e.tensor_add(
                        pb[:, (4 + jc) * T:(5 + jc) * T], vt1[:], vt2[:]))
                    if jc == 0:
                        P.sop("vector", lambda e: e.tensor_mul(
                            vt1[0:1, :], pl_ps[3][0:1, :], fvt[0:1, T:2 * T]), "v")
                        P.op("vector", lambda e: e.tensor_add(
                            pb[0:1, 0:T], pb[0:1, 0:T], vt1[0:1, :]))
                        P.op("vector", lambda e: e.tensor_copy(
                            pb[0:1, 4 * T:5 * T], vt1[0:1, :]))
                    P.sop("vector", lambda e, jc=jc: e.tensor_reduce(
                        ttp[:], pb[:, jc * T:(jc + 1) * T], AX, ALU.add), "v")
                    P.sop("vector", lambda e, jc=jc: e.tensor_add(
                        totals[:, jc:jc + 1], totals[:, jc:jc + 1], ttp[:]), "v")
                    P.sop("vector", lambda e, jc=jc: e.tensor_reduce(
                        ttp[:], pb[:, (4 + jc) * T:(5 + jc) * T], AX, ALU.add), "v")
                    tlast = P.sop("vector", lambda e, jc=jc: e.tensor_add(
                        totals[:, 4 + jc:5 + jc], totals[:, 4 + jc:5 + jc],
                        ttp[:]), "v")
                    p_toks.append(tlast)
                    P.wait("tensor", [tpv, tlast, e1, e2])
                tok_proj_done = p_toks[-1]
                fq_ev = [e1, e2, P.settle("scalar", "a", duma[:])]
                p_toks = p_toks + [P.settle("vector", "v", dumv[:])]
                tok_proj_done = p_toks[-1]
                P.wait("gpsimd", p_toks)
                P.wait("gpsimd", fq_ev)
                tok_fq_sp = P.dma("gpsimd", fq_d[:, i * 8 * T:(i + 1) * 8 * T],
                                  fqb[:], "sp")
                tok_p_sp = P.dma("gpsimd", p_d[:, i * 8 * T:(i + 1) * 8 * T],
                                 pb[:], "sp")

        st_atab.close()
        P.quiesce()
        st_if = contextlib.ExitStack()
        ifa = alloc("ifa", [128, 4 * D], stack=st_if)
        ifb = alloc("ifb", [128, 4 * D], stack=st_if)
        with contextlib.ExitStack() as bar1:
            br1_ps = bar1.enter_context(nc.psum_tensor("br1_ps", [128, 256], F32))
            coef_t = alloc("coef_t", [128, 8], stack=bar1)
            t_mia = P.dma("sync", ifa[:], mia_in, "ld")
            t_mib = P.dma("sync", ifb[:], mib_in, "ld")
            t_ca = P.dma("sync", coef_t[:, 0:4], coefa_in, "ld")
            t_cb = P.dma("sync", coef_t[:, 4:8], coefb_in, "ld")
            P.wait("scalar", t_mia)
            t_ifa = P.op("scalar", lambda e: e.activation(
                ifa[:], ifa[:], AF.Sin, bias=0.0, scale=SIN_SCL), sem="a")
            P.wait("scalar", t_mib)
            t_ifb = P.op("scalar", lambda e: e.activation(
                ifb[:], ifb[:], AF.Sin, bias=0.0, scale=SIN_SCL), sem="a")
            P.wait("vector", [t_ifa, t_ca])
            for cc in range(4):
                P.op("vector", lambda e, cc=cc: e.tensor_scalar(
                    ifa[:, cc * D:(cc + 1) * D], ifa[:, cc * D:(cc + 1) * D],
                    coef_t[:, cc:cc + 1], None, ALU.mult))
            P.wait("vector", [t_ifb, t_cb])
            t_iftab = None
            for cc in range(4):
                t_iftab = P.op("vector", lambda e, cc=cc: e.tensor_scalar(
                    ifb[:, cc * D:(cc + 1) * D], ifb[:, cc * D:(cc + 1) * D],
                    coef_t[:, 4 + cc:5 + cc], None, ALU.mult), sem="v")

            P.wait("vector", [tok_proj_done, t_m1])
            P.sop("vector", lambda e: e.tensor_copy(agst[:, 0:8], totals[:]), "v")
            P.sop("vector", lambda e: e.tensor_copy(agst[:, 8:9], m1loc[:]), "v")
            P.sop("vector", lambda e: e.memset(agst[:, 9:10], 0.0), "v")
            t_stl = P.settle("vector", "v", dumv[:])
            P.wait("sync", t_stl)
            t_a2a = P.dma("sync", ag2_in.ap(), agst[:], "ld")
            P.wait("gpsimd", t_a2a)
            t_ag2 = P.op("gpsimd", lambda e: e.collective_compute(
                "AllGather", ALU.bypass, replica_groups=[list(range(NCORES))],
                ins=[ag2_in.ap()], outs=[ag2_out.ap()]), sem="cc")
            P.wait("sync", t_ag2)
            t_agb = P.dma("sync", agb[:].rearrange("p (r c) -> p r c", r=8),
                          ag2_out.ap().rearrange("(r p) c -> p r c", p=128), "ld")

            P.wait("vector", t_agb)
            toff = None
            for r in range(8):
                toff = P.sop("vector", lambda e, r=r: e.scalar_tensor_tensor(
                    offs[:], agb[:, 10 * r:10 * r + 8], mask8[:, r:r + 1], offs[:],
                    ALU.mult, ALU.add), "v")
            m1cands = agb[:].rearrange("p (r c) -> p c r", r=8)[:, 8:9, :]
            tmax = emit_allmax_bcast(m1cands, s1pp[:], t_agb, br1_ps)
            P.wait("vector", tmax)
            P.sop("vector", lambda e: e.tensor_scalar(
                s1pp[:], s1pp[:], EPS_Q, 1.0 / 255.0, ALU.max, ALU.mult), "v")
            t_s1 = P.sop("vector", lambda e: e.reciprocal(r1pp[:], s1pp[:]), "v")

            P.wait("vector", [t_sw1, t_b1, t_s1])
            P.sop("vector", lambda e: e.tensor_scalar(
                scale1[:], sw1t[:], s1pp[:, 0:1], None, ALU.mult), "v")
            P.sop("vector", lambda e: e.reciprocal(qb1[:], scale1[:]), "v")
            tyb = P.sop("vector", lambda e: e.tensor_mul(qb2[:], b1t[:], qb1[:]),
                        "v")
            tq1 = emit_round_wide(P, qb2[:], qb3[:], qb4[:], qb1[:], qb6[:],
                                  qb5[:], zbias, [tyb])
            P.wait("vector", tq1)
            P.sop("vector", lambda e: e.tensor_mul(qb3[:], qb3[:], scale1[:]), "v")
            P.sop("vector", lambda e: e.tensor_scalar(
                qb1[:], cs1t[:], 127.5, None, ALU.mult), "v")
            P.sop("vector", lambda e: e.tensor_mul(qb1[:], qb1[:], scale1[:]), "v")
            t_bias1 = P.sop("vector", lambda e: e.tensor_add(
                bias1[:], qb3[:], qb1[:]), "v")
            t_bias1 = P.settle("vector", "v", dumv[:])

        with contextlib.ExitStack() as b2:
            pb2 = [alloc("pb2_0", [128, 8 * T], stack=b2),
                   alloc("pb2_1", [128, 8 * T], stack=b2)]
            fb2 = [alloc("fb2_0", [128, 8 * T], stack=b2),
                   alloc("fb2_1", [128, 8 * T], stack=b2)]
            memb = [alloc("memb0", [128, 8 * T], stack=b2),
                    alloc("memb1", [128, 8 * T], stack=b2)]
            zb = alloc("zb", [128, 8 * T], stack=b2)
            sob = alloc("sob", [128, KD * T], stack=b2)
            zt0 = alloc("zt0", [128, T], stack=b2)
            vt1b = alloc("vt1b", [128, T], stack=b2)
            vt2b = alloc("vt2b", [128, T], stack=b2)
            if_ps = [b2.enter_context(nc.psum_tensor("ifp0", [128, 512], F32)),
                     b2.enter_context(nc.psum_tensor("ifp1", [128, 512], F32))]

            P.op("vector", lambda e: e.memset(zt0[:], 0.0))
            tok_pf_free = [None, None]
            tok_s_sp = None
            tok_z_done = None
            tok_zb_free = None
            for i in range(NT):
                sl = i % 2
                P.wait("sync", tok_pf_free[sl])
                t_p = P.dma("sync", pb2[sl][:], p_d[:, i * 8 * T:(i + 1) * 8 * T],
                            "ld")
                t_f = P.dma("sync", fb2[sl][:], fq_d[:, i * 8 * T:(i + 1) * 8 * T],
                            "ld")
                P.wait("vector", t_p)
                if i == 0:
                    P.wait("vector", toff)
                scan_toks = []
                for q in range(8):
                    if i == 0:
                        init = offs[:, q:q + 1]
                    else:
                        init = memb[1 - sl][:, q * T + T - 1:q * T + T]
                    tk = P.op("vector", lambda e, q=q, sl=sl, init=init: (
                        e.tensor_tensor_scan(
                            memb[sl][:, q * T:(q + 1) * T],
                            pb2[sl][:, q * T:(q + 1) * T], zt0[:],
                            init, ALU.add, ALU.add)), sem="v")
                    scan_toks.append(tk)
                P.wait("vector", t_f)
                P.wait("vector", tok_zb_free)
                z_toks = []
                for jc in range(4):
                    mre = memb[sl][:, jc * T:(jc + 1) * T]
                    mim = memb[sl][:, (4 + jc) * T:(5 + jc) * T]
                    qre = fb2[sl][:, jc * T:(jc + 1) * T]
                    qim = fb2[sl][:, (4 + jc) * T:(5 + jc) * T]
                    zre = zb[:, jc * T:(jc + 1) * T]
                    zim = zb[:, (4 + jc) * T:(5 + jc) * T]
                    P.op("vector", lambda e, a=mre, b=qre: e.tensor_mul(
                        vt1b[:], a, b))
                    P.sop("vector", lambda e, a=mim, b=qim: e.tensor_mul(
                        vt2b[:], a, b), "v")
                    P.op("vector", lambda e, o=zre: e.tensor_add(
                        o, vt1b[:], vt2b[:]))
                    P.op("vector", lambda e, a=mim, b=qre: e.tensor_mul(
                        vt1b[:], a, b))
                    P.sop("vector", lambda e, a=mre, b=qim: e.tensor_mul(
                        vt2b[:], a, b), "v")
                    tk = P.op("vector", lambda e, o=zim: e.tensor_sub(
                        o, vt1b[:], vt2b[:]), sem="v")
                    if jc == 0:
                        P.sop("vector", lambda e, a=mim, b=qim: e.tensor_mul(
                            vt1b[0:1, :], a[0:1, :], b[0:1, :]), "v")
                        P.op("vector", lambda e, o=zre: e.tensor_sub(
                            o[0:1, :], o[0:1, :], vt1b[0:1, :]))
                        tk = P.op("vector", lambda e, o=zim: e.tensor_copy(
                            o[0:1, :], vt1b[0:1, :]), sem="v")
                    z_toks.append(tk)
                tok_pf_free[sl] = z_toks[-1]
                tok_z_done = z_toks[-1]
                s_ev = []
                for k in range(KD):
                    ps = if_ps[k % 2]
                    P.wait("tensor", z_toks)
                    P.wait("tensor", t_iftab)
                    tmm = None
                    for q in range(8):
                        tbl = ifa if q < 4 else ifb
                        cc = q % 4
                        sem = "t" if q == 7 else None
                        tmm = P.op("tensor", lambda e, ps=ps, tbl=tbl, cc=cc, k=k,
                                   q=q: e.matmul(
                                       ps[:], tbl[:, cc * D + k * 128:
                                                  cc * D + (k + 1) * 128],
                                       zb[:, q * T:(q + 1) * T],
                                       start=(q == 0), stop=(q == 7)),
                                   sem=sem) or tmm
                    P.wait("scalar", tmm)
                    P.wait("scalar", tok_s_sp)
                    tev = P.op("scalar", lambda e, k=k, ps=ps: e.copy(
                        sob[:, k * T:(k + 1) * T], ps[:]), sem="a")
                    s_ev.append(tev)
                    P.wait("tensor", tev)
                tok_zb_free = tmm
                s_ev.append(P.settle("scalar", "a", duma[:]))
                P.wait("gpsimd", s_ev)
                tok_s_sp = P.dma("gpsimd", s_d[:, i * KD * T:(i + 1) * KD * T],
                                 sob[:], "sp")
            tok_b2_sp = tok_s_sp
        st_if.close()
        P.quiesce()

        with contextlib.ExitStack() as pc:
            qw1sb = alloc("qw1sb", [128, KD * FF], BF16, stack=pc)
            xc = [alloc("xc0", [128, KD * T], stack=pc),
                  alloc("xc1", [128, KD * T], stack=pc)]
            xq = alloc("xq", [128, KD * T], BF16, stack=pc)
            hb = alloc("hb", [128, KF * T], stack=pc)
            qbufs = tuple(alloc("qcb%d" % i, [128, 1024], stack=pc)
                          for i in range(6))
            ttp2 = alloc("ttp2", [128, 1], stack=pc)
            h_ps = [pc.enter_context(nc.psum_tensor("hp%d" % v_, [128, 512], F32))
                    for v_ in range(2)]

            t_qw1 = P.dma("sync", qw1sb[:], qw1t_d, "ld")
            tok_xc_free = [None, None]
            tok_h_sp = [None, None]
            tok_m2 = None
            tok_m2_prev = None
            tok_xq_free = None
            tq_prev = None
            for i in range(NT):
                sl = i % 2
                P.wait("sync", tok_xc_free[sl])
                t_xc = P.dma("sync", xc[sl][:],
                             xnt_d[:, i * KD * T:(i + 1) * KD * T], "ld")
                tq_toks = []
                for pr in range(2):
                    a0, a1 = 2 * pr * 1024, (2 * pr + 1) * 1024
                    tkA, tkB, tq_prev = emit_quant_pair(
                        P, [xc[sl][:, a0:a0 + 1024], xc[sl][:, a1:a1 + 1024]],
                        [r1pp[:, 0:1], r1pp[:, 0:1]], qbufs,
                        [xq[:, a0:a0 + 1024], xq[:, a1:a1 + 1024]],
                        [t_xc, t_s1, tq_prev, tok_xq_free], sbias_a)
                    tq_toks += [tkA, tkB]
                tok_xc_free[sl] = tq_toks[-1]
                if debug and i == 0:
                    P.wait("sync", tq_toks)
                    P.dma("sync", xq_dbg, xq[:], "ld")
                ev_toks = []
                for fj in range(KF):
                    ps = h_ps[fj % 2]
                    P.wait("tensor", tq_toks)
                    P.wait("tensor", t_qw1)
                    tmm = None
                    for k in range(KD):
                        sem = "t" if k == KD - 1 else None
                        tmm = P.op("tensor", lambda e, ps=ps, fj=fj, k=k: e.matmul(
                            ps[:], qw1sb[:, fj * 1024 + k * 128:fj * 1024 + (k + 1) * 128],
                            xq[:, k * T:(k + 1) * T],
                            start=(k == 0), stop=(k == KD - 1)), sem=sem) or tmm
                    P.wait("scalar", tmm)
                    P.wait("scalar", [t_bias1, tok_h_sp[0 if fj < 16 else 1],
                                      tok_m2_prev])
                    tev = P.op("scalar", lambda e, fj=fj, ps=ps: e.activation(
                        hb[:, fj * T:(fj + 1) * T], ps[:], AF.Relu,
                        bias=bias1[:, fj:fj + 1], scale=scale1[:, fj:fj + 1]),
                        sem="a")
                    ev_toks.append(tev)
                    P.wait("tensor", tev)
                    tok_xq_free = tmm
                P.wait("vector", ev_toks)
                P.sop("vector", lambda e: e.tensor_reduce(
                    ttp2[:], hb[:], AX, ALU.max), "v")
                tok_m2 = P.sop("vector", lambda e: e.tensor_max(
                    m2loc[:], m2loc[:], ttp2[:]), "v")
                tok_m2_prev = tok_m2
                t_set_c = P.settle("scalar", "a", duma[:])
                for hf in range(2):
                    P.wait("gpsimd", [ev_toks[hf * 16 + 15], t_set_c])
                    tok_h_sp[hf] = P.dma(
                        "gpsimd",
                        h_d[:, i * KF * T + hf * 16 * T:
                            i * KF * T + (hf + 1) * 16 * T],
                        hb[:, hf * 16 * T:(hf + 1) * 16 * T], "sp")

        P.quiesce()
        with contextlib.ExitStack() as bar2:
            br2_ps = bar2.enter_context(nc.psum_tensor("br2_ps", [128, 256], F32))
            t_stl2 = P.settle("vector", "v", dumv[:])
            P.wait("sync", [tok_m2, t_stl2])
            t_a3 = P.dma("sync", ag3_in.ap(), m2loc[:], "ld")
            P.wait("gpsimd", t_a3)
            t_ag3 = P.op("gpsimd", lambda e: e.collective_compute(
                "AllGather", ALU.bypass, replica_groups=[list(range(NCORES))],
                ins=[ag3_in.ap()], outs=[ag3_out.ap()]), sem="cc")
            P.wait("sync", t_ag3)
            t_agb3 = P.dma("sync", agb[:, 0:8].rearrange("p (r c) -> p r c", r=8),
                           ag3_out.ap().rearrange("(r p) c -> p r c", p=128), "ld",
                           slow=True)
            tmax2 = emit_allmax_bcast(agb[:, 0:8], s2pp[:], t_agb3, br2_ps)
            P.wait("vector", tmax2)
            P.sop("vector", lambda e: e.tensor_scalar(
                s2pp[:], s2pp[:], EPS_Q, 1.0 / 255.0, ALU.max, ALU.mult), "v")
            t_s2 = P.sop("vector", lambda e: e.reciprocal(r2pp[:], s2pp[:]), "v")
            P.wait("vector", [t_sw2, t_b2, t_s2])
            P.sop("vector", lambda e: e.tensor_scalar(
                scale2[:], sw2t[:], s2pp[:, 0:1], None, ALU.mult), "v")
            P.sop("vector", lambda e: e.reciprocal(qb1[:, 0:KD], scale2[:]), "v")
            tyb2 = P.sop("vector", lambda e: e.tensor_mul(
                qb2[:, 0:KD], b2t[:], qb1[:, 0:KD]), "v")
            tq2r = emit_round_wide(P, qb2[:, 0:KD], qb3[:, 0:KD], qb4[:, 0:KD],
                                   qb1[:, 0:KD], qb6[:, 0:KD], qb5[:, 0:KD],
                                   zbias, [tyb2])
            P.wait("vector", tq2r)
            P.sop("vector", lambda e: e.tensor_mul(
                qb3[:, 0:KD], qb3[:, 0:KD], scale2[:]), "v")
            P.sop("vector", lambda e: e.tensor_scalar(
                qb1[:, 0:KD], cs2t[:], 127.5, None, ALU.mult), "v")
            P.sop("vector", lambda e: e.tensor_mul(
                qb1[:, 0:KD], qb1[:, 0:KD], scale2[:]), "v")
            t_bias2 = P.sop("vector", lambda e: e.tensor_add(
                bias2[:], qb3[:, 0:KD], qb1[:, 0:KD]), "v")
            t_bias2 = P.settle("vector", "v", dumv[:])

        with contextlib.ExitStack() as pd:
            qw2sb = alloc("qw2sb", [128, KF * D], BF16, stack=pd)
            hD = alloc("hd0", [128, KF * TD], stack=pd)
            qh = alloc("qh", [128, KF * TD], BF16, stack=pd)
            gb = alloc("gb", [128, KD * TD], stack=pd)
            sD = [alloc("sd0", [128, KD * TD], stack=pd),
                  alloc("sd1", [128, KD * TD], stack=pd)]
            rb = alloc("rb", [128, KD * TD], stack=pd)
            xD = [alloc("xd0", [128, 2 * D], stack=pd),
                  alloc("xd1", [128, 2 * D], stack=pd)]
            ob = alloc("ob", [128, 2 * D], stack=pd)
            dbufs = tuple(alloc("qdb%d" % i, [128, 1024], stack=pd)
                          for i in range(6))
            y2_ps = [pd.enter_context(nc.psum_tensor("y2p%d" % v_, [128, 256], F32))
                     for v_ in range(2)]
            rt_ps = [pd.enter_context(nc.psum_tensor("rtp%d" % v_, [128, 1024], F32))
                     for v_ in range(2)]
            scrap_ps = pd.enter_context(nc.psum_tensor("scrap_ps", [128, 128], F32))

            t_qw2 = P.dma("sync", qw2sb[:], qw2t_d, "ld")
            if debug:
                P.wait("vector", t_bias2)
                dcols = [(s1pp, 0), (r1pp, 1), (s2pp, 2), (r2pp, 3), (m1loc, 4),
                         (m2loc, 5)]
                for t_, j in dcols:
                    P.sop("vector", lambda e, t_=t_, j=j: e.tensor_copy(
                        diag_dbg_sb[:, j:j + 1], t_[:]), "v")
                for t_, j0, w in [(scale2, 8, KD), (bias2, 16, KD),
                                  (scale1, 24, KF), (bias1, 56, KF),
                                  (sw2t, 88, KD), (cs2t, 96, KD), (b2t, 104, KD)]:
                    P.sop("vector", lambda e, t_=t_, j0=j0, w=w: e.tensor_copy(
                        diag_dbg_sb[:, j0:j0 + w], t_[:]), "v")
                tdg = P.op("vector", lambda e: e.tensor_copy(
                    diag_dbg_sb[:, 120:121], eps_ln[:]), sem="v")
                P.wait("sync", tdg)
                P.dma("sync", diag_dbg, diag_dbg_sb[:], "ld")
                P.dma("sync", qw1_dbg, qw1t_d, "ld")
                P.dma("sync", qw2_dbg, qw2t_d, "ld")
            tok_hD_free = [None, None]
            tok_sD_free = [None, None]
            tok_xD_free = [None, None]
            tok_ob_free = None
            tok_qh_free = None
            tok_rb_free = None
            tok_gb_free = None
            dq_prev = None
            t_out_last = None
            for t in range(NTD):
                sl = t % 2
                ci, hf = t // 2, t % 2
                t_hh = []
                for hh in range(2):
                    P.wait("sync", tok_hD_free[hh])
                    tk = P.dma(
                        "sync",
                        hD[:, hh * 16 * TD:(hh + 1) * 16 * TD].rearrange(
                            "p (o f w) -> p o f w", o=1, f=16),
                        h_d.rearrange("p (i f w) -> p i f w", i=NT, f=KF)[
                            :, ci:ci + 1, hh * 16:(hh + 1) * 16,
                            hf * TD:(hf + 1) * TD], "ld")
                    t_hh.append(tk)
                P.wait("sync", tok_sD_free[sl])
                t_s = P.dma(
                    "sync", sD[sl][:].rearrange("p (o k w) -> p o k w", o=1, k=KD),
                    s_d.rearrange("p (i k w) -> p i k w", i=NT, k=KD)[
                        :, ci:ci + 1, :, hf * TD:(hf + 1) * TD], "ld")
                P.wait("sync", tok_xD_free[sl])
                t_x2 = P.dma("sync", xD[sl][:].rearrange("p (c d) -> p c d", c=2),
                             x_in[t * TD:(t + 1) * TD, :].rearrange(
                                 "(c p) d -> p c d", p=128), "ld")
                tq_toks = []
                for pr in range(4):
                    a0, a1 = 2 * pr * 1024, (2 * pr + 1) * 1024
                    tkA, tkB, dq_prev = emit_quant_pair(
                        P, [hD[:, a0:a0 + 1024], hD[:, a1:a1 + 1024]],
                        [r2pp[:, 0:1], r2pp[:, 0:1]], dbufs,
                        [qh[:, a0:a0 + 1024], qh[:, a1:a1 + 1024]],
                        [t_hh[pr // 2], t_s2, dq_prev, tok_qh_free], sbias_a)
                    tq_toks += [tkA, tkB]
                    if pr == 1:
                        tok_hD_free[0] = tkB
                tok_hD_free[1] = tq_toks[-1]
                g_toks = []
                for k in range(KD):
                    ps = y2_ps[k % 2]
                    P.wait("tensor", tq_toks)
                    P.wait("tensor", t_qw2)
                    tmm = None
                    for fj in range(KF):
                        sem = "t" if fj == KF - 1 else None
                        col = (k * 4 + fj // 8) * 1024 + (fj % 8) * 128
                        tmm = P.op("tensor", lambda e, ps=ps, fj=fj, c0=col: e.matmul(
                            ps[:], qw2sb[:, c0:c0 + 128],
                            qh[:, fj * TD:(fj + 1) * TD],
                            start=(fj == 0), stop=(fj == KF - 1)), sem=sem) or tmm
                    P.wait("scalar", [tmm, t_bias2, tok_gb_free])
                    tev = P.op("scalar", lambda e, k=k, ps=ps: e.activation(
                        gb[:, k * TD:(k + 1) * TD], ps[:], AF.Silu,
                        bias=bias2[:, k:k + 1], scale=scale2[:, k:k + 1]), sem="a")
                    g_toks.append(tev)
                    P.wait("tensor", tev)
                tok_qh_free = tmm
                if debug and t == 0:
                    P.wait("sync", tq_toks)
                    P.dma("sync", qh_dbg, qh[:], "ld")
                    P.wait("sync", g_toks)
                    P.dma("sync", gb_dbg, gb[:], "ld")
                P.wait("vector", [t_s, tok_rb_free] + g_toks)
                t_r = P.op("vector", lambda e, sl=sl: e.tensor_mul(
                    rb[:], sD[sl][:], gb[:]), sem="v")
                tok_sD_free[sl] = t_r
                tok_gb_free = t_r
                t_ob = []
                for tc in range(2):
                    ps = rt_ps[tc]
                    P.wait("tensor", t_r)
                    ttr = None
                    for k in range(KD):
                        sem = "t" if k == KD - 1 else None
                        ttr = P.op("tensor", lambda e, ps=ps, k=k, tc=tc: e.transpose(
                            ps[:, k * 128:(k + 1) * 128],
                            rb[:, k * TD + tc * 128:k * TD + (tc + 1) * 128],
                            ident[:]), sem=sem) or ttr
                    P.wait("vector", [ttr, t_x2, tok_ob_free])
                    tado = P.op("vector", lambda e, tc=tc, sl=sl, ps=ps: e.tensor_add(
                        ob[:, tc * D:(tc + 1) * D], xD[sl][:, tc * D:(tc + 1) * D],
                        ps[:]), sem="v")
                    t_ob.append(tado)
                    P.wait("tensor", tado)
                if debug:
                    P.wait("sync", tq_toks)
                    P.dma("sync", qhfull_dbg[:, t * KF * TD:(t + 1) * KF * TD],
                          qh[:], "ld")
                    P.wait("sync", g_toks)
                    P.dma("sync", gfull_dbg[:, t * KD * TD:(t + 1) * KD * TD],
                          gb[:], "ld")
                if debug and t == 0:
                    P.wait("sync", [t_r] + t_ob)
                    P.dma("sync", rb_dbg, rb[:], "ld")
                    P.dma("sync", ob_dbg, ob[:], "ld")
                    P.dma("sync", sd_dbg, sD[sl][:], "ld")
                tok_xD_free[sl] = t_ob[-1]
                tok_rb_free = ttr
                t_ob = t_ob + [P.settle("vector", "v", dumv[:])]
                P.wait("sync", t_ob)
                t_out_last = P.dma(
                    "sync", out_d[t * TD:(t + 1) * TD, :].rearrange(
                        "(c p) d -> p c d", p=128),
                    ob[:].rearrange("p (c d) -> p c d", c=2), "st")
                tok_ob_free = t_out_last

            P.wait("sync", t_out_last)

        block = es.enter_context(nc.Block())
        P.replay(block)

    return nc


def _make_in_maps(L0_attn_x, L0_attn_Wq, L0_attn_Wk, L0_attn_Wv,
                  L0_ffn_w1, L0_ffn_b1, L0_ffn_w2, L0_ffn_b2):
    x = np.ascontiguousarray(np.asarray(L0_attn_x, np.float32)).reshape(32768, D)
    wq = np.ascontiguousarray(np.asarray(L0_attn_Wq, np.float32))
    wk = np.ascontiguousarray(np.asarray(L0_attn_Wk, np.float32))
    wv = np.ascontiguousarray(np.asarray(L0_attn_Wv, np.float32))
    w1 = np.ascontiguousarray(np.asarray(L0_ffn_w1, np.float32))
    b1 = np.ascontiguousarray(np.asarray(L0_ffn_b1, np.float32))
    w2 = np.ascontiguousarray(np.asarray(L0_ffn_w2, np.float32))
    b2 = np.ascontiguousarray(np.asarray(L0_ffn_b2, np.float32))
    cst = _host_consts()
    b1t = np.ascontiguousarray(b1.reshape(KF, 128).T)
    b2t = np.ascontiguousarray(b2.reshape(KD, 128).T)

    in_maps = []
    for c in range(NCORES):
        mask = np.zeros((128, 8), np.float32)
        if c % 2 == 1:
            mask[:, c - 1] = 1.0
        in_maps.append({
            "x": np.ascontiguousarray(x[c * NTOK:(c + 1) * NTOK]),
            "wq_my": np.ascontiguousarray(wq[:, 128 * c:128 * (c + 1)]),
            "wk_my": np.ascontiguousarray(wk[:, 128 * c:128 * (c + 1)]),
            "wv_my": np.ascontiguousarray(wv[:, 128 * c:128 * (c + 1)]),
            "w1_my": np.ascontiguousarray(w1[512 * c:512 * (c + 1)]),
            "w2_my": np.ascontiguousarray(w2[128 * c:128 * (c + 1)]),
            "b1t": b1t, "b2t": b2t,
            "ident": cst["ident"], "mcc": cst["mcc"], "mbb": cst["mbb"],
            "mia": cst["mia"], "mib": cst["mib"],
            "coefa": cst["coefa"], "coefb": cst["coefb"],
            "mask8": mask,
        })
    return in_maps


def kernel(**inputs):
    nc = build_program()
    in_maps = _make_in_maps(**inputs)
    res = run_bass_kernel_spmd(nc, in_maps, core_ids=list(range(NCORES)))
    out = np.concatenate([r["out"] for r in res.results], axis=0)
    return out.reshape(4, 8192, 1024)


# version 14 (final)
# speedup vs baseline: 1.0509x; 1.0386x over previous
import contextlib

import numpy as np
import ml_dtypes
import concourse.bass as bass
import concourse.mybir as mybir
from concourse.bass_utils import run_bass_kernel_spmd

F32 = mybir.dt.float32
BF16 = mybir.dt.bfloat16
ALU = mybir.AluOpType
AF = mybir.ActivationFunctionType
AX = mybir.AxisListType.X

NCORES = 8
D = 1024
FF = 4096
NTOK = 4096
T = 512
NT = NTOK // T
TD = 256
NTD = NTOK // TD
KD = D // 128
KF = FF // 128
C2 = 512
EPS_LN = 1e-5
EPS_Q = 1e-8
SIN_SCL = float(2.0 * np.pi / 1024.0)

ENGINES = ("sync", "tensor", "vector", "scalar", "gpsimd")


def _ctr(m):
    m = np.asarray(m, np.int64) % 1024
    return (m - 1024 * (m >= 512)).astype(np.float32)


def _rows_to_chunks(a, p=128):
    r, w = a.shape
    return np.ascontiguousarray(
        a.reshape(r // p, p, w).transpose(1, 0, 2).reshape(p, (r // p) * w))


_CACHE = {}


def _host_consts():
    if _CACHE:
        return _CACHE
    e = np.arange(1024)[:, None]
    c = np.arange(C2)[None, :]
    mcc = _ctr(e * c + 256)
    mbb = _ctr(-e * c)
    mbb[:, 0] = _ctr(256 + 512 * e)[:, 0]
    ci = np.arange(C2)[:, None]
    dd = np.arange(D)[None, :]
    mia = _ctr(ci * dd + 256)
    mib = _ctr(-ci * dd)
    mib[0, :] = _ctr(256 + 512 * dd)[0, :]
    coefa = np.full((C2, 1), 2.0 / 1024.0, np.float32)
    coefa[0, 0] = 1.0 / 1024.0
    coefb = coefa.copy()
    _CACHE.update(
        ident=np.eye(128, dtype=np.float32),
        mcc=_rows_to_chunks(mcc), mbb=_rows_to_chunks(mbb),
        mia=_rows_to_chunks(mia), mib=_rows_to_chunks(mib),
        coefa=_rows_to_chunks(coefa), coefb=_rows_to_chunks(coefb),
    )
    return _CACHE


SEM_OWNER = {"v": "vector", "a": "scalar", "g": "gpsimd"}


class Prog:
    def __init__(self):
        self.q = {e: [] for e in ENGINES}
        self.sem = {}
        self.cnt = {}
        self.mark = {e: {} for e in ENGINES}
        self.scratch = {}
        self.settled = {}
        self.dma_rr = {}

    def add_sem(self, name, handle):
        self.sem[name] = handle
        self.cnt[name] = 0

    def wait(self, eng, tok, raw=False):
        if tok is None:
            return
        if isinstance(tok, list):
            for t in tok:
                self.wait(eng, t, raw=raw)
            return
        name, val = tok
        owner = SEM_OWNER.get(name)
        if not raw and owner is not None and owner != eng:
            st = self.settled.get(name)
            if st is None or st[1] < val:
                cover = self.cnt[name]
                stok = self.sop(owner, self._mk_settle(owner), name)
                self.settled[name] = (stok, cover)
                tok = stok
            else:
                tok = st[0]
            name, val = tok
        if self.mark[eng].get(name, 0) >= val:
            return
        self.mark[eng][name] = val
        h = self.sem[name]
        self.q[eng].append(lambda e, h=h, v=val: e.wait_ge(h, v))

    def _mk_settle(self, owner):
        a = self.scratch[owner]
        if owner == "scalar":
            return lambda e, a=a: e.copy(a, a)
        return lambda e, a=a: e.tensor_copy(a, a)

    def op(self, eng, build, sem=None, n=1):
        if sem is None:
            self.q[eng].append(build)
            return None
        self.cnt[sem] += n
        tok = (sem, self.cnt[sem])
        h = self.sem[sem]
        self.q[eng].append(lambda e, b=build, h=h, n=n: b(e).then_inc(h, n))
        return tok

    def settle(self, eng, sem, scratch_ap):
        if eng == "scalar":
            return self.sop(eng, lambda e, a=scratch_ap: e.copy(a, a), sem)
        return self.sop(eng, lambda e, a=scratch_ap: e.tensor_copy(a, a), sem)

    def quiesce(self):
        for s, owner in SEM_OWNER.items():
            if self.cnt.get(s, 0) > 0:
                self.sop(owner, self._mk_settle(owner), s)
        snap = [(s, c) for s, c in self.cnt.items() if c > 0]
        for eng in ENGINES:
            for s, c in snap:
                self.wait(eng, (s, c), raw=True)

    def sop(self, eng, build, sem):
        tok = self.op(eng, build, sem=sem)
        self.wait(eng, tok, raw=True)
        return tok

    def dma(self, eng, out_ap, in_ap, sem, slow=False):
        self.dma_rr[sem] = self.dma_rr.get(sem, 0) + 1
        name = "%s%d" % (sem, self.dma_rr[sem] % 8)
        return self.op(eng, lambda e, o=out_ap, i=in_ap, s=slow: e.dma_start(
            out=o, in_=i, allow_slow_non_contiguous=s), sem=name, n=16)

    def replay(self, block):
        for eng in ENGINES:
            queue = self.q[eng]

            def run(e, queue=queue):
                for fn in queue:
                    fn(e)
            getattr(block, eng)(run)


def emit_quant_pair(P, srcs, r_pps, bufs, outs, deps, sbias_col, *,
                    weight_mode=False):
    yA, fA, gA, yB, fB, gB = [b[:] for b in bufs]
    lo = -128.0 if weight_mode else 0.0
    hi = 127.0 if weight_mode else 255.0
    P.wait("vector", deps)
    ty = {}
    for s, (src_ap, r_pp, y, f) in enumerate(
            ((srcs[0], r_pps[0], yA, fA), (srcs[1], r_pps[1], yB, fB))):
        P.op("vector", lambda e, y=y, src_ap=src_ap, r=r_pp: e.tensor_scalar(
            y, src_ap, r, lo, ALU.mult, ALU.max))
        P.op("vector", lambda e, y=y: e.tensor_single_scalar(y, y, hi, ALU.min))
        ty[s] = P.op("vector", lambda e, f=f, y=y: e.tensor_copy(f, y), sem="v")
    va = {0: ty[0], 1: ty[1]}
    ta = {}
    for i, k in enumerate(range(7, -1, -1)):
        c = -float(2 ** (k - 1))
        for s, (f, g) in enumerate(((fA, gA), (fB, gB))):
            P.wait("scalar", va[s], raw=True)
            ta[s] = P.op("scalar", lambda e, f=f, g=g: e.activation(
                g, f, AF.Sign, bias=sbias_col[:, 0:1], scale=1.0), sem="a")
        for s, (f, g) in enumerate(((fA, gA), (fB, gB))):
            P.wait("vector", ta[s], raw=True)
            va[s] = P.op("vector", lambda e, f=f, g=g, c=c: e.scalar_tensor_tensor(
                f, g, c, f, ALU.mult, ALU.add), sem="v")
    touts = []
    for s, (y, f, out_ap) in enumerate(((yA, fA, outs[0]), (yB, fB, outs[1]))):
        P.wait("vector", va[s], raw=True)
        if weight_mode:
            tk = P.op("vector", lambda e, y=y, f=f, o=out_ap: e.scalar_tensor_tensor(
                o, y, -0.5, f, ALU.add, ALU.subtract), sem="v")
        else:
            tk = P.op("vector", lambda e, y=y, f=f, o=out_ap: e.tensor_sub(
                o, y, f), sem="v")
        touts.append(tk)
    return touts[0], touts[1], ta[1]


def emit_round_wide(P, y_ap, q_ap, ab_ap, t1_ap, t2_ap, sg_ap, zbias, deps):
    P.wait("scalar", deps)
    tsg = P.op("scalar", lambda e: e.activation(sg_ap, y_ap, AF.Sign,
                                                bias=zbias[:, 0:1], scale=1.0),
               sem="a")
    tab = P.op("scalar", lambda e: e.activation(ab_ap, y_ap, AF.Abs,
                                                bias=zbias[:, 0:1], scale=1.0),
               sem="a")
    P.wait("vector", [tab, tsg])
    P.sop("vector", lambda e: e.tensor_scalar_add(t1_ap, ab_ap, 0.5), "v")
    for k in range(39, -1, -1):
        c = float(2 ** k)
        P.sop("vector", lambda e, c=c: e.tensor_scalar(t2_ap, t1_ap, c, -c,
                                                       ALU.is_ge, ALU.mult), "v")
        P.sop("vector", lambda e: e.tensor_add(t1_ap, t1_ap, t2_ap), "v")
    P.sop("vector", lambda e: e.scalar_tensor_tensor(t2_ap, ab_ap, 0.5, t1_ap,
                                                     ALU.add, ALU.subtract), "v")
    return P.sop("vector", lambda e: e.tensor_mul(q_ap, t2_ap, sg_ap), "v")


def build_program(debug=False):
    nc = bass.Bass("TRN2", target_bir_lowering=False)

    def din(name, shape, dtype=F32):
        return nc.dram_tensor(name, shape, dtype, kind="ExternalInput").ap()

    x_in = din("x", [NTOK, D])
    wq_my = din("wq_my", [D, 128])
    wk_my = din("wk_my", [D, 128])
    wv_my = din("wv_my", [D, 128])
    w1_my = din("w1_my", [512, D])
    w2_my = din("w2_my", [128, FF])
    b1_in = din("b1t", [128, KF])
    b2_in = din("b2t", [128, KD])
    ident_in = din("ident", [128, 128])
    mcc_in = din("mcc", [128, KD * C2])
    mbb_in = din("mbb", [128, KD * C2])
    mia_in = din("mia", [128, 4 * D])
    mib_in = din("mib", [128, 4 * D])
    coefa_in = din("coefa", [128, 4])
    coefb_in = din("coefb", [128, 4])
    mask_in = din("mask8", [128, 8])

    out_d = nc.dram_tensor("out", [NTOK, D], F32, kind="ExternalOutput").ap()

    kind_dbg = "ExternalOutput" if debug else "Internal"
    xnt_d = nc.dram_tensor("xnt_sp", [128, NT * KD * T], F32, kind=kind_dbg).ap()
    p_d = nc.dram_tensor("p_sp", [128, NT * 8 * T], F32, kind=kind_dbg).ap()
    fq_d = nc.dram_tensor("fq_sp", [128, NT * 8 * T], F32, kind=kind_dbg).ap()
    s_d = nc.dram_tensor("s_sp", [128, NT * KD * T], F32, kind=kind_dbg).ap()
    h_d = nc.dram_tensor("h_sp", [128, KF * NTOK], F32, kind=kind_dbg).ap()
    xn_dbg = nc.dram_tensor("xn_dbg", [128, 4 * D], F32, kind=kind_dbg).ap()
    cc_dbg = nc.dram_tensor("cc_dbg", [128, KD * C2], F32, kind=kind_dbg).ap()
    tab_dbg = nc.dram_tensor("tab_dbg", [128, KD * C2], F32, kind=kind_dbg).ap()
    agi_dbg = nc.dram_tensor("agi_dbg", [128, 6 * C2 + 5], F32, kind=kind_dbg).ap()
    diag_dbg = nc.dram_tensor("diag_dbg", [128, 128], F32, kind=kind_dbg).ap()
    qw1_dbg = nc.dram_tensor("qw1_dbg", [128, KD * FF], BF16, kind=kind_dbg).ap()
    xq_dbg = nc.dram_tensor("xq_dbg", [128, KD * T], BF16, kind=kind_dbg).ap()
    qh_dbg = nc.dram_tensor("qh_dbg", [128, KF * TD], BF16, kind=kind_dbg).ap()
    qw2_dbg = nc.dram_tensor("qw2_dbg", [128, KF * D], BF16, kind=kind_dbg).ap()
    gb_dbg = nc.dram_tensor("gb_dbg", [128, KD * TD], F32, kind=kind_dbg).ap()
    rb_dbg = nc.dram_tensor("rb_dbg", [128, KD * TD], F32, kind=kind_dbg).ap()
    gfull_dbg = nc.dram_tensor("gfull_dbg", [128, KD * NTOK], F32, kind=kind_dbg).ap()
    qhfull_dbg = nc.dram_tensor("qhfull_dbg", [128, KF * NTOK], BF16, kind=kind_dbg).ap()
    ob_dbg = nc.dram_tensor("ob_dbg", [128, 2 * D], F32, kind=kind_dbg).ap()
    sd_dbg = nc.dram_tensor("sd_dbg", [128, KD * TD], F32, kind=kind_dbg).ap()
    id_dbg = nc.dram_tensor("id_dbg", [128, 128], F32, kind=kind_dbg).ap()
    qw1t_d = nc.dram_tensor("qw1t_d", [128, KD * FF], BF16).ap()
    qw2t_d = nc.dram_tensor("qw2t_d", [128, KF * D], BF16).ap()

    ag1_in = nc.dram_tensor("ag1_in", [128, 6 * C2 + 5], F32)
    ag1_out = nc.dram_tensor("ag1_out", [1024, 6 * C2 + 5], F32)
    agw1_in = nc.dram_tensor("agw1_in", [512, D], F32)
    agw1_out = nc.dram_tensor("agw1_out", [FF, D], F32)
    agw2_in = nc.dram_tensor("agw2_in", [128, FF], F32)
    agw2_out = nc.dram_tensor("agw2_out", [D, FF], F32)
    ag2_in = nc.dram_tensor("ag2_in", [128, 10], F32)
    ag2_out = nc.dram_tensor("ag2_out", [1024, 10], F32)
    ag3_in = nc.dram_tensor("ag3_in", [128, 1], F32)
    ag3_out = nc.dram_tensor("ag3_out", [1024, 1], F32)

    P = Prog()
    es = contextlib.ExitStack()
    with es:
        for s in ("ld", "sp", "cc", "t", "v", "a", "g", "st"):
            P.add_sem(s, es.enter_context(nc.semaphore(s)))
        for q in ("ld", "sp", "st"):
            for i in range(8):
                nm = "%s%d" % (q, i)
                P.add_sem(nm, es.enter_context(nc.semaphore(nm)))

        sb = {}

        def alloc(name, shape, dtype=F32, stack=es):
            t = stack.enter_context(nc.sbuf_tensor("sb_" + name, shape, dtype))
            sb[name] = t
            return t

        ident = alloc("ident", [128, 128])
        identb = alloc("identb", [128, 128], BF16)
        ones128 = alloc("ones128", [128, 128])
        sbias_a = alloc("sbias_a", [128, 1])
        sbias_w = alloc("sbias_w", [128, 1])
        zbias = alloc("zbias", [128, 1])
        mask8 = alloc("mask8", [128, 8])
        m1loc = alloc("m1loc", [128, 1])
        m2loc = alloc("m2loc", [128, 1])
        totals = alloc("totals", [128, 8])
        offs = alloc("offs", [128, 8])
        s1pp = alloc("s1pp", [128, 1])
        r1pp = alloc("r1pp", [128, 1])
        s2pp = alloc("s2pp", [128, 1])
        r2pp = alloc("r2pp", [128, 1])
        sw1t = alloc("sw1t", [128, KF])
        sw2t = alloc("sw2t", [128, KD])
        cs1t = alloc("cs1t", [128, KF])
        cs2t = alloc("cs2t", [128, KD])
        scale1 = alloc("scale1", [128, KF])
        bias1 = alloc("bias1", [128, KF])
        scale2 = alloc("scale2", [128, KD])
        bias2 = alloc("bias2", [128, KD])
        b1t = alloc("b1t", [128, KF])
        b2t = alloc("b2t", [128, KD])
        eps_ln = alloc("eps_ln", [128, 1])
        zcol = alloc("zcol", [128, 1])
        red1 = alloc("red1", [128, 1])
        redA = alloc("redA", [128, 1])
        agb = alloc("agb", [128, 80])
        qb1 = alloc("qb1", [128, KF])
        qb2 = alloc("qb2", [128, KF])
        qb3 = alloc("qb3", [128, KF])
        qb4 = alloc("qb4", [128, KF])
        qb5 = alloc("qb5", [128, KF])
        qb6 = alloc("qb6", [128, KF])
        diag_dbg_sb = alloc("diag_sb", [128, 128])
        dumv = alloc("dumv", [128, 1])
        dumg = alloc("dumg", [128, 1])
        agst = alloc("agst", [128, 10])
        duma = alloc("duma", [128, 1])

        t_id = P.dma("sync", ident[:], ident_in, "ld")
        t_mk = P.dma("sync", mask8[:], mask_in, "ld")
        t_b1 = P.dma("sync", b1t[:], b1_in, "ld")
        t_b2 = P.dma("sync", b2t[:], b2_in, "ld")
        P.wait("vector", t_id)
        P.op("vector", lambda e: e.tensor_copy(identb[:], ident[:]))
        P.op("vector", lambda e: e.memset(ones128[:], 1.0))
        P.op("vector", lambda e: e.memset(eps_ln[:], EPS_LN))
        P.op("vector", lambda e: e.memset(m1loc[:], -1e30))
        P.op("vector", lambda e: e.memset(m2loc[:], 0.0))
        P.op("vector", lambda e: e.memset(totals[:], 0.0))
        P.op("vector", lambda e: e.memset(offs[:], 0.0))
        P.op("vector", lambda e: e.memset(dumv[:], 0.0))
        P.op("vector", lambda e: e.memset(duma[:], 0.0))
        P.op("vector", lambda e: e.memset(dumg[:], 0.0))
        P.scratch = {"vector": dumv[:], "scalar": duma[:], "gpsimd": dumg[:]}
        P.op("vector", lambda e: e.memset(sbias_a[:], -127.5))
        P.op("vector", lambda e: e.memset(sbias_w[:], 0.5))
        P.op("vector", lambda e: e.memset(zbias[:], 0.0))
        t_init = P.op("vector", lambda e: e.tensor_copy(zcol[:], eps_ln[:]), sem="v")

        def emit_allmax_bcast(src8_ap, dst_pp, dep, br_ps):
            P.wait("vector", dep)
            tr = P.sop("vector", lambda e: e.tensor_reduce(
                red1[:], src8_ap, AX, ALU.max), "v")
            P.wait("tensor", tr)
            tt = P.op("tensor", lambda e: e.transpose(
                br_ps[0:1, 0:128], red1[:], ident[:]), sem="t")
            P.wait("vector", tt)
            tr2 = P.sop("vector", lambda e: e.tensor_reduce(
                redA[0:1, 0:1], br_ps[0:1, 0:128], AX, ALU.max), "v")
            P.sop("vector", lambda e: e.memset(zcol[:], 0.0), "v")
            tz = P.sop("vector", lambda e: e.tensor_copy(
                zcol[0:1, 0:1], redA[0:1, 0:1]), "v")
            tz = P.settle("vector", "v", dumv[:])
            P.wait("tensor", tz)
            tb = P.op("tensor", lambda e: e.matmul(
                br_ps[:, 0:1], ones128[:], zcol[:], start=True, stop=True), sem="t")
            P.wait("vector", tb)
            P.sop("vector", lambda e: e.tensor_copy(dst_pp, br_ps[:, 0:1]), "v")
            return P.settle("vector", "v", dumv[:])

        st_atab = contextlib.ExitStack()
        atabs = [alloc("tab" + str(i), [128, KD * C2], stack=st_atab)
                 for i in range(6)]

        with contextlib.ExitStack() as st1:
            wsh1 = alloc("wsh1", [128, 4 * D], stack=st1)
            wsh2 = alloc("wsh2", [128, FF], stack=st1)
            swsh1 = alloc("swsh1", [128, 4], stack=st1)
            rwsh1 = alloc("rwsh1", [128, 4], stack=st1)
            swsh2 = alloc("swsh2", [128, 1], stack=st1)
            rwsh2 = alloc("rwsh2", [128, 1], stack=st1)
            st1a = contextlib.ExitStack()
            cc_t = alloc("cc_t", [128, KD * C2], stack=st1a)
            bb_t = alloc("bb_t", [128, KD * C2], stack=st1a)
            wmy = alloc("wmy", [128, KD * 128], stack=st1a)
            agin = alloc("agin", [128, 6 * C2 + 5], stack=st1a)
            comb_ps = st1a.enter_context(nc.psum_tensor("comb_ps", [128, C2], F32))

            t_mcc = P.dma("sync", cc_t[:], mcc_in, "ld")
            t_mbb = P.dma("sync", bb_t[:], mbb_in, "ld")
            P.wait("scalar", t_mcc)
            t_cc = P.op("scalar", lambda e: e.activation(
                cc_t[:], cc_t[:], AF.Sin, bias=0.0, scale=SIN_SCL), sem="a")
            P.wait("scalar", t_mbb)
            t_bb = P.op("scalar", lambda e: e.activation(
                bb_t[:], bb_t[:], AF.Sin, bias=0.0, scale=SIN_SCL), sem="a")

            t_w1 = P.dma("sync", wsh1[:].rearrange("p (c d) -> p c d", c=4),
                         w1_my.rearrange("(c p) d -> p c d", p=128), "ld")
            t_w2 = P.dma("sync", wsh2[:], w2_my, "ld")
            P.wait("vector", t_w1)
            for ch in range(4):
                P.sop("vector", lambda e, ch=ch: e.tensor_reduce(
                    swsh1[:, ch:ch + 1], wsh1[:, ch * D:(ch + 1) * D], AX, ALU.max,
                    apply_absolute_value=True), "v")
            P.sop("vector", lambda e: e.tensor_scalar(
                swsh1[:], swsh1[:], EPS_Q, 1.0 / 127.0, ALU.max, ALU.mult), "v")
            t_sw1l = P.sop("vector", lambda e: e.reciprocal(rwsh1[:], swsh1[:]),
                           "v")
            P.wait("vector", t_w2)
            P.sop("vector", lambda e: e.tensor_reduce(
                swsh2[:], wsh2[:], AX, ALU.max, apply_absolute_value=True), "v")
            P.sop("vector", lambda e: e.tensor_scalar(
                swsh2[:], swsh2[:], EPS_Q, 1.0 / 127.0, ALU.max, ALU.mult), "v")
            t_sw2l = P.sop("vector", lambda e: e.reciprocal(rwsh2[:], swsh2[:]),
                           "v")
            P.sop("vector", lambda e: e.tensor_copy(agin[:, 6 * C2:6 * C2 + 4],
                                                    swsh1[:]), "v")
            P.sop("vector", lambda e: e.tensor_copy(
                agin[:, 6 * C2 + 4:6 * C2 + 5], swsh2[:]), "v")
            t_scl = P.settle("vector", "v", dumv[:])

            tok_last_ev = None
            tok_last_mm = None
            for wi, wap in enumerate((wq_my, wk_my, wv_my)):
                P.wait("sync", tok_last_mm)
                t_wm = P.dma("sync", wmy[:].rearrange("p (k d) -> p k d", k=KD),
                             wap.rearrange("(k p) d -> p k d", p=128), "ld")
                for ti, (tbl, ttok) in enumerate(((cc_t, t_cc), (bb_t, t_bb))):
                    P.wait("tensor", [t_wm, ttok])
                    if tok_last_ev is not None:
                        P.wait("tensor", tok_last_ev)
                    tmm = None
                    for k in range(KD):
                        sem = "t" if k == KD - 1 else None
                        tmm = P.op("tensor", lambda e, k=k, tbl=tbl: e.matmul(
                            comb_ps[:], wmy[:, k * 128:(k + 1) * 128],
                            tbl[:, k * C2:(k + 1) * C2],
                            start=(k == 0), stop=(k == KD - 1)), sem=sem) or tmm
                    slot = 2 * wi + ti
                    tok_last_mm = tmm
                    P.wait("scalar", tmm)
                    tok_last_ev = P.op("scalar", lambda e, s=slot: e.copy(
                        agin[:, s * C2:(s + 1) * C2], comb_ps[:]), sem="a")

            tok_last_ev = P.settle("scalar", "a", duma[:])
            P.wait("sync", [tok_last_ev, t_scl])
            if debug:
                P.dma("sync", cc_dbg, cc_t[:], "ld")
                P.dma("sync", agi_dbg, agin[:], "ld")
            t_agin = P.dma("sync", ag1_in.ap(), agin[:], "ld")
            P.wait("gpsimd", t_agin)
            t_ag1 = P.op("gpsimd", lambda e: e.collective_compute(
                "AllGather", ALU.bypass, replica_groups=[list(range(NCORES))],
                ins=[ag1_in.ap()], outs=[ag1_out.ap()]), sem="cc")
            P.wait("sync", t_ag1)
            t_tabs = None
            for m in range(6):
                t_tabs = P.dma(
                    "sync", atabs[m][:].rearrange("p (k c) -> p k c", k=KD),
                    ag1_out.ap()[:, m * C2:(m + 1) * C2].rearrange(
                        "(k p) c -> p k c", p=128), "ld")
            if debug:
                P.wait("sync", t_tabs)
                P.dma("sync", tab_dbg, atabs[0][:], "ld")
            t_sw1 = P.dma("sync", sw1t[:].rearrange("p (r c) -> p r c", r=8),
                          ag1_out.ap()[:, 6 * C2:6 * C2 + 4].rearrange(
                              "(r p) c -> p r c", p=128), "ld")
            t_sw2 = P.dma("sync", sw2t[:].rearrange("p (r c) -> p r c", r=8),
                          ag1_out.ap()[:, 6 * C2 + 4:6 * C2 + 5].rearrange(
                              "(r p) c -> p r c", p=128), "ld", slow=True)

            st1a.close()
            qsh = alloc("qsh", [128, 4 * D], F32, stack=st1)
            wbufs = tuple(alloc("wqb%d" % i, [128, D], stack=st1)
                          for i in range(6))
            tq = []
            tprev = None
            for pr in range(2):
                s0, s1 = 2 * pr, 2 * pr + 1
                tkA, tkB, tprev = emit_quant_pair(
                    P, [wsh1[:, s0 * D:(s0 + 1) * D], wsh1[:, s1 * D:(s1 + 1) * D]],
                    [rwsh1[:, s0:s0 + 1], rwsh1[:, s1:s1 + 1]], wbufs,
                    [qsh[:, s0 * D:(s0 + 1) * D], qsh[:, s1 * D:(s1 + 1) * D]],
                    [t_sw1l, tprev, t_agin], sbias_w, weight_mode=True)
                tq += [tkA, tkB]
            tq.append(P.settle("vector", "v", dumv[:]))
            P.wait("sync", tq)
            t_q1d = P.dma("sync", agw1_in.ap().rearrange("(c p) d -> p c d", p=128),
                          qsh[:].rearrange("p (c d) -> p c d", c=4), "ld")
            P.wait("gpsimd", t_q1d)
            t_agw1 = P.op("gpsimd", lambda e: e.collective_compute(
                "AllGather", ALU.bypass, replica_groups=[list(range(NCORES))],
                ins=[agw1_in.ap()], outs=[agw1_out.ap()]), sem="cc")

            tq = []
            for pr in range(2):
                s0, s1 = 2 * pr, 2 * pr + 1
                tkA, tkB, tprev = emit_quant_pair(
                    P, [wsh2[:, s0 * D:(s0 + 1) * D], wsh2[:, s1 * D:(s1 + 1) * D]],
                    [rwsh2[:], rwsh2[:]], wbufs,
                    [qsh[:, s0 * D:(s0 + 1) * D], qsh[:, s1 * D:(s1 + 1) * D]],
                    [t_sw2l, tprev, t_q1d], sbias_w, weight_mode=True)
                tq += [tkA, tkB]
            tq.append(P.settle("vector", "v", dumv[:]))
            P.wait("sync", tq)
            t_q2d = P.dma("sync", agw2_in.ap(), qsh[:], "ld")
            P.wait("gpsimd", t_q2d)
            t_agw2 = P.op("gpsimd", lambda e: e.collective_compute(
                "AllGather", ALU.bypass, replica_groups=[list(range(NCORES))],
                ins=[agw2_in.ap()], outs=[agw2_out.ap()]), sem="cc")

            tbin = alloc("tbin", [128, FF], F32, stack=st1)
            tbout = alloc("tbout", [128, KD * 128], BF16, stack=st1)
            tb_ps = st1.enter_context(nc.psum_tensor("tb_ps", [128, 1024], F32))

            P.wait("sync", t_agw1)
            t_dma_prev = None
            t_cs_prev = None
            for j in range(KF):
                if t_dma_prev is not None:
                    P.wait("sync", t_dma_prev)
                P.wait("sync", t_cs_prev)
                t_in = P.dma("sync", tbin[:, 0:D],
                             agw1_out.ap()[j * 128:(j + 1) * 128, 0:D], "ld")
                P.wait("vector", t_in)
                t_cs_prev = P.sop("vector", lambda e, j=j: e.tensor_reduce(
                    cs1t[:, j:j + 1], tbin[:, 0:D], AX, ALU.add), "v")
                P.wait("tensor", t_in)
                ttr = None
                for k in range(KD):
                    sem = "t" if k == KD - 1 else None
                    ttr = P.op("tensor", lambda e, k=k: e.transpose(
                        tb_ps[:, k * 128:(k + 1) * 128],
                        tbin[:, k * 128:(k + 1) * 128], ident[:]), sem=sem) or ttr
                P.wait("scalar", ttr)
                P.wait("scalar", t_dma_prev)
                tev = P.op("scalar", lambda e: e.copy(tbout[:], tb_ps[:, 0:1024]),
                           sem="a")
                P.wait("tensor", tev)
                tset = P.settle("scalar", "a", duma[:])
                P.wait("sync", [tset, ttr])
                t_dma_prev = P.dma(
                    "sync", qw1t_d[:, j * 1024:(j + 1) * 1024], tbout[:], "ld")

            P.wait("sync", t_agw2)
            for j in range(KD):
                P.wait("sync", t_dma_prev)
                P.wait("sync", t_cs_prev)
                t_in = P.dma("sync", tbin[:, 0:FF],
                             agw2_out.ap()[j * 128:(j + 1) * 128, 0:FF], "ld")
                P.wait("vector", t_in)
                t_cs_prev = P.sop("vector", lambda e, j=j: e.tensor_reduce(
                    cs2t[:, j:j + 1], tbin[:, 0:FF], AX, ALU.add), "v")
                P.wait("tensor", t_in)
                for kb in range(4):
                    ttr = None
                    for k in range(8):
                        kk = kb * 8 + k
                        sem = "t" if k == 7 else None
                        ttr = P.op("tensor", lambda e, kk=kk, k=k: e.transpose(
                            tb_ps[:, k * 128:(k + 1) * 128],
                            tbin[:, kk * 128:(kk + 1) * 128], ident[:]),
                            sem=sem) or ttr
                    P.wait("scalar", ttr)
                    P.wait("scalar", t_dma_prev)
                    tev = P.op("scalar", lambda e: e.copy(
                        tbout[:, 0:1024], tb_ps[:, 0:1024]), sem="a")
                    P.wait("tensor", tev)
                    tset = P.settle("scalar", "a", duma[:])
                    P.wait("sync", tset)
                    t_dma_prev = P.dma(
                        "sync",
                        qw2t_d[:, (j * 4 + kb) * 1024:(j * 4 + kb + 1) * 1024],
                        tbout[:, 0:1024], "ld")
            t_wt_done = t_dma_prev

        P.quiesce()
        with contextlib.ExitStack() as b1:
            xbuf = [alloc("xb0", [128, 4 * D], stack=b1),
                    alloc("xb1", [128, 4 * D], stack=b1)]
            sqt = alloc("sqt", [128, D], stack=b1)
            mt = alloc("mt", [128, 1], stack=b1)
            vt = alloc("vt", [128, 1], stack=b1)
            rstd = alloc("rstd", [128, 1], stack=b1)
            xnt = alloc("xnt", [128, KD * T], stack=b1)
            fqb = alloc("fqb", [128, 8 * T], stack=b1)
            fvt = alloc("fvt", [128, 2 * T], stack=b1)
            pb = alloc("pb", [128, 8 * T], stack=b1)
            vt1 = alloc("vt1", [128, T], stack=b1)
            vt2 = alloc("vt2", [128, T], stack=b1)
            ttp = alloc("ttp", [128, 1], stack=b1)
            tr_ps = [b1.enter_context(nc.psum_tensor("trp0", [128, 512], F32)),
                     b1.enter_context(nc.psum_tensor("trp1", [128, 512], F32))]
            pl_ps = [b1.enter_context(nc.psum_tensor("plp%d" % i, [128, 512], F32))
                     for i in range(6)]

            tok_x_free = [None, None]
            tok_xnt_sp = None
            tok_fq_sp = None
            tok_p_sp = None
            tok_proj_done = None
            tok_ev_done = None

            for i in range(NT):
                sl = i % 2
                P.wait("sync", tok_x_free[sl])
                t_x = P.dma("sync",
                            xbuf[sl][:].rearrange("p (c d) -> p c d", c=4),
                            x_in[i * T:(i + 1) * T, :].rearrange(
                                "(c p) d -> p c d", p=128), "ld")
                ln_toks = []
                P.wait("vector", t_x)
                for tc in range(4):
                    ch = xbuf[sl][:, tc * D:(tc + 1) * D]
                    P.sop("vector", lambda e, ch=ch: e.tensor_reduce(
                        mt[:], ch, AX, ALU.add), "v")
                    P.sop("vector", lambda e: e.tensor_single_scalar(
                        mt[:], mt[:], 1.0 / D, ALU.mult), "v")
                    tv = P.op("vector", lambda e, ch=ch: e.tensor_scalar(
                        ch, ch, mt[:, 0:1], None, ALU.subtract), sem="v")
                    P.wait("scalar", tv)
                    ta = P.op("scalar", lambda e, ch=ch: e.activation(
                        sqt[:], ch, AF.Square, bias=0.0, scale=1.0), sem="a")
                    P.wait("vector", ta)
                    P.sop("vector", lambda e: e.tensor_reduce(
                        vt[:], sqt[:], AX, ALU.add), "v")
                    tv2 = P.op("vector", lambda e: e.tensor_copy(vt[:], vt[:]),
                               sem="v")
                    P.wait("scalar", tv2)
                    P.op("scalar", lambda e: e.activation(
                        rstd[:], vt[:], AF.Sqrt, bias=eps_ln[:, 0:1],
                        scale=1.0 / D))
                    ta2 = P.op("scalar", lambda e: e.copy(duma[:], duma[:]),
                               sem="a")
                    P.wait("vector", ta2)
                    P.sop("vector", lambda e: e.reciprocal(rstd[:], rstd[:]), "v")
                    tln = P.op("vector", lambda e, ch=ch: e.tensor_scalar(
                        ch, ch, rstd[:, 0:1], None, ALU.mult), sem="v")
                    ln_toks.append(tln)
                if debug and i == 0:
                    P.wait("gpsimd", ln_toks)
                    P.dma("gpsimd", xn_dbg, xbuf[sl][:], "sp")
                    P.dma("gpsimd", id_dbg, ident[:], "sp")
                P.sop("vector", lambda e, sl=sl: e.tensor_reduce(
                    ttp[:], xbuf[sl][:], AX, ALU.max), "v")
                t_m1 = P.sop("vector", lambda e: e.tensor_max(
                    m1loc[:], m1loc[:], ttp[:]), "v")

                ev_toks = []
                for tc in range(4):
                    P.wait("tensor", ln_toks[tc])
                    ttr = None
                    for half in range(2):
                        ps = tr_ps[half]
                        for k in range(4):
                            kk = half * 4 + k
                            sem = "t" if k == 3 else None
                            ttr = P.op("tensor", lambda e, ps=ps, k=k, kk=kk, sl=sl,
                                       tc=tc: e.transpose(
                                           ps[:, k * 128:(k + 1) * 128],
                                           xbuf[sl][:, tc * D + kk * 128:
                                                    tc * D + (kk + 1) * 128],
                                           ident[:]), sem=sem) or ttr
                        P.wait("scalar", ttr)
                        if tok_proj_done is not None:
                            P.wait("scalar", tok_proj_done)
                        P.wait("scalar", tok_xnt_sp)
                        dst = xnt[:].rearrange("p (k t) -> p k t", k=KD)[
                            :, half * 4:(half + 1) * 4, tc * 128:(tc + 1) * 128]
                        tev = P.op("scalar", lambda e, dst=dst, ps=ps: e.copy(
                            dst, ps[:].rearrange("p (k t) -> p k t", k=4)), sem="a")
                        ev_toks.append(tev)
                        P.wait("tensor", tev)
                tok_x_free[sl] = ttr
                tok_ev_done = ev_toks[-1]

                t_set_a = P.settle("scalar", "a", duma[:])
                P.wait("gpsimd", [t_set_a] + ev_toks)
                tok_xnt_sp = P.dma("gpsimd", xnt_d[:, i * KD * T:(i + 1) * KD * T],
                                   xnt[:], "sp")

                p_toks = []
                for jc in range(4):
                    mm_tok = [None] * 6
                    for pl in range(6):
                        P.wait("tensor", ev_toks)
                        if i > 0 or jc > 0:
                            pass
                        P.wait("tensor", t_tabs)
                        for k in range(KD):
                            sem = "t" if k == KD - 1 else None
                            tk = P.op("tensor", lambda e, pl=pl, k=k, jc=jc: e.matmul(
                                pl_ps[pl][:],
                                atabs[pl][:, k * C2 + jc * 128:k * C2 + (jc + 1) * 128],
                                xnt[:, k * T:(k + 1) * T],
                                start=(k == 0), stop=(k == KD - 1)), sem=sem)
                            if sem:
                                mm_tok[pl] = tk
                    P.wait("scalar", [mm_tok[0], mm_tok[1], mm_tok[4], mm_tok[5]])
                    P.wait("scalar", tok_fq_sp)
                    e1 = P.op("scalar", lambda e, jc=jc: e.copy(
                        fqb[:, jc * T:(jc + 1) * T], pl_ps[0][:]), sem="a")
                    e2 = P.op("scalar", lambda e, jc=jc: e.copy(
                        fqb[:, (4 + jc) * T:(5 + jc) * T], pl_ps[1][:]), sem="a")
                    e3 = P.op("scalar", lambda e: e.copy(fvt[:, 0:T], pl_ps[4][:]),
                              sem="a")
                    e4 = P.op("scalar", lambda e: e.copy(fvt[:, T:2 * T],
                                                         pl_ps[5][:]), sem="a")
                    P.wait("vector", [mm_tok[2], mm_tok[3], e3, e4])
                    P.wait("vector", tok_p_sp)
                    P.op("vector", lambda e: e.tensor_mul(
                        vt1[:], pl_ps[2][:], fvt[:, 0:T]))
                    P.sop("vector", lambda e: e.tensor_mul(
                        vt2[:], pl_ps[3][:], fvt[:, T:2 * T]), "v")
                    P.op("vector", lambda e, jc=jc: e.tensor_sub(
                        pb[:, jc * T:(jc + 1) * T], vt1[:], vt2[:]))
                    P.op("vector", lambda e: e.tensor_mul(
                        vt1[:], pl_ps[2][:], fvt[:, T:2 * T]))
                    tpv = P.sop("vector", lambda e: e.tensor_mul(
                        vt2[:], pl_ps[3][:], fvt[:, 0:T]), "v")
                    P.op("vector", lambda e, jc=jc: e.tensor_add(
                        pb[:, (4 + jc) * T:(5 + jc) * T], vt1[:], vt2[:]))
                    if jc == 0:
                        P.sop("vector", lambda e: e.tensor_mul(
                            vt1[0:1, :], pl_ps[3][0:1, :], fvt[0:1, T:2 * T]), "v")
                        P.op("vector", lambda e: e.tensor_add(
                            pb[0:1, 0:T], pb[0:1, 0:T], vt1[0:1, :]))
                        P.op("vector", lambda e: e.tensor_copy(
                            pb[0:1, 4 * T:5 * T], vt1[0:1, :]))
                    P.sop("vector", lambda e, jc=jc: e.tensor_reduce(
                        ttp[:], pb[:, jc * T:(jc + 1) * T], AX, ALU.add), "v")
                    P.sop("vector", lambda e, jc=jc: e.tensor_add(
                        totals[:, jc:jc + 1], totals[:, jc:jc + 1], ttp[:]), "v")
                    P.sop("vector", lambda e, jc=jc: e.tensor_reduce(
                        ttp[:], pb[:, (4 + jc) * T:(5 + jc) * T], AX, ALU.add), "v")
                    tlast = P.sop("vector", lambda e, jc=jc: e.tensor_add(
                        totals[:, 4 + jc:5 + jc], totals[:, 4 + jc:5 + jc],
                        ttp[:]), "v")
                    p_toks.append(tlast)
                    P.wait("tensor", [tpv, tlast, e1, e2])
                tok_proj_done = p_toks[-1]
                fq_ev = [e1, e2, P.settle("scalar", "a", duma[:])]
                p_toks = p_toks + [P.settle("vector", "v", dumv[:])]
                tok_proj_done = p_toks[-1]
                P.wait("gpsimd", p_toks)
                P.wait("gpsimd", fq_ev)
                tok_fq_sp = P.dma("gpsimd", fq_d[:, i * 8 * T:(i + 1) * 8 * T],
                                  fqb[:], "sp")
                tok_p_sp = P.dma("gpsimd", p_d[:, i * 8 * T:(i + 1) * 8 * T],
                                 pb[:], "sp")

        st_atab.close()
        P.quiesce()
        st_if = contextlib.ExitStack()
        ifa = alloc("ifa", [128, 4 * D], stack=st_if)
        ifb = alloc("ifb", [128, 4 * D], stack=st_if)
        with contextlib.ExitStack() as bar1:
            br1_ps = bar1.enter_context(nc.psum_tensor("br1_ps", [128, 256], F32))
            coef_t = alloc("coef_t", [128, 8], stack=bar1)
            t_mia = P.dma("sync", ifa[:], mia_in, "ld")
            t_mib = P.dma("sync", ifb[:], mib_in, "ld")
            t_ca = P.dma("sync", coef_t[:, 0:4], coefa_in, "ld")
            t_cb = P.dma("sync", coef_t[:, 4:8], coefb_in, "ld")
            P.wait("scalar", t_mia)
            t_ifa = P.op("scalar", lambda e: e.activation(
                ifa[:], ifa[:], AF.Sin, bias=0.0, scale=SIN_SCL), sem="a")
            P.wait("scalar", t_mib)
            t_ifb = P.op("scalar", lambda e: e.activation(
                ifb[:], ifb[:], AF.Sin, bias=0.0, scale=SIN_SCL), sem="a")
            P.wait("vector", [t_ifa, t_ca])
            for cc in range(4):
                P.op("vector", lambda e, cc=cc: e.tensor_scalar(
                    ifa[:, cc * D:(cc + 1) * D], ifa[:, cc * D:(cc + 1) * D],
                    coef_t[:, cc:cc + 1], None, ALU.mult))
            P.wait("vector", [t_ifb, t_cb])
            t_iftab = None
            for cc in range(4):
                t_iftab = P.op("vector", lambda e, cc=cc: e.tensor_scalar(
                    ifb[:, cc * D:(cc + 1) * D], ifb[:, cc * D:(cc + 1) * D],
                    coef_t[:, 4 + cc:5 + cc], None, ALU.mult), sem="v")

            P.wait("vector", [tok_proj_done, t_m1])
            P.sop("vector", lambda e: e.tensor_copy(agst[:, 0:8], totals[:]), "v")
            P.sop("vector", lambda e: e.tensor_copy(agst[:, 8:9], m1loc[:]), "v")
            P.sop("vector", lambda e: e.memset(agst[:, 9:10], 0.0), "v")
            t_stl = P.settle("vector", "v", dumv[:])
            P.wait("sync", t_stl)
            t_a2a = P.dma("sync", ag2_in.ap(), agst[:], "ld")
            P.wait("gpsimd", t_a2a)
            t_ag2 = P.op("gpsimd", lambda e: e.collective_compute(
                "AllGather", ALU.bypass, replica_groups=[list(range(NCORES))],
                ins=[ag2_in.ap()], outs=[ag2_out.ap()]), sem="cc")
            P.wait("sync", t_ag2)
            t_agb = P.dma("sync", agb[:].rearrange("p (r c) -> p r c", r=8),
                          ag2_out.ap().rearrange("(r p) c -> p r c", p=128), "ld")

            P.wait("vector", t_agb)
            toff = None
            for r in range(8):
                toff = P.sop("vector", lambda e, r=r: e.scalar_tensor_tensor(
                    offs[:], agb[:, 10 * r:10 * r + 8], mask8[:, r:r + 1], offs[:],
                    ALU.mult, ALU.add), "v")
            m1cands = agb[:].rearrange("p (r c) -> p c r", r=8)[:, 8:9, :]
            tmax = emit_allmax_bcast(m1cands, s1pp[:], t_agb, br1_ps)
            P.wait("vector", tmax)
            P.sop("vector", lambda e: e.tensor_scalar(
                s1pp[:], s1pp[:], EPS_Q, 1.0 / 255.0, ALU.max, ALU.mult), "v")
            t_s1 = P.sop("vector", lambda e: e.reciprocal(r1pp[:], s1pp[:]), "v")

            P.wait("vector", [t_sw1, t_b1, t_s1])
            P.sop("vector", lambda e: e.tensor_scalar(
                scale1[:], sw1t[:], s1pp[:, 0:1], None, ALU.mult), "v")
            P.sop("vector", lambda e: e.reciprocal(qb1[:], scale1[:]), "v")
            tyb = P.sop("vector", lambda e: e.tensor_mul(qb2[:], b1t[:], qb1[:]),
                        "v")
            tq1 = emit_round_wide(P, qb2[:], qb3[:], qb4[:], qb1[:], qb6[:],
                                  qb5[:], zbias, [tyb])
            P.wait("vector", tq1)
            P.sop("vector", lambda e: e.tensor_mul(qb3[:], qb3[:], scale1[:]), "v")
            P.sop("vector", lambda e: e.tensor_scalar(
                qb1[:], cs1t[:], 127.5, None, ALU.mult), "v")
            P.sop("vector", lambda e: e.tensor_mul(qb1[:], qb1[:], scale1[:]), "v")
            t_bias1 = P.sop("vector", lambda e: e.tensor_add(
                bias1[:], qb3[:], qb1[:]), "v")
            t_bias1 = P.settle("vector", "v", dumv[:])

        with contextlib.ExitStack() as b2:
            pb2 = [alloc("pb2_0", [128, 8 * T], stack=b2),
                   alloc("pb2_1", [128, 8 * T], stack=b2)]
            fb2 = [alloc("fb2_0", [128, 8 * T], stack=b2),
                   alloc("fb2_1", [128, 8 * T], stack=b2)]
            memb = [alloc("memb0", [128, 8 * T], stack=b2),
                    alloc("memb1", [128, 8 * T], stack=b2)]
            zb = alloc("zb", [128, 8 * T], stack=b2)
            sob = alloc("sob", [128, KD * T], stack=b2)
            zt0 = alloc("zt0", [128, T], stack=b2)
            vt1b = alloc("vt1b", [128, T], stack=b2)
            vt2b = alloc("vt2b", [128, T], stack=b2)
            if_ps = [b2.enter_context(nc.psum_tensor("ifp0", [128, 512], F32)),
                     b2.enter_context(nc.psum_tensor("ifp1", [128, 512], F32))]

            P.op("vector", lambda e: e.memset(zt0[:], 0.0))
            tev_bank_b2 = [None, None]
            tok_pf_free = [None, None]
            tok_s_sp = None
            tok_z_done = None
            tok_zb_free = None
            for i in range(NT):
                sl = i % 2
                P.wait("sync", tok_pf_free[sl])
                t_p = P.dma("sync", pb2[sl][:], p_d[:, i * 8 * T:(i + 1) * 8 * T],
                            "ld")
                t_f = P.dma("sync", fb2[sl][:], fq_d[:, i * 8 * T:(i + 1) * 8 * T],
                            "ld")
                P.wait("vector", t_p)
                if i == 0:
                    P.wait("vector", toff)
                scan_toks = []
                for q in range(8):
                    if i == 0:
                        init = offs[:, q:q + 1]
                    else:
                        init = memb[1 - sl][:, q * T + T - 1:q * T + T]
                    tk = P.op("vector", lambda e, q=q, sl=sl, init=init: (
                        e.tensor_tensor_scan(
                            memb[sl][:, q * T:(q + 1) * T],
                            pb2[sl][:, q * T:(q + 1) * T], zt0[:],
                            init, ALU.add, ALU.add)), sem="v")
                    scan_toks.append(tk)
                P.wait("vector", t_f)
                P.wait("vector", tok_zb_free)
                z_toks = []
                for jc in range(4):
                    mre = memb[sl][:, jc * T:(jc + 1) * T]
                    mim = memb[sl][:, (4 + jc) * T:(5 + jc) * T]
                    qre = fb2[sl][:, jc * T:(jc + 1) * T]
                    qim = fb2[sl][:, (4 + jc) * T:(5 + jc) * T]
                    zre = zb[:, jc * T:(jc + 1) * T]
                    zim = zb[:, (4 + jc) * T:(5 + jc) * T]
                    P.op("vector", lambda e, a=mre, b=qre: e.tensor_mul(
                        vt1b[:], a, b))
                    P.sop("vector", lambda e, a=mim, b=qim: e.tensor_mul(
                        vt2b[:], a, b), "v")
                    P.op("vector", lambda e, o=zre: e.tensor_add(
                        o, vt1b[:], vt2b[:]))
                    P.op("vector", lambda e, a=mim, b=qre: e.tensor_mul(
                        vt1b[:], a, b))
                    P.sop("vector", lambda e, a=mre, b=qim: e.tensor_mul(
                        vt2b[:], a, b), "v")
                    tk = P.op("vector", lambda e, o=zim: e.tensor_sub(
                        o, vt1b[:], vt2b[:]), sem="v")
                    if jc == 0:
                        P.sop("vector", lambda e, a=mim, b=qim: e.tensor_mul(
                            vt1b[0:1, :], a[0:1, :], b[0:1, :]), "v")
                        P.op("vector", lambda e, o=zre: e.tensor_sub(
                            o[0:1, :], o[0:1, :], vt1b[0:1, :]))
                        tk = P.op("vector", lambda e, o=zim: e.tensor_copy(
                            o[0:1, :], vt1b[0:1, :]), sem="v")
                    z_toks.append(tk)
                tok_pf_free[sl] = z_toks[-1]
                tok_z_done = z_toks[-1]
                s_ev = []
                for k in range(KD):
                    ps = if_ps[k % 2]
                    P.wait("tensor", tev_bank_b2[k % 2])
                    P.wait("tensor", z_toks)
                    P.wait("tensor", t_iftab)
                    tmm = None
                    for q in range(8):
                        tbl = ifa if q < 4 else ifb
                        cc = q % 4
                        sem = "t" if q == 7 else None
                        tmm = P.op("tensor", lambda e, ps=ps, tbl=tbl, cc=cc, k=k,
                                   q=q: e.matmul(
                                       ps[:], tbl[:, cc * D + k * 128:
                                                  cc * D + (k + 1) * 128],
                                       zb[:, q * T:(q + 1) * T],
                                       start=(q == 0), stop=(q == 7)),
                                   sem=sem) or tmm
                    P.wait("scalar", tmm)
                    P.wait("scalar", tok_s_sp)
                    tev = P.op("scalar", lambda e, k=k, ps=ps: e.copy(
                        sob[:, k * T:(k + 1) * T], ps[:]), sem="a")
                    s_ev.append(tev)
                    tev_bank_b2[k % 2] = tev
                tok_zb_free = tmm
                s_ev.append(P.settle("scalar", "a", duma[:]))
                P.wait("gpsimd", s_ev)
                tok_s_sp = P.dma("gpsimd", s_d[:, i * KD * T:(i + 1) * KD * T],
                                 sob[:], "sp")
            tok_b2_sp = tok_s_sp
        st_if.close()
        P.quiesce()

        with contextlib.ExitStack() as pc:
            qw1sb = alloc("qw1sb", [128, KD * FF], BF16, stack=pc)
            xc = [alloc("xc0", [128, KD * T], stack=pc),
                  alloc("xc1", [128, KD * T], stack=pc)]
            xq = alloc("xq", [128, KD * T], BF16, stack=pc)
            hb = alloc("hb", [128, KF * T], stack=pc)
            qbufs = tuple(alloc("qcb%d" % i, [128, 1024], stack=pc)
                          for i in range(6))
            ttp2 = alloc("ttp2", [128, 1], stack=pc)
            h_ps = [pc.enter_context(nc.psum_tensor("hp%d" % v_, [128, 512], F32))
                    for v_ in range(2)]

            t_qw1 = P.dma("sync", qw1sb[:], qw1t_d, "ld")
            tev_bank_c = [None, None]
            tok_xc_free = [None, None]
            tok_h_sp = [None, None]
            tok_m2 = None
            tok_m2_prev = None
            tok_xq_free = None
            tq_prev = None
            for i in range(NT):
                sl = i % 2
                P.wait("sync", tok_xc_free[sl])
                t_xc = P.dma("sync", xc[sl][:],
                             xnt_d[:, i * KD * T:(i + 1) * KD * T], "ld")
                tq_toks = []
                for pr in range(2):
                    a0, a1 = 2 * pr * 1024, (2 * pr + 1) * 1024
                    tkA, tkB, tq_prev = emit_quant_pair(
                        P, [xc[sl][:, a0:a0 + 1024], xc[sl][:, a1:a1 + 1024]],
                        [r1pp[:, 0:1], r1pp[:, 0:1]], qbufs,
                        [xq[:, a0:a0 + 1024], xq[:, a1:a1 + 1024]],
                        [t_xc, t_s1, tq_prev, tok_xq_free], sbias_a)
                    tq_toks += [tkA, tkB]
                tok_xc_free[sl] = tq_toks[-1]
                if debug and i == 0:
                    P.wait("sync", tq_toks)
                    P.dma("sync", xq_dbg, xq[:], "ld")
                ev_toks = []
                for fj in range(KF):
                    ps = h_ps[fj % 2]
                    P.wait("tensor", tev_bank_c[fj % 2])
                    P.wait("tensor", tq_toks)
                    P.wait("tensor", t_qw1)
                    tmm = None
                    for k in range(KD):
                        sem = "t" if k == KD - 1 else None
                        tmm = P.op("tensor", lambda e, ps=ps, fj=fj, k=k: e.matmul(
                            ps[:], qw1sb[:, fj * 1024 + k * 128:fj * 1024 + (k + 1) * 128],
                            xq[:, k * T:(k + 1) * T],
                            start=(k == 0), stop=(k == KD - 1)), sem=sem) or tmm
                    P.wait("scalar", tmm)
                    P.wait("scalar", [t_bias1, tok_h_sp[0 if fj < 16 else 1],
                                      tok_m2_prev])
                    tev = P.op("scalar", lambda e, fj=fj, ps=ps: e.activation(
                        hb[:, fj * T:(fj + 1) * T], ps[:], AF.Relu,
                        bias=bias1[:, fj:fj + 1], scale=scale1[:, fj:fj + 1]),
                        sem="a")
                    ev_toks.append(tev)
                    tev_bank_c[fj % 2] = tev
                    tok_xq_free = tmm
                P.wait("vector", ev_toks)
                P.sop("vector", lambda e: e.tensor_reduce(
                    ttp2[:], hb[:], AX, ALU.max), "v")
                tok_m2 = P.sop("vector", lambda e: e.tensor_max(
                    m2loc[:], m2loc[:], ttp2[:]), "v")
                tok_m2_prev = tok_m2
                t_set_c = P.settle("scalar", "a", duma[:])
                for hf in range(2):
                    P.wait("gpsimd", [ev_toks[hf * 16 + 15], t_set_c])
                    tok_h_sp[hf] = P.dma(
                        "gpsimd",
                        h_d[:, i * KF * T + hf * 16 * T:
                            i * KF * T + (hf + 1) * 16 * T],
                        hb[:, hf * 16 * T:(hf + 1) * 16 * T], "sp")

        P.quiesce()
        with contextlib.ExitStack() as bar2:
            br2_ps = bar2.enter_context(nc.psum_tensor("br2_ps", [128, 256], F32))
            t_stl2 = P.settle("vector", "v", dumv[:])
            P.wait("sync", [tok_m2, t_stl2])
            t_a3 = P.dma("sync", ag3_in.ap(), m2loc[:], "ld")
            P.wait("gpsimd", t_a3)
            t_ag3 = P.op("gpsimd", lambda e: e.collective_compute(
                "AllGather", ALU.bypass, replica_groups=[list(range(NCORES))],
                ins=[ag3_in.ap()], outs=[ag3_out.ap()]), sem="cc")
            P.wait("sync", t_ag3)
            t_agb3 = P.dma("sync", agb[:, 0:8].rearrange("p (r c) -> p r c", r=8),
                           ag3_out.ap().rearrange("(r p) c -> p r c", p=128), "ld",
                           slow=True)
            tmax2 = emit_allmax_bcast(agb[:, 0:8], s2pp[:], t_agb3, br2_ps)
            P.wait("vector", tmax2)
            P.sop("vector", lambda e: e.tensor_scalar(
                s2pp[:], s2pp[:], EPS_Q, 1.0 / 255.0, ALU.max, ALU.mult), "v")
            t_s2 = P.sop("vector", lambda e: e.reciprocal(r2pp[:], s2pp[:]), "v")
            P.wait("vector", [t_sw2, t_b2, t_s2])
            P.sop("vector", lambda e: e.tensor_scalar(
                scale2[:], sw2t[:], s2pp[:, 0:1], None, ALU.mult), "v")
            P.sop("vector", lambda e: e.reciprocal(qb1[:, 0:KD], scale2[:]), "v")
            tyb2 = P.sop("vector", lambda e: e.tensor_mul(
                qb2[:, 0:KD], b2t[:], qb1[:, 0:KD]), "v")
            tq2r = emit_round_wide(P, qb2[:, 0:KD], qb3[:, 0:KD], qb4[:, 0:KD],
                                   qb1[:, 0:KD], qb6[:, 0:KD], qb5[:, 0:KD],
                                   zbias, [tyb2])
            P.wait("vector", tq2r)
            P.sop("vector", lambda e: e.tensor_mul(
                qb3[:, 0:KD], qb3[:, 0:KD], scale2[:]), "v")
            P.sop("vector", lambda e: e.tensor_scalar(
                qb1[:, 0:KD], cs2t[:], 127.5, None, ALU.mult), "v")
            P.sop("vector", lambda e: e.tensor_mul(
                qb1[:, 0:KD], qb1[:, 0:KD], scale2[:]), "v")
            t_bias2 = P.sop("vector", lambda e: e.tensor_add(
                bias2[:], qb3[:, 0:KD], qb1[:, 0:KD]), "v")
            t_bias2 = P.settle("vector", "v", dumv[:])

        with contextlib.ExitStack() as pd:
            qw2sb = alloc("qw2sb", [128, KF * D], BF16, stack=pd)
            hD = alloc("hd0", [128, KF * TD], stack=pd)
            qh = alloc("qh", [128, KF * TD], BF16, stack=pd)
            gb = alloc("gb", [128, KD * TD], stack=pd)
            sD = [alloc("sd0", [128, KD * TD], stack=pd),
                  alloc("sd1", [128, KD * TD], stack=pd)]
            rb = alloc("rb", [128, KD * TD], stack=pd)
            xD = [alloc("xd0", [128, 2 * D], stack=pd),
                  alloc("xd1", [128, 2 * D], stack=pd)]
            ob = alloc("ob", [128, 2 * D], stack=pd)
            dbufs = tuple(alloc("qdb%d" % i, [128, 1024], stack=pd)
                          for i in range(6))
            y2_ps = [pd.enter_context(nc.psum_tensor("y2p%d" % v_, [128, 256], F32))
                     for v_ in range(2)]
            rt_ps = [pd.enter_context(nc.psum_tensor("rtp%d" % v_, [128, 1024], F32))
                     for v_ in range(2)]
            scrap_ps = pd.enter_context(nc.psum_tensor("scrap_ps", [128, 128], F32))

            t_qw2 = P.dma("sync", qw2sb[:], qw2t_d, "ld")
            tev_bank_d = [None, None]
            if debug:
                P.wait("vector", t_bias2)
                dcols = [(s1pp, 0), (r1pp, 1), (s2pp, 2), (r2pp, 3), (m1loc, 4),
                         (m2loc, 5)]
                for t_, j in dcols:
                    P.sop("vector", lambda e, t_=t_, j=j: e.tensor_copy(
                        diag_dbg_sb[:, j:j + 1], t_[:]), "v")
                for t_, j0, w in [(scale2, 8, KD), (bias2, 16, KD),
                                  (scale1, 24, KF), (bias1, 56, KF),
                                  (sw2t, 88, KD), (cs2t, 96, KD), (b2t, 104, KD)]:
                    P.sop("vector", lambda e, t_=t_, j0=j0, w=w: e.tensor_copy(
                        diag_dbg_sb[:, j0:j0 + w], t_[:]), "v")
                tdg = P.op("vector", lambda e: e.tensor_copy(
                    diag_dbg_sb[:, 120:121], eps_ln[:]), sem="v")
                P.wait("sync", tdg)
                P.dma("sync", diag_dbg, diag_dbg_sb[:], "ld")
                P.dma("sync", qw1_dbg, qw1t_d, "ld")
                P.dma("sync", qw2_dbg, qw2t_d, "ld")
            tok_hD_free = [None, None]
            tok_sD_free = [None, None]
            tok_xD_free = [None, None]
            tok_ob_free = None
            tok_qh_free = None
            tok_rb_free = None
            tok_gb_free = None
            dq_prev = None
            t_out_last = None
            for t in range(NTD):
                sl = t % 2
                ci, hf = t // 2, t % 2
                t_hh = []
                for hh in range(2):
                    P.wait("sync", tok_hD_free[hh])
                    tk = P.dma(
                        "sync",
                        hD[:, hh * 16 * TD:(hh + 1) * 16 * TD].rearrange(
                            "p (o f w) -> p o f w", o=1, f=16),
                        h_d.rearrange("p (i f w) -> p i f w", i=NT, f=KF)[
                            :, ci:ci + 1, hh * 16:(hh + 1) * 16,
                            hf * TD:(hf + 1) * TD], "ld")
                    t_hh.append(tk)
                P.wait("sync", tok_sD_free[sl])
                t_s = P.dma(
                    "sync", sD[sl][:].rearrange("p (o k w) -> p o k w", o=1, k=KD),
                    s_d.rearrange("p (i k w) -> p i k w", i=NT, k=KD)[
                        :, ci:ci + 1, :, hf * TD:(hf + 1) * TD], "ld")
                P.wait("sync", tok_xD_free[sl])
                t_x2 = P.dma("sync", xD[sl][:].rearrange("p (c d) -> p c d", c=2),
                             x_in[t * TD:(t + 1) * TD, :].rearrange(
                                 "(c p) d -> p c d", p=128), "ld")
                tq_toks = []
                for pr in range(4):
                    a0, a1 = 2 * pr * 1024, (2 * pr + 1) * 1024
                    tkA, tkB, dq_prev = emit_quant_pair(
                        P, [hD[:, a0:a0 + 1024], hD[:, a1:a1 + 1024]],
                        [r2pp[:, 0:1], r2pp[:, 0:1]], dbufs,
                        [qh[:, a0:a0 + 1024], qh[:, a1:a1 + 1024]],
                        [t_hh[pr // 2], t_s2, dq_prev, tok_qh_free], sbias_a)
                    tq_toks += [tkA, tkB]
                    if pr == 1:
                        tok_hD_free[0] = tkB
                tok_hD_free[1] = tq_toks[-1]
                g_toks = []
                for k in range(KD):
                    ps = y2_ps[k % 2]
                    P.wait("tensor", tev_bank_d[k % 2])
                    P.wait("tensor", tq_toks)
                    P.wait("tensor", t_qw2)
                    tmm = None
                    for fj in range(KF):
                        sem = "t" if fj == KF - 1 else None
                        col = (k * 4 + fj // 8) * 1024 + (fj % 8) * 128
                        tmm = P.op("tensor", lambda e, ps=ps, fj=fj, c0=col: e.matmul(
                            ps[:], qw2sb[:, c0:c0 + 128],
                            qh[:, fj * TD:(fj + 1) * TD],
                            start=(fj == 0), stop=(fj == KF - 1)), sem=sem) or tmm
                    P.wait("scalar", [tmm, t_bias2, tok_gb_free])
                    tev = P.op("scalar", lambda e, k=k, ps=ps: e.activation(
                        gb[:, k * TD:(k + 1) * TD], ps[:], AF.Silu,
                        bias=bias2[:, k:k + 1], scale=scale2[:, k:k + 1]), sem="a")
                    g_toks.append(tev)
                    tev_bank_d[k % 2] = tev
                tok_qh_free = tmm
                if debug and t == 0:
                    P.wait("sync", tq_toks)
                    P.dma("sync", qh_dbg, qh[:], "ld")
                    P.wait("sync", g_toks)
                    P.dma("sync", gb_dbg, gb[:], "ld")
                P.wait("vector", [t_s, tok_rb_free] + g_toks)
                t_r = P.op("vector", lambda e, sl=sl: e.tensor_mul(
                    rb[:], sD[sl][:], gb[:]), sem="v")
                tok_sD_free[sl] = t_r
                tok_gb_free = t_r
                t_ob = []
                for tc in range(2):
                    ps = rt_ps[tc]
                    P.wait("tensor", t_r)
                    ttr = None
                    for k in range(KD):
                        sem = "t" if k == KD - 1 else None
                        ttr = P.op("tensor", lambda e, ps=ps, k=k, tc=tc: e.transpose(
                            ps[:, k * 128:(k + 1) * 128],
                            rb[:, k * TD + tc * 128:k * TD + (tc + 1) * 128],
                            ident[:]), sem=sem) or ttr
                    P.wait("vector", [ttr, t_x2, tok_ob_free])
                    tado = P.op("vector", lambda e, tc=tc, sl=sl, ps=ps: e.tensor_add(
                        ob[:, tc * D:(tc + 1) * D], xD[sl][:, tc * D:(tc + 1) * D],
                        ps[:]), sem="v")
                    t_ob.append(tado)
                    P.wait("tensor", tado)
                if debug:
                    P.wait("sync", tq_toks)
                    P.dma("sync", qhfull_dbg[:, t * KF * TD:(t + 1) * KF * TD],
                          qh[:], "ld")
                    P.wait("sync", g_toks)
                    P.dma("sync", gfull_dbg[:, t * KD * TD:(t + 1) * KD * TD],
                          gb[:], "ld")
                if debug and t == 0:
                    P.wait("sync", [t_r] + t_ob)
                    P.dma("sync", rb_dbg, rb[:], "ld")
                    P.dma("sync", ob_dbg, ob[:], "ld")
                    P.dma("sync", sd_dbg, sD[sl][:], "ld")
                tok_xD_free[sl] = t_ob[-1]
                tok_rb_free = ttr
                t_ob = t_ob + [P.settle("vector", "v", dumv[:])]
                P.wait("sync", t_ob)
                t_out_last = P.dma(
                    "sync", out_d[t * TD:(t + 1) * TD, :].rearrange(
                        "(c p) d -> p c d", p=128),
                    ob[:].rearrange("p (c d) -> p c d", c=2), "st")
                tok_ob_free = t_out_last

            P.wait("sync", t_out_last)

        block = es.enter_context(nc.Block())
        P.replay(block)

    return nc


def _make_in_maps(L0_attn_x, L0_attn_Wq, L0_attn_Wk, L0_attn_Wv,
                  L0_ffn_w1, L0_ffn_b1, L0_ffn_w2, L0_ffn_b2):
    x = np.ascontiguousarray(np.asarray(L0_attn_x, np.float32)).reshape(32768, D)
    wq = np.ascontiguousarray(np.asarray(L0_attn_Wq, np.float32))
    wk = np.ascontiguousarray(np.asarray(L0_attn_Wk, np.float32))
    wv = np.ascontiguousarray(np.asarray(L0_attn_Wv, np.float32))
    w1 = np.ascontiguousarray(np.asarray(L0_ffn_w1, np.float32))
    b1 = np.ascontiguousarray(np.asarray(L0_ffn_b1, np.float32))
    w2 = np.ascontiguousarray(np.asarray(L0_ffn_w2, np.float32))
    b2 = np.ascontiguousarray(np.asarray(L0_ffn_b2, np.float32))
    cst = _host_consts()
    b1t = np.ascontiguousarray(b1.reshape(KF, 128).T)
    b2t = np.ascontiguousarray(b2.reshape(KD, 128).T)

    in_maps = []
    for c in range(NCORES):
        mask = np.zeros((128, 8), np.float32)
        if c % 2 == 1:
            mask[:, c - 1] = 1.0
        in_maps.append({
            "x": np.ascontiguousarray(x[c * NTOK:(c + 1) * NTOK]),
            "wq_my": np.ascontiguousarray(wq[:, 128 * c:128 * (c + 1)]),
            "wk_my": np.ascontiguousarray(wk[:, 128 * c:128 * (c + 1)]),
            "wv_my": np.ascontiguousarray(wv[:, 128 * c:128 * (c + 1)]),
            "w1_my": np.ascontiguousarray(w1[512 * c:512 * (c + 1)]),
            "w2_my": np.ascontiguousarray(w2[128 * c:128 * (c + 1)]),
            "b1t": b1t, "b2t": b2t,
            "ident": cst["ident"], "mcc": cst["mcc"], "mbb": cst["mbb"],
            "mia": cst["mia"], "mib": cst["mib"],
            "coefa": cst["coefa"], "coefb": cst["coefb"],
            "mask8": mask,
        })
    return in_maps


def kernel(**inputs):
    nc = build_program()
    in_maps = _make_in_maps(**inputs)
    res = run_bass_kernel_spmd(nc, in_maps, core_ids=list(range(NCORES)))
    out = np.concatenate([r["out"] for r in res.results], axis=0)
    return out.reshape(4, 8192, 1024)
